# version 19
# baseline (speedup 1.0000x reference)
import numpy as np
import ml_dtypes
import concourse.bass as bass
import concourse.mybir as mybir
from concourse.bass_utils import run_bass_kernel_spmd
from contextlib import ExitStack

F32 = mybir.dt.float32
BF16 = mybir.dt.bfloat16
AF = mybir.ActivationFunctionType
ALU = mybir.AluOpType
AX = mybir.AxisListType
BF = ml_dtypes.bfloat16

ENG = ("sync", "tensor", "vector", "scalar", "gpsimd")

L = 8192
D = 1024
H = 4
DK = 256
CH = 128
STRIP = 1024
NSTRIP = L // STRIP
SPC = STRIP // CH
NE = 8
FF = 4096
TW = 2048
NT = TW // 128
C1 = float(2.0 * np.sqrt(2.0 / np.pi))
C3 = float(0.044715 * 2.0 * np.sqrt(2.0 / np.pi))


def TH(name, *a, **k):
    return lambda h: getattr(h, name)(*a, **k)


class Sem:
    __slots__ = ("h", "n")


class Prog:

    def __init__(self, nc, ctx):
        self.nc = nc
        self.ctx = ctx
        self.streams = {e: [] for e in ENG}
        self.sat = {e: {} for e in ENG}
        self._n = 0

    def sem(self, name=None):
        s = Sem()
        s.h = self.ctx.enter_context(self.nc.semaphore(name or f"s{self._n}"))
        self._n += 1
        s.n = 0
        return s

    def emit(self, eng, fn, waits=(), inc=None, k=1):
        wl = []
        sat = self.sat[eng]
        def add(w):
            if w is None:
                return
            if isinstance(w, list):
                for ww in w:
                    add(ww)
                return
            s, c = w
            if c > sat.get(s, 0):
                wl.append((s, c))
                sat[s] = c
        for w in waits:
            add(w)
        incs = []
        tok = None
        if inc is not None:
            inc.n += k
            incs.append((inc, k))
            tok = (inc, inc.n)
        self.streams[eng].append([wl, fn, incs, tok])
        return tok

    def barrier(self, engines=("tensor", "vector", "scalar")):
        toks = []
        for e in engines:
            if not self.streams[e]:
                continue
            rec = self.streams[e][-1]
            if rec[3] is not None:
                toks.append(rec[3])
            else:
                b = self.sem()
                rec[2].append((b, 1))
                b.n += 1
                rec[3] = (b, b.n)
                toks.append((b, b.n))
        return toks

    def run(self, block):
        def mk(recs):
            def go(h):
                for wl, fn, incs, _t in recs:
                    for s, c in wl:
                        h.wait_ge(s.h, c)
                    ins = fn(h)
                    for s, k in incs:
                        ins = ins.then_inc(s.h, k)
            return go
        block.sync(mk(self.streams["sync"]))
        block.tensor(mk(self.streams["tensor"]))
        block.vector(mk(self.streams["vector"]))
        block.scalar(mk(self.streams["scalar"]))
        block.gpsimd(mk(self.streams["gpsimd"]))


class DmaStream:

    def __init__(self, prog, k=1):
        self.prog = prog
        self.sems = [prog.sem() for _ in range(k)]
        self.k = k
        self.i = 0

    def emit(self, eng, fn, waits=()):
        sem = self.sems[self.i % self.k]
        self.i += 1
        w = list(waits) + [(sem, sem.n)]
        return self.prog.emit(eng, fn, w, sem, 16)

    def total_tok(self):
        return [(sem, sem.n) for sem in self.sems]


def make_consts(nc, ctx, prog, dsem, masks=True):
    c = {}
    sb = ctx.enter_context
    ident_d = nc.inline_tensor(np.eye(128, dtype=BF), "identd").ap()
    ones_row_d = nc.inline_tensor(np.ones((1, 512), dtype=BF), "onesrowd").ap()
    onescol_d = nc.inline_tensor(np.ones((128, 1), dtype=BF), "onescold").ap()
    eps_d = nc.inline_tensor(
        np.tile(np.array([[1e-6, 1e-5, 1.0, 0.0]], np.float32), (128, 1)), "epsd").ap()
    c["ident"] = sb(nc.sbuf_tensor("c_ident", [128, 128], BF16))
    c["ones_row"] = sb(nc.sbuf_tensor("c_onesrow", [1, 512], BF16))
    c["ones_col"] = sb(nc.sbuf_tensor("c_onescol", [128, 1], BF16))
    c["eps"] = sb(nc.sbuf_tensor("c_eps", [128, 4], F32))
    dsem.emit("sync", TH("dma_start", out=c["ident"][:], in_=ident_d))
    dsem.emit("sync", TH("dma_start", out=c["ones_row"][:], in_=ones_row_d))
    dsem.emit("sync", TH("dma_start", out=c["ones_col"][:], in_=onescol_d))
    tok = dsem.emit("sync", TH("dma_start", out=c["eps"][:], in_=eps_d))
    if masks:
        tril_s = np.tril(np.ones((128, 128), np.float32), -1)
        mdat = np.stack([tril_s, tril_s.T,
                         np.triu(np.ones((128, 128), np.float32)),
                         np.eye(128, dtype=np.float32)], axis=1)
        m_d = nc.inline_tensor(mdat, "masksd").ap()
        c["masks"] = sb(nc.sbuf_tensor("c_masks", [128, 4, 128], F32))
        tok = dsem.emit("sync", TH("dma_start", out=c["masks"][:], in_=m_d))
        m4dat = np.stack([np.tile(tril_s.T.astype(BF), (1, 4)),
                          np.tile(tril_s.astype(BF), (1, 4)),
                          np.tile(np.eye(128, dtype=BF), (1, 4)),
                          np.tile(np.triu(np.ones((128, 128), BF)), (1, 4))],
                         axis=1)
        m4_d = nc.inline_tensor(m4dat, "masks4d").ap()
        c["masks4"] = sb(nc.sbuf_tensor("c_masks4", [128, 4, 512], BF16))
        tok = dsem.emit("sync", TH("dma_start", out=c["masks4"][:], in_=m4_d))
    c["tok"] = tok
    return c


def build_attn():
    nc = bass.Bass("TRN2", target_bir_lowering=False, debug=False)
    x_h = nc.dram_tensor("x", [L, D], F32, kind="ExternalInput").ap()
    wq_h = nc.dram_tensor("wq", [D, DK], F32, kind="ExternalInput").ap()
    wk_h = nc.dram_tensor("wk", [D, DK], F32, kind="ExternalInput").ap()
    wv_h = nc.dram_tensor("wv", [D, DK], F32, kind="ExternalInput").ap()
    wbg_h = nc.dram_tensor("wbg", [128, 8, 2], F32, kind="ExternalInput").ap()
    wmx_h = nc.dram_tensor("wmx", [128, 8, 1], F32, kind="ExternalInput").ap()
    cvq_h = nc.dram_tensor("cvq", [2, 128, 4], F32, kind="ExternalInput").ap()
    cvk_h = nc.dram_tensor("cvk", [2, 128, 4], F32, kind="ExternalInput").ap()
    cvv_h = nc.dram_tensor("cvv", [2, 128, 4], F32, kind="ExternalInput").ap()
    cvm_h = nc.dram_tensor("cvm", [1, 4], F32, kind="ExternalInput").ap()
    mxb_h = nc.dram_tensor("mxb", [1, 1], F32, kind="ExternalInput").ap()
    onT_h = nc.dram_tensor("onT", [2, 128, L], BF16, kind="ExternalOutput").ap()

    ctx = ExitStack()
    with ctx:
        prog = Prog(nc, ctx)
        sb = ctx.enter_context
        EM = prog.emit

        dX = DmaStream(prog, 4)
        dW = DmaStream(prog, 1)
        dO = DmaStream(prog, 2)
        sT = prog.sem("sT")
        sV = prog.sem("sV")
        sA = prog.sem("sA")

        cst = make_consts(nc, ctx, prog, dW)
        mTRILS = cst["masks"][:, 0, :]
        mTRIUS = cst["masks"][:, 1, :]
        mTRIUI = cst["masks"][:, 2, :]
        mEYE = cst["masks"][:, 3, :]
        IDB = cst["ident"][:]

        wq = sb(nc.sbuf_tensor("wqb", [128, 8, DK], BF16))
        wk = sb(nc.sbuf_tensor("wkb", [128, 8, DK], BF16))
        wv = sb(nc.sbuf_tensor("wvb", [128, 8, DK], BF16))
        wbg = sb(nc.sbuf_tensor("wbgb", [128, 8, 2], BF16))
        wmx = sb(nc.sbuf_tensor("wmxb", [128, 8, 1], BF16))
        cvq = sb(nc.sbuf_tensor("cvqs", [128, 2, 4], F32))
        cvk = sb(nc.sbuf_tensor("cvks", [128, 2, 4], F32))
        cvv = sb(nc.sbuf_tensor("cvvs", [128, 2, 4], F32))
        cvm = sb(nc.sbuf_tensor("cvms", [1, 4], F32))
        mxb = sb(nc.sbuf_tensor("mxbs", [1, 1], F32))
        ctxw = ExitStack()
        wstg = ctxw.enter_context(nc.sbuf_tensor("wstg", [128, 2048], F32))

        tok = cst["tok"]
        for dst, src, cols in ((wq, wq_h, DK), (wk, wk_h, DK), (wv, wv_h, DK),
                               (wbg, wbg_h, 2), (wmx, wmx_h, 1)):
            src_ap = (src.rearrange("(a p) c -> p a c", p=128)
                      if cols == DK else src)
            tok = dW.emit("sync", TH("dma_start",
                                      out=wstg[:, 0:8 * cols].rearrange(
                                          "p (a c) -> p a c", a=8),
                                      in_=src_ap),
                          [tok])
            tok = EM("vector", TH("tensor_copy",
                                  dst[:].rearrange("p a c -> p (a c)"),
                                  wstg[:, 0:8 * cols]), [tok], sV, 1)
        w_tok = tok
        for dst, src in ((cvq, cvq_h), (cvk, cvk_h), (cvv, cvv_h)):
            dW.emit("sync", TH("dma_start", out=dst[:],
                               in_=src.rearrange("a p c -> p a c")))
        dW.emit("sync", TH("dma_start", out=cvm[:], in_=cvm_h))
        cv_tok = dW.emit("sync", TH("dma_start", out=mxb[:], in_=mxb_h))
        ctxw.close()

        xstg = [sb(nc.sbuf_tensor(f"xstg{i}", [128, D], F32)) for i in range(2)]
        xT = [sb(nc.sbuf_tensor("xT0", [128, 8, STRIP], BF16))] * 2
        xbf = sb(nc.sbuf_tensor("xbf", [128, D], BF16))
        qraw = sb(nc.sbuf_tensor("qraw", [128, 2, STRIP + 3], BF16))
        kraw = sb(nc.sbuf_tensor("kraw", [128, 2, STRIP + 3], BF16))
        vraw = sb(nc.sbuf_tensor("vraw", [128, 2, STRIP + 3], BF16))
        mixraw = sb(nc.sbuf_tensor("mixraw", [1, STRIP + 3], F32))
        cvt_a = sb(nc.sbuf_tensor("cvt_a", [128, 2, STRIP], BF16))
        cvt_b = sb(nc.sbuf_tensor("cvt_b", [128, 2, STRIP], BF16))
        q3 = sb(nc.sbuf_tensor("q3", [128, 2, STRIP], BF16))
        k3 = sb(nc.sbuf_tensor("k3", [128, 2, STRIP], BF16))
        v3 = sb(nc.sbuf_tensor("v3", [128, 2, STRIP], BF16))
        beta_row = sb(nc.sbuf_tensor("beta_row", [1, STRIP], F32))
        gam_row = sb(nc.sbuf_tensor("gam_row", [1, STRIP], F32))
        beta_bf = sb(nc.sbuf_tensor("beta_bf", [1, STRIP], BF16))
        mxs = sb(nc.sbuf_tensor("mxs", [1, STRIP], F32))
        l2r = sb(nc.sbuf_tensor("l2r", [1, 4, 512], F32))
        rr = sb(nc.sbuf_tensor("rr", [1, 4, 512], F32))
        rr_bf = sb(nc.sbuf_tensor("rr_bf", [1, 2, 512], BF16))
        g_bf = sb(nc.sbuf_tensor("g_bf", [1, STRIP], BF16))
        g_lo = sb(nc.sbuf_tensor("g_lo", [1, STRIP], BF16))
        Rrk = sb(nc.sbuf_tensor("Rrk", [128, STRIP], BF16))
        Rb = sb(nc.sbuf_tensor("Rb", [128, STRIP], BF16))
        Rg = sb(nc.sbuf_tensor("Rg", [128, STRIP], F32))
        Rg1m = sb(nc.sbuf_tensor("Rg1m", [128, STRIP], F32))
        khT = sb(nc.sbuf_tensor("khT", [128, 2, STRIP], BF16))
        kbT = sb(nc.sbuf_tensor("kbT", [128, 2, STRIP], BF16))
        vhT = sb(nc.sbuf_tensor("vhT", [128, 2, STRIP], BF16))
        vgb = sb(nc.sbuf_tensor("vgb", [128, 2, STRIP], BF16))
        ema_o = sb(nc.sbuf_tensor("ema_o", [128, 2, STRIP], F32))
        ema_bf = sb(nc.sbuf_tensor("ema_bf", [128, 2, STRIP], BF16))
        ema_st = sb(nc.sbuf_tensor("ema_st", [128, 2], F32))
        tokb = sb(nc.sbuf_tensor("tokb", [128, SPC, 4 * DK], BF16))
        sc_rq = sb(nc.sbuf_tensor("sc_rq", [128, SPC], F32))
        sc_mx = sb(nc.sbuf_tensor("sc_mx", [128, SPC], F32))
        S_sb = sb(nc.sbuf_tensor("S_sb", [128, 2, DK], BF16))
        S_f32 = sb(nc.sbuf_tensor("S_f32", [128, 2, DK], F32))
        nt_b = sb(nc.sbuf_tensor("nt_b", [128, SPC, 128], BF16))
        nn_b = sb(nc.sbuf_tensor("nn_b", [128, SPC, 128], BF16))
        xg = [sb(nc.sbuf_tensor(f"xgv{i}", [128, SPC, 128], BF16)) for i in range(2)]
        xgt = [sb(nc.sbuf_tensor(f"xgt{i}", [128, SPC, 128], BF16)) for i in range(2)]
        ccb = [sb(nc.sbuf_tensor(f"ccv{i}", [128, SPC, 128], BF16)) for i in range(2)]
        pt_b = sb(nc.sbuf_tensor("pt_b", [128, SPC, 128], BF16))
        u_b = sb(nc.sbuf_tensor("u_b", [128, SPC, DK], BF16))
        wt_b = sb(nc.sbuf_tensor("wt_b", [128, SPC, 2, 128], BF16))
        uadj = sb(nc.sbuf_tensor("uadj", [128, DK], BF16))
        o1 = sb(nc.sbuf_tensor("o1b", [128, DK], F32))
        tgg = sb(nc.sbuf_tensor("tgg", [128, DK], F32))
        ogg = sb(nc.sbuf_tensor("ogg", [128, DK], F32))
        onb = sb(nc.sbuf_tensor("onb", [128, DK], BF16))
        stat = sb(nc.sbuf_tensor("stat", [128, 4], F32))
        spcr = sb(nc.sbuf_tensor("spcr", [128, 2], F32))
        onT_sb = [sb(nc.sbuf_tensor("onTs0", [128, 2, STRIP], BF16))] * 2

        S_dl = sb(nc.psum_tensor("S_dl", [128, 512], F32))
        p_tr = sb(nc.psum_tensor("p_tr", [128, 1024], BF16))
        p_a = sb(nc.psum_tensor("p_a", [128, 512], F32))
        p_b = sb(nc.psum_tensor("p_b", [128, 512], F32))
        p_c = sb(nc.psum_tensor("p_c", [128, 512], F32))
        p_d = sb(nc.psum_tensor("p_d", [128, 512], F32))
        p_sm = sb(nc.psum_tensor("p_sm", [2, 1024], F32))

        spc_tok = [None]

        def spacer():
            spc_tok[0] = EM("vector", TH("memset", spcr[:, 0:1], 0.0),
                            [spc_tok[0]], sV, 1)

        z0 = EM("vector", TH("memset", qraw[:, :, 0:3], 0.0), [w_tok], sV, 1)
        EM("vector", TH("memset", kraw[:, :, 0:3], 0.0), (), sV, 1)
        EM("vector", TH("memset", vraw[:, :, 0:3], 0.0), (), sV, 1)
        EM("vector", TH("memset", mixraw[0:1, 0:3], 0.0), (), sV, 1)
        EM("vector", TH("memset", ema_st[:, :], 0.0), (), sV, 1)
        EM("vector", TH("memset", S_sb[:].rearrange("p a c -> p (a c)"), 0.0),
           (), sV, 1)
        EM("vector", TH("memset", S_f32[:].rearrange("p a c -> p (a c)"), 0.0),
           (), sV, 1)

        bar = prog.barrier()
        onT_free = [None, None]
        xstg_free = [None, None]
        sadd_tok = [None]

        for s in range(NSTRIP):
            off = s * STRIP
            par = s % 2
            xTs = xT[par]

            xt_evt = None
            for t in range(STRIP // 128):
                pp = t % 2
                tin = dX.emit("sync", TH(
                    "dma_start", out=xstg[pp][:],
                    in_=x_h[off + t * 128: off + (t + 1) * 128, :]),
                    [xstg_free[pp],
                     w_tok if (s == 0 and t < 2) else None])
                tc = EM("vector", TH("tensor_copy", xbf[:], xstg[pp][:]),
                        [tin, xt_evt], sV, 1)
                xstg_free[pp] = tc
                trs = None
                for ds in range(8):
                    trs = EM("tensor", TH("transpose",
                                          p_tr[:, ds * 128:(ds + 1) * 128],
                                          xbf[:, ds * 128:(ds + 1) * 128], IDB),
                             [tc, cst["tok"] if s == 0 and t == 0 else None],
                             sT, 1)
                xt_evt = EM("vector", TH("tensor_copy",
                                         xTs[:, :, t * 128:(t + 1) * 128],
                                         p_tr[:].rearrange("p (a c) -> p a c", a=8)),
                            [trs], sV, 1)
            xt_done = xt_evt

            if s == 0:
                hq = hk = hv = hm = z0
            else:
                hq = EM("vector", TH("tensor_copy", qraw[:, :, 0:3],
                                     qraw[:, :, STRIP:STRIP + 3]),
                        [bar], sV, 1)
                hk = EM("vector", TH("tensor_copy", kraw[:, :, 0:3],
                                     kraw[:, :, STRIP:STRIP + 3]), (), sV, 1)
                hv = EM("vector", TH("tensor_copy", vraw[:, :, 0:3],
                                     vraw[:, :, STRIP:STRIP + 3]), (), sV, 1)
                hm = EM("vector", TH("tensor_copy", mixraw[0:1, 0:3],
                                     mixraw[0:1, STRIP:STRIP + 3]), (), sV, 1)

            def proj(dstraw, wgt, hist_tok):
                ev = None
                for hf in range(2):
                    for cq in range(2):
                        pp = p_a if (hf * 2 + cq) % 2 == 0 else p_b
                        mmt = None
                        for ds in range(8):
                            mmt = EM("tensor", TH(
                                "matmul", pp[:],
                                wgt[:, ds, hf * 128:(hf + 1) * 128],
                                xTs[:, ds, cq * 512:(cq + 1) * 512],
                                start=(ds == 0), stop=(ds == 7)),
                                [xt_done if ds == 0 else None,
                                 w_tok if ds == 0 else None,
                                 ev if ds == 0 else None], sT, 1)
                        ev = EM("scalar", TH(
                            "copy", dstraw[:, hf, 3 + cq * 512:3 + (cq + 1) * 512],
                            pp[:]), [mmt, hist_tok], sA, 1)
                return ev
            pk = proj(kraw, wk, hk)
            pv = proj(vraw, wv, hv)
            pq = proj(qraw, wq, hq)

            bgmm = None
            for cq in range(2):
                for ds in range(8):
                    bgmm = EM("tensor", TH(
                        "matmul", p_sm[0:1, cq * 512:(cq + 1) * 512],
                        wbg[:, ds, 0:1], xTs[:, ds, cq * 512:(cq + 1) * 512],
                        start=(ds == 0), stop=(ds == 7)),
                        [xt_done, bar if s > 0 else None], sT, 1)
            gmm = []
            for cq, pp in ((0, p_c), (1, p_d)):
                t1 = None
                for ds in range(8):
                    t1 = EM("tensor", TH(
                        "matmul", pp[0:1, 0:512],
                        wbg[:, ds, 1:2], xTs[:, ds, cq * 512:(cq + 1) * 512],
                        start=(ds == 0), stop=(ds == 7)),
                        [xt_done, bar if s > 0 else None], sT, 1)
                gmm.append(t1)
            bg_ev = EM("scalar", TH("activation", out=beta_row[0:1, :],
                                    in_=p_sm[0:1, 0:1024], func=AF.Sigmoid),
                       [bgmm], sA, 1)
            g_ev0 = EM("scalar", TH("activation", out=gam_row[0:1, 0:512],
                                    in_=p_c[0:1, 0:512], func=AF.Sigmoid),
                       [gmm[0]], sA, 1)
            g_ev = EM("scalar", TH("activation", out=gam_row[0:1, 512:1024],
                                   in_=p_d[0:1, 0:512], func=AF.Sigmoid),
                      [gmm[1]], sA, 1)
            mxmm = None
            for cq in range(2):
                for ds in range(8):
                    mxmm = EM("tensor", TH(
                        "matmul", p_sm[0:1, cq * 512:(cq + 1) * 512],
                        wmx[:, ds, :], xTs[:, ds, cq * 512:(cq + 1) * 512],
                        start=(ds == 0), stop=(ds == 7)), [bg_ev], sT, 1)
            mx_ev = EM("scalar", TH("copy", mixraw[0:1, 3:3 + STRIP],
                                    p_sm[0:1, 0:1024]), [mxmm, hm], sA, 1)

            def conv(raw, cw, ev_tok):
                t_last = None
                for hf in range(2):
                    t0 = EM("vector", TH(
                        "tensor_scalar", out=cvt_a[:, hf, :], in0=raw[:, hf, 0:STRIP],
                        scalar1=cw[:, hf, 0:1], scalar2=None, op0=ALU.mult),
                        [ev_tok, cv_tok], sV, 1)
                    t1 = EM("vector", TH(
                        "scalar_tensor_tensor", out=cvt_b[:, hf, :],
                        in0=raw[:, hf, 1:1 + STRIP], scalar=cw[:, hf, 1:2],
                        in1=cvt_a[:, hf, :], op0=ALU.mult, op1=ALU.add),
                        [t0], sV, 1)
                    t2 = EM("vector", TH(
                        "scalar_tensor_tensor", out=cvt_a[:, hf, :],
                        in0=raw[:, hf, 2:2 + STRIP], scalar=cw[:, hf, 2:3],
                        in1=cvt_b[:, hf, :], op0=ALU.mult, op1=ALU.add),
                        [t1], sV, 1)
                    t_last = EM("vector", TH(
                        "scalar_tensor_tensor", out=cvt_b[:, hf, :],
                        in0=raw[:, hf, 3:3 + STRIP], scalar=cw[:, hf, 3:4],
                        in1=cvt_a[:, hf, :], op0=ALU.mult, op1=ALU.add),
                        [t2], sV, 1)
                return t_last

            cvk_t = conv(kraw, cvk, pk)
            sk1 = EM("scalar", TH("activation",
                                  out=cvt_a[:].rearrange("p a c -> p (a c)"),
                                  in_=cvt_b[:].rearrange("p a c -> p (a c)"),
                                  func=AF.Silu), [cvk_t], sA, 1)
            sk2 = EM("scalar", TH("activation",
                                  out=k3[:].rearrange("p a c -> p (a c)"),
                                  in_=cvt_a[:].rearrange("p a c -> p (a c)"),
                                  func=AF.Silu), [sk1], sA, 1)
            cvv_t = conv(vraw, cvv, sk2)
            sv1 = EM("scalar", TH("activation",
                                  out=v3[:].rearrange("p a c -> p (a c)"),
                                  in_=cvt_b[:].rearrange("p a c -> p (a c)"),
                                  func=AF.Silu), [cvv_t], sA, 1)
            cvq_t = conv(qraw, cvq, sv1)
            sq1 = EM("scalar", TH("activation",
                                  out=cvt_a[:].rearrange("p a c -> p (a c)"),
                                  in_=cvt_b[:].rearrange("p a c -> p (a c)"),
                                  func=AF.Silu), [cvq_t], sA, 1)
            sq2t = EM("scalar", TH("activation",
                                   out=q3[:].rearrange("p a c -> p (a c)"),
                                   in_=cvt_a[:].rearrange("p a c -> p (a c)"),
                                   func=AF.Silu), [sq1], sA, 1)
            m0 = EM("vector", TH("tensor_scalar", out=cvt_a[0:1, 0, :],
                                 in0=mixraw[0:1, 0:STRIP], scalar1=cvm[0:1, 0:1],
                                 scalar2=None, op0=ALU.mult), [mx_ev, sq2t], sV, 1)
            m1 = EM("vector", TH("scalar_tensor_tensor", out=cvt_b[0:1, 0, :],
                                 in0=mixraw[0:1, 1:1 + STRIP], scalar=cvm[0:1, 1:2],
                                 in1=cvt_a[0:1, 0, :], op0=ALU.mult, op1=ALU.add),
                    [m0], sV, 1)
            m2 = EM("vector", TH("scalar_tensor_tensor", out=cvt_a[0:1, 0, :],
                                 in0=mixraw[0:1, 2:2 + STRIP], scalar=cvm[0:1, 2:3],
                                 in1=cvt_b[0:1, 0, :], op0=ALU.mult, op1=ALU.add),
                    [m1], sV, 1)
            m3 = EM("vector", TH("scalar_tensor_tensor", out=cvt_b[0:1, 0, :],
                                 in0=mixraw[0:1, 3:3 + STRIP], scalar=cvm[0:1, 3:4],
                                 in1=cvt_a[0:1, 0, :], op0=ALU.mult, op1=ALU.add),
                    [m2], sV, 1)
            m4 = EM("scalar", TH("activation", out=cvt_a[0:1, 0, :],
                                 in_=cvt_b[0:1, 0, :], func=AF.Silu),
                    [m3], sA, 1)
            mx_t = EM("scalar", TH("activation", out=mxs[0:1, :],
                                   in_=cvt_a[0:1, 0, :], func=AF.Sigmoid,
                                   bias=mxb[0:1, 0:1], scale=1.0), [m4], sA, 1)

            sq = cvt_a
            sqk = EM("scalar", TH("activation",
                                  out=sq[:].rearrange("p a c -> p (a c)"),
                                  in_=k3[:].rearrange("p a c -> p (a c)"),
                                  func=AF.Square),
                     [sk2, mx_t, bar if s > 0 else None], sA, 1)
            ssk = None
            for cq in range(2):
                EM("tensor", TH("matmul", p_sm[0:1, cq * 512:(cq + 1) * 512],
                                cst["ones_col"][:], sq[:, 0, cq * 512:(cq + 1) * 512],
                                start=True, stop=False), [sqk, mx_ev], sT, 1)
                ssk = EM("tensor", TH("matmul", p_sm[0:1, cq * 512:(cq + 1) * 512],
                                      cst["ones_col"][:], sq[:, 1, cq * 512:(cq + 1) * 512],
                                      start=False, stop=True), (), sT, 1)
            sqq = EM("scalar", TH("activation",
                                  out=sq[:].rearrange("p a c -> p (a c)"),
                                  in_=q3[:].rearrange("p a c -> p (a c)"),
                                  func=AF.Square), [sq2t, ssk], sA, 1)
            ssq = []
            for cq, pp in ((0, p_a), (1, p_b)):
                EM("tensor", TH("matmul", pp[0:1, 0:512], cst["ones_col"][:],
                                sq[:, 0, cq * 512:(cq + 1) * 512],
                                start=True, stop=False), [sqq], sT, 1)
                t1 = EM("tensor", TH("matmul", pp[0:1, 0:512], cst["ones_col"][:],
                                     sq[:, 1, cq * 512:(cq + 1) * 512],
                                     start=False, stop=True), (), sT, 1)
                ssq.append(t1)
            srk = EM("scalar", TH("activation", out=l2r[0:1, 0:2, :],
                                  in_=p_sm[0:1, 0:1024], func=AF.Sqrt,
                                  bias=cst["eps"][0:1, 0:1], scale=1.0),
                     [ssk], sA, 1)
            srq0 = EM("scalar", TH("activation", out=l2r[0:1, 2, :],
                                   in_=p_a[0:1, 0:512], func=AF.Sqrt,
                                   bias=cst["eps"][0:1, 0:1], scale=1.0),
                      [ssq[0]], sA, 1)
            srq1 = EM("scalar", TH("activation", out=l2r[0:1, 3, :],
                                   in_=p_b[0:1, 0:512], func=AF.Sqrt,
                                   bias=cst["eps"][0:1, 0:1], scale=1.0),
                      [ssq[1]], sA, 1)
            rcp = EM("vector", TH("reciprocal",
                                  rr[0:1].rearrange("p a c -> p (a c)"),
                                  l2r[0:1].rearrange("p a c -> p (a c)")),
                     [srk, srq0, srq1], sV, 1)
            spacer()
            rkb = EM("vector", TH("tensor_copy",
                                  rr_bf[0:1].rearrange("p a c -> p (a c)"),
                                  rr[0:1, 0:2, :].rearrange("p a c -> p (a c)")),
                     [rcp], sV, 1)
            gb = EM("vector", TH("tensor_copy", g_bf[0:1, :], gam_row[0:1, :]),
                    [g_ev, g_ev0], sV, 1)
            spacer()
            gl = EM("vector", TH("tensor_tensor", g_lo[0:1, :], gam_row[0:1, :],
                                 g_bf[0:1, :], ALU.subtract), [gb], sV, 1)
            bgb = EM("vector", TH("tensor_copy", beta_bf[0:1, :], beta_row[0:1, :]),
                     [gl, bg_ev], sV, 1)

            ev_rrk = None
            for cq in range(2):
                pp = p_c if cq == 0 else p_d
                t1 = EM("tensor", TH("matmul", pp[:], cst["ones_row"][0:1, 0:128],
                                     rr_bf[0:1, cq, :], start=True, stop=True),
                        [rkb], sT, 1)
                ev_rrk = EM("scalar", TH("copy", Rrk[:, cq * 512:(cq + 1) * 512],
                                         pp[:]), [t1], sA, 1)
            ev_rb = None
            for cq in range(2):
                pp = p_c if cq == 0 else p_d
                t1 = EM("tensor", TH("matmul", pp[:], cst["ones_row"][0:1, 0:128],
                                     beta_bf[0:1, cq * 512:(cq + 1) * 512],
                                     start=True, stop=True), [bgb, ev_rrk], sT, 1)
                ev_rb = EM("scalar", TH("copy", Rb[:, cq * 512:(cq + 1) * 512],
                                        pp[:]), [t1], sA, 1)
            ev_rg = None
            for cq in range(2):
                pp = p_c if cq == 0 else p_d
                EM("tensor", TH("matmul", pp[:], cst["ones_row"][0:1, 0:128],
                                g_bf[0:1, cq * 512:(cq + 1) * 512],
                                start=True, stop=False), [bgb, ev_rb], sT, 1)
                t1 = EM("tensor", TH("matmul", pp[:], cst["ones_row"][0:1, 0:128],
                                     g_lo[0:1, cq * 512:(cq + 1) * 512],
                                     start=False, stop=True), (), sT, 1)
                ev_rg = EM("scalar", TH("copy", Rg[:, cq * 512:(cq + 1) * 512],
                                        pp[:]), [t1], sA, 1)
            rg1 = EM("vector", TH("tensor_scalar", out=Rg1m[:, :], in0=Rg[:, :],
                                  scalar1=-1.0, scalar2=1.0,
                                  op0=ALU.mult, op1=ALU.add), [ev_rg], sV, 1)

            nk = nb = nv = ng = None
            for hf in range(2):
                nk = EM("vector", TH("tensor_tensor", khT[:, hf, :], k3[:, hf, :],
                                     Rrk[:, :], ALU.mult), [ev_rrk, sk2], sV, 1)
                nb = EM("vector", TH("tensor_tensor", kbT[:, hf, :], khT[:, hf, :],
                                     Rb[:, :], ALU.mult), [ev_rb, nk], sV, 1)
                nv = EM("vector", TH("tensor_tensor", vhT[:, hf, :], v3[:, hf, :],
                                     Rb[:, :], ALU.mult), [sv1], sV, 1)
                ng = EM("vector", TH("tensor_tensor", vgb[:, hf, :], v3[:, hf, :],
                                     Rg1m[:, :], ALU.mult), [rg1], sV, 1)

            em0 = EM("vector", TH("tensor_tensor_scan", out=ema_o[:, 0, :],
                                  data0=Rg[:, :], data1=vgb[:, 0, :],
                                  initial=ema_st[:, 0:1], op0=ALU.mult,
                                  op1=ALU.add), [ng, ev_rg], sV, 1)
            em1 = EM("vector", TH("tensor_tensor_scan", out=ema_o[:, 1, :],
                                  data0=Rg[:, :], data1=vgb[:, 1, :],
                                  initial=ema_st[:, 1:2], op0=ALU.mult,
                                  op1=ALU.add), (), sV, 1)
            es0 = EM("vector", TH("tensor_copy", ema_st[:, 0:1],
                                  ema_o[:, 0, STRIP - 1:STRIP]), [em1], sV, 1)
            es1 = EM("vector", TH("tensor_copy", ema_st[:, 1:2],
                                  ema_o[:, 1, STRIP - 1:STRIP]), (), sV, 1)
            emb = EM("vector", TH("tensor_copy",
                                  ema_bf[:].rearrange("p a c -> p (a c)"),
                                  ema_o[:].rearrange("p a c -> p (a c)")),
                     [es1], sV, 1)

            rr_flat = rr[0:1].rearrange("p a c -> p (a c)")
            sc_mm = None
            for c in range(SPC):
                sc_mm = EM("tensor", TH("transpose", p_c[:, c:c + 1],
                                        rr_flat[:, 1024 + c * 128:1024 + (c + 1) * 128],
                                        mEYE[0:1, 0:1]),
                           [rcp if c == 0 else None, ev_rg if c == 0 else None],
                           sT, 1)
            for c in range(SPC):
                sc_mm = EM("tensor", TH("transpose", p_c[:, 8 + c:9 + c],
                                        mxs[0:1, c * 128:(c + 1) * 128],
                                        mEYE[0:1, 0:1]),
                           [mx_t if c == 0 else None], sT, 1)
            scv1 = EM("vector", TH("tensor_copy", sc_rq[:, :], p_c[:, 0:8]),
                      [sc_mm], sV, 1)
            scv2 = EM("vector", TH("tensor_copy", sc_mx[:, :], p_c[:, 8:16]),
                      (), sV, 1)

            tok_evs = []
            for c in range(SPC):
                cw = slice(c * 128, (c + 1) * 128)
                trs = None
                srcs = (khT, kbT, vhT, ema_bf)
                deps = (nk, nb, nv, emb)
                for j in range(4):
                    for hf in range(2):
                        trs = EM("tensor", TH("transpose",
                                              p_tr[:, j * 256 + hf * 128:
                                                   j * 256 + (hf + 1) * 128],
                                              srcs[j][:, hf, cw], IDB),
                                 [deps[j] if hf == 0 else None,
                                  tok_evs[c - 1] if (j == 0 and hf == 0 and c > 0)
                                  else None,
                                  scv2 if (j == 0 and hf == 0 and c == 0) else None],
                                 sT, 1)
                ev = EM("vector", TH("tensor_copy", tokb[:, c, :], p_tr[:, :]),
                        [trs], sV, 1)
                tok_evs.append(ev)

            nt_toks = [None] * SPC
            nn_toks = [None] * SPC
            for wv_i in range(2):
                mm_last = None
                for ci in range(4):
                    c = wv_i * 4 + ci
                    cw = slice(c * 128, (c + 1) * 128)
                    EM("tensor", TH("matmul", p_a[:, ci * 128:(ci + 1) * 128],
                                    khT[:, 0, cw], kbT[:, 0, cw],
                                    start=True, stop=False),
                       [nb, nn_toks[3] if (wv_i == 1 and ci == 0) else None,
                        ssq[1] if (wv_i == 0 and ci == 0) else None], sT, 1)
                    EM("tensor", TH("matmul",
                                    p_a[:, ci * 128:(ci + 1) * 128],
                                    khT[:, 1, cw], kbT[:, 1, cw],
                                    start=False, stop=True), (), sT, 1)
                    EM("tensor", TH("matmul", p_b[:, ci * 128:(ci + 1) * 128],
                                    kbT[:, 0, cw], khT[:, 0, cw],
                                    start=True, stop=False), (), sT, 1)
                    mm_last = EM("tensor", TH("matmul",
                                              p_b[:, ci * 128:(ci + 1) * 128],
                                              kbT[:, 1, cw], khT[:, 1, cw],
                                              start=False, stop=True), (), sT, 1)
                for ci in range(4):
                    c = wv_i * 4 + ci
                    nt_toks[c] = EM("vector", TH(
                        "tensor_tensor", nt_b[:, c, :],
                        p_a[:, ci * 128:(ci + 1) * 128], mTRIUS, ALU.mult),
                        [mm_last], sV, 1)
                    nn_toks[c] = EM("vector", TH(
                        "tensor_tensor", nn_b[:, c, :],
                        p_b[:, ci * 128:(ci + 1) * 128], mTRILS, ALU.mult),
                        [mm_last], sV, 1)
            cc_tok = [None] * SPC
            for c in range(SPC):
                cc_tok[c] = EM("vector", TH("tensor_tensor", ccb[0][:, c, :],
                                            mEYE, nt_b[:, c, :], ALU.subtract),
                               [nt_toks[c]], sV, 1)

            x_cur, xt_cur = nt_b, nn_b
            x_tok, xt_tok = nt_toks, nn_toks
            cgen = 0
            prev_ncc_last = scv2
            for lvl in range(6):
                xng = xg[lvl % 2]
                xtg = xgt[lvl % 2]
                nx_tok = [None] * SPC
                nxt_tok = [None] * SPC
                ncc_tok = [None] * SPC
                for wv_i in range(2):
                    mm_last = None
                    for ci in range(4):
                        c = wv_i * 4 + ci
                        if wv_i == 0 and ci == 0:
                            guard = (nn_toks[SPC - 1] if lvl == 0
                                     else xt_tok[SPC - 1])
                        elif wv_i == 1 and ci == 0:
                            guard = nxt_tok[3]
                        else:
                            guard = None
                        if lvl < 5:
                            EM("tensor", TH(
                                "matmul", p_a[:, ci * 128:(ci + 1) * 128],
                                xt_cur[:, c, :], x_cur[:, c, :],
                                start=True, stop=True),
                                [x_tok[c], xt_tok[c], guard], sT, 1)
                        mm_last = EM("tensor", TH(
                            "matmul", p_b[:, ci * 128:(ci + 1) * 128],
                            x_cur[:, c, :], xt_cur[:, c, :],
                            start=True, stop=True),
                            [x_tok[c], xt_tok[c], guard], sT, 1)
                    for ci in range(4):
                        c = wv_i * 4 + ci
                        if lvl < 5:
                            nx_tok[c] = EM("vector", TH(
                                "tensor_copy", xng[:, c, :],
                                p_a[:, ci * 128:(ci + 1) * 128]), [mm_last], sV, 1)
                        nxt_tok[c] = EM("vector", TH(
                            "tensor_copy", xtg[:, c, :],
                            p_b[:, ci * 128:(ci + 1) * 128]), [mm_last], sV, 1)
                    mm_cu_last = None
                    for ci in range(4):
                        c = wv_i * 4 + ci
                        mm_cu_last = EM("tensor", TH(
                            "matmul", p_c[:, ci * 128:(ci + 1) * 128],
                            xtg[:, c, :], ccb[cgen][:, c, :],
                            start=True, stop=True),
                            [nxt_tok[c], cc_tok[c],
                             prev_ncc_last if ci == 0 else None], sT, 1)
                    for ci in range(4):
                        c = wv_i * 4 + ci
                        ncc_tok[c] = EM("vector", TH(
                            "tensor_tensor", ccb[1 - cgen][:, c, :],
                            p_c[:, ci * 128:(ci + 1) * 128],
                            ccb[cgen][:, c, :], ALU.add), [mm_cu_last], sV, 1)
                    prev_ncc_last = ncc_tok[wv_i * 4 + 3]
                x_cur, xt_cur = xng, xtg
                x_tok, xt_tok = nx_tok, nxt_tok
                cc_tok = ncc_tok
                cgen = 1 - cgen
            CCF = ccb[cgen]
            cc_all = cc_tok[SPC - 1]

            u_toks = [None] * SPC
            wt_toks = [None] * SPC
            pt_toks = [None] * SPC
            for c in range(SPC):
                cw = slice(c * 128, (c + 1) * 128)
                sl = c % 2
                mm_u = EM("tensor", TH(
                    "matmul", p_a[:, sl * 256:(sl + 1) * 256],
                    CCF[:, c, :], tokb[:, c, 512:768],
                    start=True, stop=True),
                    [cc_all, tok_evs[c],
                     u_toks[c - 1] if c >= 1 else None], sT, 1)
                mm_w = None
                for hf in range(2):
                    mm_w = EM("tensor", TH(
                        "matmul",
                        p_b[:, sl * 256 + hf * 128:sl * 256 + (hf + 1) * 128],
                        tokb[:, c, 256 + hf * 128:256 + (hf + 1) * 128],
                        CCF[:, c, :], start=True, stop=True),
                        [wt_toks[c - 1] if (c >= 1 and hf == 0) else None],
                        sT, 1)
                EM("tensor", TH(
                    "matmul", p_c[:, sl * 128:(sl + 1) * 128],
                    khT[:, 0, cw], q3[:, 0, cw], start=True, stop=False),
                    [sq2t, pt_toks[c - 1] if c >= 1 else None,
                     prev_ncc_last if c == 0 else None], sT, 1)
                mm_p = EM("tensor", TH(
                    "matmul", p_c[:, sl * 128:(sl + 1) * 128],
                    khT[:, 1, cw], q3[:, 1, cw], start=False, stop=True),
                    (), sT, 1)
                u_toks[c] = EM("vector", TH(
                    "tensor_copy", u_b[:, c, :],
                    p_a[:, sl * 256:(sl + 1) * 256]), [mm_u], sV, 1)
                wt_toks[c] = EM("vector", TH(
                    "tensor_copy", wt_b[:, c, :, :].rearrange("p a c -> p (a c)"),
                    p_b[:, sl * 256:(sl + 1) * 256]), [mm_w], sV, 1)
                pt_toks[c] = EM("vector", TH(
                    "tensor_tensor", pt_b[:, c, :],
                    p_c[:, sl * 128:(sl + 1) * 128], mTRIUI, ALU.mult),
                    [mm_p], sV, 1)

            ssb_tok = None
            ot1_prev = [None]
            for c in range(SPC):
                cw = slice(c * 128, (c + 1) * 128)
                slot = s * SPC + c
                EM("tensor", TH("matmul", p_c[:, 0:256], wt_b[:, c, 0, :],
                                S_sb[:, 0, :], start=True, stop=False),
                   [wt_toks[c], ssb_tok,
                    pt_toks[SPC - 1] if c == 0 else None], sT, 1)
                ws1 = EM("tensor", TH("matmul", p_c[:, 0:256], wt_b[:, c, 1, :],
                                      S_sb[:, 1, :], start=False, stop=True),
                         (), sT, 1)
                ua = EM("vector", TH("tensor_tensor", uadj[:, :], u_b[:, c, :],
                                     p_c[:, 0:256], ALU.subtract),
                        [ws1, u_toks[c]], sV, 1)
                EM("tensor", TH("matmul", p_d[:, 0:256], q3[:, 0, cw],
                                S_sb[:, 0, :], start=True, stop=False),
                   [ua, ev_rg if c == 0 else None, ot1_prev[0]], sT, 1)
                EM("tensor", TH("matmul", p_d[:, 0:256], q3[:, 1, cw],
                                S_sb[:, 1, :], start=False, stop=False),
                   (), sT, 1)
                om = EM("tensor", TH("matmul", p_d[:, 0:256], pt_b[:, c, :],
                                     uadj[:, :], start=False, stop=True),
                        (), sT, 1)
                EM("tensor", TH("matmul", S_dl[:, 0:256],
                                tokb[:, c, 0:128], uadj[:, :],
                                start=True, stop=False),
                   [sadd_tok[0]], sT, 1)
                su = EM("tensor", TH("matmul", S_dl[:, 256:512],
                                     tokb[:, c, 128:256], uadj[:, :],
                                     start=False, stop=True), (), sT, 1)
                sadd_tok[0] = EM("vector", TH(
                    "tensor_tensor", S_f32[:].rearrange("p a c -> p (a c)"),
                    S_f32[:].rearrange("p a c -> p (a c)"), S_dl[:, :],
                    ALU.add), [su], sV, 1)
                ssb_tok = EM("scalar", TH("copy",
                                          S_sb[:].rearrange("p a c -> p (a c)"),
                                          S_f32[:].rearrange("p a c -> p (a c)")),
                            [sadd_tok[0]], sA, 1)
                ot1 = EM("vector", TH("tensor_scalar", out=o1[:, :],
                                      in0=p_d[:, 0:256], scalar1=sc_rq[:, c:c + 1],
                                      scalar2=None, op0=ALU.mult),
                         [om, scv1], sV, 1)
                ot1_prev[0] = ot1
                ot2 = EM("vector", TH("tensor_tensor", tgg[:, :],
                                      tokb[:, c, 768:1024], o1[:, :],
                                      ALU.subtract), [ot1], sV, 1)
                ot3 = EM("vector", TH("scalar_tensor_tensor", out=ogg[:, :],
                                      in0=tgg[:, :], scalar=sc_mx[:, c:c + 1],
                                      in1=o1[:, :], op0=ALU.mult, op1=ALU.add),
                         [ot2, scv2], sV, 1)
                sq2e = EM("scalar", TH("activation", out=tgg[:, :], in_=ogg[:, :],
                                       func=AF.Square), [ot3], sA, 1)
                rs = EM("vector", TH("reduce_sum", stat[:, 0:1], tgg[:, :],
                                     axis=AX.X), [sq2e], sV, 1)
                st = EM("scalar", TH("activation", out=stat[:, 1:2],
                                     in_=stat[:, 0:1], func=AF.Sqrt,
                                     bias=cst["eps"][:, 1:2], scale=1.0 / DK),
                        [rs], sA, 1)
                rc = EM("vector", TH("reciprocal", stat[:, 2:3], stat[:, 1:2]),
                        [st], sV, 1)
                spacer()
                onb_t = EM("vector", TH("tensor_scalar", out=onb[:, :],
                                        in0=ogg[:, :], scalar1=stat[:, 2:3],
                                        scalar2=None, op0=ALU.mult), [rc], sV, 1)
                EM("tensor", TH("transpose", p_tr[:, 0:128],
                                onb[:, 0:128], IDB), [onb_t], sT, 1)
                tr1 = EM("tensor", TH("transpose", p_tr[:, 128:256],
                                      onb[:, 128:256], IDB), (), sT, 1)
                EM("vector", TH("tensor_copy",
                                onT_sb[par][:, :, cw],
                                p_tr[:, 0:256].rearrange("p (a c) -> p a c", a=2)),
                   [tr1, onT_free[0] if c == 0 else None], sV, 1)
            out_ready = prog.barrier(("vector",))
            onT_free[0] = dO.emit("gpsimd", TH(
                "dma_start",
                out=onT_h[:, :, off:off + STRIP].rearrange("s p t -> p s t"),
                in_=onT_sb[par][:, :, :]), [out_ready])

            bar = prog.barrier()

        EM("gpsimd", TH("nop"), [dO.total_tok()])

        with nc.Block() as block:
            prog.run(block)
    return nc


def build_ffn():
    nc = bass.Bass("TRN2", target_bir_lowering=False, debug=False)
    oc_h = nc.dram_tensor("oc", [D, TW], BF16, kind="ExternalInput").ap()
    xr_h = nc.dram_tensor("xr", [TW, D], F32, kind="ExternalInput").ap()
    wo_h = nc.dram_tensor("wo", [D, D], F32, kind="ExternalInput").ap()
    rms_h = nc.dram_tensor("rmsc", [128, 8], F32, kind="ExternalInput").ap()
    sw_h = nc.dram_tensor("sww", [D, NE], F32, kind="ExternalInput").ap()
    swb_h = nc.dram_tensor("swbi", [1, NE], F32, kind="ExternalInput").ap()
    w1_h = nc.dram_tensor("w1", [NE, D, FF], F32, kind="ExternalInput").ap()
    b1_h = nc.dram_tensor("b1r", [NE, 1, FF], F32, kind="ExternalInput").ap()
    w2_h = nc.dram_tensor("w2", [NE, FF, D], F32, kind="ExternalInput").ap()
    b2_h = nc.dram_tensor("b2", [NE, D], F32, kind="ExternalInput").ap()
    out_h = nc.dram_tensor("out", [TW, D], F32, kind="ExternalOutput").ap()
    w1bf = nc.dram_tensor("w1bf", [NE, 8, 128, FF], BF16, kind="Internal").ap()
    w2bf = nc.dram_tensor("w2bf", [NE, 32, 128, D], BF16, kind="Internal").ap()

    ctx = ExitStack()
    with ctx:
        prog = Prog(nc, ctx)
        sb = ctx.enter_context
        EM = prog.emit

        dI = DmaStream(prog, 4)
        dWF = DmaStream(prog, 2)
        dWB = DmaStream(prog, 4)
        dSL = DmaStream(prog, 8)
        dO = DmaStream(prog, 2)
        sT = prog.sem("sT")
        sV = prog.sem("sV")
        sA = prog.sem("sA")
        sG = prog.sem("sG")

        cst = make_consts(nc, ctx, prog, dI, masks=False)
        IDB = cst["ident"][:]
        su8_d = nc.inline_tensor(np.triu(np.ones((8, 8), dtype=BF), 1), "su8d").ap()
        su8 = sb(nc.sbuf_tensor("su8", [8, 8], BF16))
        dI.emit("sync", TH("dma_start", out=su8[:], in_=su8_d))
        id8_d = nc.inline_tensor(np.eye(8, dtype=BF), "id8d").ap()
        id8 = sb(nc.sbuf_tensor("id8", [8, 8], BF16))
        c_tok = dI.emit("sync", TH("dma_start", out=id8[:], in_=id8_d))

        x1T = sb(nc.sbuf_tensor("x1T", [128, 8, TW], BF16))
        out_acc = sb(nc.sbuf_tensor("out_acc", [128, NT, D], BF16))
        maskp = sb(nc.sbuf_tensor("maskp", [128, NT, NE], F32))
        lgs_all = sb(nc.sbuf_tensor("lgs_all", [128, NT, NE], F32))
        b1rb = sb(nc.sbuf_tensor("b1rb", [1, FF], BF16))
        wstg = [sb(nc.sbuf_tensor(f"wstg{i}", [128, 2048], F32)) for i in range(2)]
        w2cv = sb(nc.sbuf_tensor("w2cv", [128, 2, D], BF16))
        w1cv = sb(nc.sbuf_tensor("w1cv", [128, 2048], BF16))
        w1sl = [sb(nc.sbuf_tensor(f"w1sl{i}", [128, 8, 512], BF16)) for i in range(2)]
        w2sl = [sb(nc.sbuf_tensor(f"w2sl{i}", [128, 4, D], BF16)) for i in range(2)]
        rmsc = sb(nc.sbuf_tensor("rmscs", [128, 8], F32))
        swb = sb(nc.sbuf_tensor("swwb", [128, 8, NE], BF16))
        swbb = sb(nc.sbuf_tensor("swbbs", [1, NE], BF16))
        b2b = sb(nc.sbuf_tensor("b2bs", [8, D], BF16))
        xt_s = [sb(nc.sbuf_tensor(f"xts{i}", [128, D], F32)) for i in range(2)]
        ybuf = sb(nc.sbuf_tensor("ybuf", [128, 2, 512], F32))
        ysq = sb(nc.sbuf_tensor("ysq", [128, D], F32))
        lst = sb(nc.sbuf_tensor("lst", [128, 8], F32))
        expo = sb(nc.sbuf_tensor("expo", [128, NE], F32))
        eqb = sb(nc.sbuf_tensor("eqb", [128, NE], BF16))
        eqT = sb(nc.sbuf_tensor("eqT", [8, 128], BF16))
        zrow = sb(nc.sbuf_tensor("zrow", [8, 128], BF16))
        mTt = sb(nc.sbuf_tensor("mTt", [8, 128], BF16))
        msk = sb(nc.sbuf_tensor("msk", [128, NE], BF16))
        gbx = [sb(nc.sbuf_tensor(f"gbx{i}", [128, 512], BF16)) for i in range(3)]
        gb2 = [sb(nc.sbuf_tensor(f"gb2{i}", [128, 512], BF16)) for i in range(3)]
        gb3 = [sb(nc.sbuf_tensor(f"gb3{i}", [128, 512], BF16)) for i in range(3)]
        spcr = sb(nc.sbuf_tensor("spcr", [128, 2], F32))

        p_y = [sb(nc.psum_tensor(f"p_y{i}", [128, 512], F32)) for i in range(7)]
        ctx2 = ExitStack()
        p_trb = ctx2.enter_context(nc.psum_tensor("p_trb", [128, 1024], BF16))
        ctxo = ExitStack()
        oc = ctxo.enter_context(nc.sbuf_tensor("ocb", [128, 8, TW], BF16))
        wob = ctxo.enter_context(nc.sbuf_tensor("wob", [128, 8, D], BF16))

        spc_tok = [None]

        def spacer():
            spc_tok[0] = EM("vector", TH("memset", spcr[:, 0:1], 0.0),
                            [spc_tok[0]], sV, 1)

        t_oc = dI.emit("sync", TH("dma_start", out=oc[:],
                                   in_=oc_h.rearrange("(a p) t -> p a t", p=128)))
        t_rm = dI.emit("sync", TH("dma_start", out=rmsc[:], in_=rms_h))
        wo_tok = None
        wo_toks = []
        for pc in range(4):
            stg = wstg[pc % 2]
            t_wo = dI.emit("sync", TH(
                "dma_start",
                out=stg[:].rearrange("p (a c) -> p a c", a=2),
                in_=wo_h[pc * 256:(pc + 1) * 256, :]
                .rearrange("(a p) c -> p a c", p=128)),
                [wo_toks[pc - 2] if pc >= 2 else None])
            for ds in range(2):
                wo_tok = EM("vector", TH(
                    "tensor_scalar", out=wob[:, pc * 2 + ds, :],
                    in0=stg[:, ds * 1024:(ds + 1) * 1024],
                    scalar1=rmsc[:, pc * 2 + ds:pc * 2 + ds + 1], scalar2=None,
                    op0=ALU.mult), [t_wo, t_rm], sV, 1)
            wo_toks.append(wo_tok)
        t_sw = dI.emit("sync", TH(
            "dma_start", out=ysq[:, 0:64].rearrange("p (a c) -> p a c", a=8),
            in_=sw_h.rearrange("(a p) c -> p a c", p=128)))
        t_sw = EM("vector", TH("tensor_copy",
                               swb[:].rearrange("p a c -> p (a c)"),
                               ysq[:, 0:64]), [t_sw], sV, 1)
        t_swb = dI.emit("sync", TH("dma_start", out=lst[0:1, 0:8], in_=swb_h))
        t_swb = EM("vector", TH("tensor_copy", swbb[:], lst[0:1, 0:8]),
                   [t_swb], sV, 1)
        t_b2 = dI.emit("sync", TH("dma_start", out=xt_s[0][0:8, :], in_=b2_h))
        t_b2 = EM("vector", TH("tensor_copy", b2b[:, :], xt_s[0][0:8, :]),
                  [t_b2, t_sw], sV, 1)

        stg_free = [None, None]
        conv1_done = [None] * NE
        conv2_done = [None] * NE
        w1cv_free = [None]
        w2cv_free = [None]

        def emit_wconv(e, first=False):
            lastw = None
            for pc in range(8):
                for hf in range(2):
                    j = pc * 2 + hf
                    pp = j % 2
                    ti = dWF.emit("sync", TH(
                        "dma_start", out=wstg[pp][:],
                        in_=w1_h[e, pc * 128:(pc + 1) * 128,
                                 hf * 2048:(hf + 1) * 2048]),
                        [stg_free[pp],
                         wo_toks[-1] if (first and j <= 1) else None])
                    tc = EM("vector", TH("tensor_copy", w1cv[:, :], wstg[pp][:]),
                            [ti, w1cv_free[0]], sV, 1)
                    stg_free[pp] = tc
                    lastw = dWB.emit("gpsimd", TH(
                        "dma_start",
                        out=w1bf[e, pc, :, hf * 2048:(hf + 1) * 2048],
                        in_=w1cv[:, :]), [tc])
                    w1cv_free[0] = lastw
            conv1_done[e] = lastw
            lastw = None
            for pc in range(16):
                pp = pc % 2
                ti = dWF.emit("sync", TH(
                    "dma_start",
                    out=wstg[pp][:].rearrange("p (a d) -> p a d", a=2),
                    in_=w2_h[e, pc * 256:(pc + 1) * 256, :]
                    .rearrange("(a p) d -> p a d", p=128)),
                    [stg_free[pp]])
                tc = EM("scalar", TH("copy",
                                     w2cv[:].rearrange("p a d -> p (a d)"),
                                     wstg[pp][:]), [ti, w2cv_free[0]], sA, 1)
                stg_free[pp] = tc
                lastw = dWB.emit("gpsimd", TH(
                    "dma_start",
                    out=w2bf[e, pc * 2:(pc + 1) * 2].rearrange("s p d -> p s d"),
                    in_=w2cv[:]), [tc])
                w2cv_free[0] = lastw
            conv2_done[e] = lastw

        emit_wconv(0, first=True)
        emit_wconv(1)

        xt_free = [None, t_b2]
        p_free = {}
        lgs_toks = [None] * NT
        x1T_tok = [None] * NT
        oa_init = [None] * NT
        srt_prev = None
        s2_prev = None
        for t in range(NT):
            pp = t % 2
            tx = dI.emit("sync", TH("dma_start", out=xt_s[pp][:],
                                     in_=xr_h[t * 128:(t + 1) * 128, :]),
                         [xt_free[pp] if t > 1 else (t_b2 if pp == 0 else None)])
            yv = {}
            for dh in range(2):
                mm = None
                for ds in range(8):
                    mm = EM("tensor", TH(
                        "matmul", p_y[dh][:], oc[:, ds, t * 128:(t + 1) * 128],
                        wob[:, ds, dh * 512:(dh + 1) * 512],
                        start=(ds == 0), stop=(ds == 7)),
                        [t_oc if ds == 0 else None,
                         wo_toks[-1] if ds == 0 else None,
                         p_free.get(dh) if ds == 0 else None,
                         c_tok if (t == 0 and ds == 0) else None], sT, 1)
                yv[dh] = EM("vector", TH(
                    "tensor_tensor", ybuf[:, dh, :], p_y[dh][:],
                    xt_s[pp][:, dh * 512:(dh + 1) * 512], ALU.add),
                    [mm, tx, srt_prev if dh == 0 else None], sV, 1)
                p_free[dh] = yv[dh]
            yflat = ybuf[:].rearrange("p a c -> p (a c)")
            s1 = EM("vector", TH("reduce_sum", lst[:, 0:1], yflat, axis=AX.X),
                    [yv[1], srt_prev], sV, 1)
            sqy = EM("scalar", TH("activation", out=ysq[:, :], in_=yflat,
                                  func=AF.Square), [yv[1], t_b2, s2_prev], sA, 1)
            s2 = EM("vector", TH("reduce_sum", lst[:, 1:2], ysq[:, :], axis=AX.X),
                    [sqy], sV, 1)
            s2_prev = s2
            mn = EM("scalar", TH("mul", lst[:, 2:3], lst[:, 0:1], 1.0 / D),
                    [s1], sA, 1)
            msq = EM("scalar", TH("activation", out=lst[:, 3:4], in_=lst[:, 2:3],
                                  func=AF.Square), [mn], sA, 1)
            var = EM("vector", TH("tensor_scalar", out=lst[:, 4:5], in0=lst[:, 1:2],
                                  scalar1=1.0 / D, scalar2=lst[:, 3:4],
                                  op0=ALU.mult, op1=ALU.subtract),
                     [s2, msq], sV, 1)
            srt = EM("scalar", TH("activation", out=lst[:, 5:6], in_=lst[:, 4:5],
                                  func=AF.Sqrt, bias=cst["eps"][:, 1:2],
                                  scale=1.0), [var], sA, 1)
            srt_prev = srt
            rst = EM("vector", TH("reciprocal", lst[:, 6:7], lst[:, 5:6]),
                     [srt], sV, 1)
            spacer()
            x1t = EM("vector", TH("tensor_scalar", out=out_acc[:, t, :], in0=yflat,
                                  scalar1=lst[:, 2:3], scalar2=lst[:, 6:7],
                                  op0=ALU.subtract, op1=ALU.mult),
                     [rst], sV, 1)
            oa_init[t] = x1t
            xt_free[pp] = x1t
            trx = None
            for ds in range(8):
                trx = EM("tensor", TH("transpose",
                                      p_trb[:, ds * 128:(ds + 1) * 128],
                                      out_acc[:, t, ds * 128:(ds + 1) * 128], IDB),
                         [x1t if ds == 0 else None,
                          x1T_tok[t - 1] if (ds == 0 and t > 0) else None], sT, 1)
            x1T_tok[t] = EM("vector", TH("tensor_copy",
                                         x1T[:, :, t * 128:(t + 1) * 128],
                                         p_trb[:].rearrange("p (a c) -> p a c", a=8)),
                            [trx], sV, 1)
            lmm = None
            for ds in range(8):
                lmm = EM("tensor", TH("matmul", p_y[2][:, 0:NE],
                                      x1T[:, ds, t * 128:(t + 1) * 128],
                                      swb[:, ds, :], start=(ds == 0), stop=False),
                         [x1T_tok[t] if ds == 0 else None,
                          t_sw if ds == 0 else None,
                          lgs_toks[t - 1] if (ds == 0 and t > 0) else None], sT, 1)
            lmm = EM("tensor", TH("matmul", p_y[2][:, 0:NE],
                                  cst["ones_row"][0:1, 0:128], swbb[0:1, :],
                                  start=False, stop=True), [t_swb], sT, 1)
            lgs_toks[t] = EM("vector", TH("tensor_copy", lgs_all[:, t, :],
                                          p_y[2][:, 0:NE]), [lmm], sV, 1)

        route_tok = [None] * NT
        mtt_tok = None
        ex_prev = srt_prev
        for t in range(NT):
            lm = EM("vector", TH("reduce_max", lst[:, 0:1], lgs_all[:, t, :],
                                 axis=AX.X),
                    [lgs_toks[NT - 1] if t == 0 else None, ex_prev], sV, 1)
            nlm = EM("scalar", TH("mul", lst[:, 1:2], lst[:, 0:1], -1.0),
                     [lm], sA, 1)
            ex = EM("scalar", TH("activation", out=expo[:, :], in_=lgs_all[:, t, :],
                                 func=AF.Exp, bias=lst[:, 1:2], scale=1.0),
                    [nlm], sA, 1)
            ex_prev = ex
            sm = EM("vector", TH("reduce_sum", lst[:, 2:3], expo[:, :], axis=AX.X),
                    [ex], sV, 1)
            eq = EM("vector", TH("tensor_single_scalar", eqb[:, :],
                                 lgs_all[:, t, :], lst[:, 0:1], ALU.is_ge),
                    [sm], sV, 1)
            rpm = EM("vector", TH("reciprocal", lst[:, 3:4], lst[:, 2:3]),
                     [eq], sV, 1)
            tr_eq = EM("tensor", TH("transpose", p_trb[0:8, 0:128], eqb[:, :], IDB),
                       [eq, mtt_tok], sT, 1)
            eqt = EM("vector", TH("tensor_copy", eqT[:, :], p_trb[0:8, 0:128]),
                     [tr_eq], sV, 1)
            cmm = EM("tensor", TH("matmul", p_y[3][0:8, 0:128], su8[:, :],
                                  eqT[:, :], start=True, stop=True),
                     [eqt, route_tok[t - 1] if t > 0 else None], sT, 1)
            zr = EM("vector", TH("tensor_single_scalar", zrow[:, :],
                                 p_y[3][0:8, 0:128], 0.0, ALU.is_equal),
                    [cmm], sV, 1)
            mt = EM("vector", TH("tensor_tensor", mTt[:, :], eqT[:, :],
                                 zrow[:, :], ALU.mult), [zr], sV, 1)
            tr_m = EM("tensor", TH("transpose", p_trb[0:128, 128:136], mTt[:, :],
                                   id8[:, :]), [mt], sT, 1)
            mk = EM("vector", TH("tensor_copy", msk[:, :], p_trb[0:128, 128:136]),
                    [tr_m], sV, 1)
            mpq = EM("vector", TH("tensor_scalar", out=maskp[:, t, :],
                                  in0=msk[:, :], scalar1=lst[:, 3:4], scalar2=None,
                                  op0=ALU.mult), [mk, rpm], sV, 1)
            ini = None
            for dh in range(2):
                bmm = EM("tensor", TH("matmul", p_y[4 + dh][:], mTt[:, :],
                                      b2b[:, dh * 512:(dh + 1) * 512],
                                      start=True, stop=True),
                         [mt, t_b2, route_tok[t - 1] if t >= 1 else None], sT, 1)
                ini = EM("vector", TH("scalar_tensor_tensor",
                                      out=out_acc[:, t, dh * 512:(dh + 1) * 512],
                                      in0=p_y[4 + dh][:], scalar=lst[:, 3:4],
                                      in1=out_acc[:, t, dh * 512:(dh + 1) * 512],
                                      op0=ALU.mult, op1=ALU.add),
                         [bmm, mpq if dh == 0 else None, oa_init[t]], sV, 1)
            route_tok[t] = ini
            mtt_tok = ini

        ctx2.close()
        p_y.append(sb(nc.psum_tensor("p_y7", [128, 512], F32)))
        bar_b2 = prog.barrier()
        ctxo.close()
        hT = sb(nc.sbuf_tensor("hT", [128, 32, 512], BF16))

        w1sl_free = [None, None]
        w2sl_free = [None, None]
        py_free = [None] * 8
        hT_rd = [None] * 32
        oa_tok = {t: route_tok[t] for t in range(NT)}
        b1_tok = None
        prev_h_all = None
        gel_free = [None] * 3
        h_psum_free = [None] * 4
        for e in range(NE):
            if e + 2 < NE:
                emit_wconv(e + 2)
            for j in range(4):
                b1_tok = dI.emit("sync", TH(
                    "dma_start", out=xt_s[0][0:1, :],
                    in_=b1_h[e, :, j * 1024:(j + 1) * 1024]),
                    [b1_tok, bar_b2 if (e == 0 and j == 0) else None])
                b1_tok = EM("vector", TH("tensor_copy",
                                         b1rb[0:1, j * 1024:(j + 1) * 1024],
                                         xt_s[0][0:1, :]),
                            [b1_tok, prev_h_all if j == 0 else None], sV, 1)
            for q in range(4):
                qs = slice(q * 512, (q + 1) * 512)
                hts = []
                for g in range(8):
                    gp = g % 2
                    t_w1 = dSL.emit("scalar", TH(
                        "dma_start", out=w1sl[gp][:],
                        in_=w1bf[e, :, :, g * 512:(g + 1) * 512]
                        .rearrange("a p c -> p a c")),
                        [w1sl_free[gp], conv1_done[e] if g <= 1 else None])
                    for i in range(4):
                        fs = g * 4 + i
                        pj = fs % 4
                        mm = None
                        for ds in range(8):
                            mm = EM("tensor", TH(
                                "matmul", p_y[pj][:],
                                w1sl[gp][:, ds, i * 128:(i + 1) * 128],
                                x1T[:, ds, qs],
                                start=(ds == 0), stop=False),
                                [t_w1 if ds == 0 else None,
                                 h_psum_free[pj] if ds == 0 else None,
                                 bar_b2 if (e == 0 and q == 0 and fs == 0
                                            and ds == 0) else None], sT, 1)
                        mm = EM("tensor", TH(
                            "matmul", p_y[pj][:],
                            b1rb[0:1, fs * 128:(fs + 1) * 128],
                            cst["ones_row"][0:1, :],
                            start=False, stop=True), [b1_tok], sT, 1)
                        gj = fs % 3
                        gx = EM("vector", TH("tensor_copy", gbx[gj][:, :],
                                             p_y[pj][:]),
                                [mm, gel_free[gj]], sV, 1)
                        h_psum_free[pj] = gx
                        g2t = EM("scalar", TH("activation", out=gb2[gj][:, :],
                                              in_=gbx[gj][:, :],
                                              func=AF.Square), [gx], sA, 1)
                        g3t = EM("vector", TH("tensor_scalar", out=gb3[gj][:, :],
                                              in0=gb2[gj][:, :], scalar1=C3,
                                              scalar2=C1, op0=ALU.mult,
                                              op1=ALU.add), [g2t], sV, 1)
                        g4t = EM("gpsimd", TH("tensor_mul", gb2[gj][:, :],
                                              gb3[gj][:, :], gbx[gj][:, :]),
                                 [g3t], sG, 1)
                        g5t = EM("scalar", TH("activation", out=gb3[gj][:, :],
                                              in_=gb2[gj][:, :],
                                              func=AF.Sigmoid), [g4t], sA, 1)
                        ht = EM("gpsimd", TH("tensor_mul", hT[:, fs, :],
                                             gb3[gj][:, :], gbx[gj][:, :]),
                                [g5t, hT_rd[fs]], sG, 1)
                        gel_free[gj] = ht
                        hts.append(ht)
                    w1sl_free[gp] = hts[-1]
                h_all = hts[-1]
                for g in range(8):
                    gp = g % 2
                    t_w2 = dSL.emit("scalar", TH(
                        "dma_start", out=w2sl[gp][:],
                        in_=w2bf[e, g * 4:(g + 1) * 4].rearrange("s p d -> p s d")),
                        [w2sl_free[gp], conv2_done[e] if g <= 1 else None])
                    mmy = None
                    for i in range(4):
                        fs = g * 4 + i
                        for tt in range(4):
                            for dh in range(2):
                                pj = tt * 2 + dh
                                mmy = EM("tensor", TH(
                                    "matmul", p_y[pj][:],
                                    hT[:, fs, tt * 128:(tt + 1) * 128],
                                    w2sl[gp][:, i, dh * 512:(dh + 1) * 512],
                                    start=(fs == 0), stop=(fs == 31)),
                                    [t_w2 if (i == 0 and tt == 0 and dh == 0)
                                     else None,
                                     h_all if (fs == 0 and tt == 0 and dh == 0)
                                     else None,
                                     py_free[pj] if fs == 0 else None], sT, 1)
                        hT_rd[fs] = mmy
                    w2sl_free[gp] = mmy
                y_done = mmy
                for tt in range(4):
                    tile = q * 4 + tt
                    for dh in range(2):
                        pj = tt * 2 + dh
                        oa_tok[tile] = EM("vector", TH(
                            "scalar_tensor_tensor",
                            out=out_acc[:, tile, dh * 512:(dh + 1) * 512],
                            in0=p_y[pj][:], scalar=maskp[:, tile, e:e + 1],
                            in1=out_acc[:, tile, dh * 512:(dh + 1) * 512],
                            op0=ALU.mult, op1=ALU.add),
                            [y_done, oa_tok[tile]], sV, 1)
                        py_free[pj] = oa_tok[tile]
                for pj in range(4):
                    h_psum_free[pj] = py_free[pj]
            prev_h_all = h_all

        osb = xt_s
        osb_free = [None, None]
        do_tok = None
        srt_prev = None
        s2_prev = None
        oa_last = oa_tok[NT - 1]
        for t in range(NT):
            pp = t % 2
            oaf = out_acc[:, t, :]
            s1 = EM("vector", TH("reduce_sum", lst[:, 0:1], oaf, axis=AX.X),
                    [oa_tok[t], srt_prev,
                     ex_prev if t == 0 else None], sV, 1)
            sqy = EM("scalar", TH("activation", out=ysq[:, :], in_=oaf,
                                  func=AF.Square), [oa_tok[t], s2_prev], sA, 1)
            s2 = EM("vector", TH("reduce_sum", lst[:, 1:2], ysq[:, :], axis=AX.X),
                    [sqy], sV, 1)
            s2_prev = s2
            mn = EM("scalar", TH("mul", lst[:, 2:3], lst[:, 0:1], 1.0 / D),
                    [s1], sA, 1)
            msq = EM("scalar", TH("activation", out=lst[:, 3:4], in_=lst[:, 2:3],
                                  func=AF.Square), [mn], sA, 1)
            var = EM("vector", TH("tensor_scalar", out=lst[:, 4:5], in0=lst[:, 1:2],
                                  scalar1=1.0 / D, scalar2=lst[:, 3:4],
                                  op0=ALU.mult, op1=ALU.subtract),
                     [s2, msq], sV, 1)
            srt = EM("scalar", TH("activation", out=lst[:, 5:6], in_=lst[:, 4:5],
                                  func=AF.Sqrt, bias=cst["eps"][:, 1:2],
                                  scale=1.0), [var], sA, 1)
            srt_prev = srt
            rst = EM("vector", TH("reciprocal", lst[:, 6:7], lst[:, 5:6]),
                     [srt], sV, 1)
            spacer()
            ot = EM("vector", TH("tensor_scalar", out=osb[pp][:, :], in0=oaf,
                                 scalar1=lst[:, 2:3], scalar2=lst[:, 6:7],
                                 op0=ALU.subtract, op1=ALU.mult),
                    [rst, osb_free[pp], oa_last if t == 0 else None], sV, 1)
            do_tok = dO.emit("gpsimd", TH("dma_start",
                                          out=out_h[t * 128:(t + 1) * 128, :],
                                          in_=osb[pp][:, :]), [ot])
            osb_free[pp] = do_tok
        EM("gpsimd", TH("nop"), [dO.total_tok()])

        with nc.Block() as block:
            prog.run(block)
    return nc


SP2 = 512
SPC2 = SP2 // CH
NST2 = L // SP2
NOWN = TW // SP2


def build_fused():
    nc = bass.Bass("TRN2", target_bir_lowering=False, debug=False)
    x_h = nc.dram_tensor("x", [L, D], F32, kind="ExternalInput").ap()
    wq_h = nc.dram_tensor("wq", [D, D], F32, kind="ExternalInput").ap()
    wk_h = nc.dram_tensor("wk", [D, D], F32, kind="ExternalInput").ap()
    wv_h = nc.dram_tensor("wv", [D, D], F32, kind="ExternalInput").ap()
    wbg_h = nc.dram_tensor("wbg", [128, 8, 8], F32, kind="ExternalInput").ap()
    wmx_h = nc.dram_tensor("wmx", [128, 8, 4], F32, kind="ExternalInput").ap()
    cvq_h = nc.dram_tensor("cvq", [128, 8, 4], F32, kind="ExternalInput").ap()
    cvk_h = nc.dram_tensor("cvk", [128, 8, 4], F32, kind="ExternalInput").ap()
    cvv_h = nc.dram_tensor("cvv", [128, 8, 4], F32, kind="ExternalInput").ap()
    cvm_h = nc.dram_tensor("cvm", [1, 4, 4], F32, kind="ExternalInput").ap()
    mxb_h = nc.dram_tensor("mxb", [1, 4], F32, kind="ExternalInput").ap()
    wo_h = nc.dram_tensor("wo", [D, D], F32, kind="ExternalInput").ap()
    rms_h = nc.dram_tensor("rmsc", [128, 8], F32, kind="ExternalInput").ap()
    sw_h = nc.dram_tensor("sww", [D, NE], F32, kind="ExternalInput").ap()
    swb_h = nc.dram_tensor("swbi", [1, NE], F32, kind="ExternalInput").ap()
    w1_h = nc.dram_tensor("w1", [NE, D, FF], F32, kind="ExternalInput").ap()
    b1_h = nc.dram_tensor("b1r", [NE, 128, 32], F32, kind="ExternalInput").ap()
    w2_h = nc.dram_tensor("w2", [NE, FF, D], F32, kind="ExternalInput").ap()
    b2_h = nc.dram_tensor("b2", [NE, D], F32, kind="ExternalInput").ap()
    out_h = nc.dram_tensor("out", [TW, D], F32, kind="ExternalOutput").ap()
    w1bf = nc.dram_tensor("w1bf", [NE, 8, 128, FF], BF16, kind="Internal").ap()
    w2bf = nc.dram_tensor("w2bf", [NE, 32, 128, D], BF16, kind="Internal").ap()

    ctx = ExitStack()
    with ctx:
        prog = Prog(nc, ctx)
        sb = ctx.enter_context
        EM = prog.emit

        dX = DmaStream(prog, 4)
        dW = DmaStream(prog, 1)
        dI = DmaStream(prog, 4)
        dWF = DmaStream(prog, 2)
        dWB = DmaStream(prog, 12)
        dSL = DmaStream(prog, 8)
        dO = DmaStream(prog, 2)
        sT = prog.sem("sT")
        sV = prog.sem("sV")
        sA = prog.sem("sA")
        sG = prog.sem("sG")

        cst = make_consts(nc, ctx, prog, dW)
        mTRILS = cst["masks"][:, 0, :]
        mTRIUS = cst["masks"][:, 1, :]
        mTRIUI = cst["masks"][:, 2, :]
        mEYE = cst["masks"][:, 3, :]
        mTRIUS4 = cst["masks4"][:, 0, :]
        mTRILS4 = cst["masks4"][:, 1, :]
        mEYE4 = cst["masks4"][:, 2, :]
        mTRIUI4 = cst["masks4"][:, 3, :]
        IDB = cst["ident"][:]
        su8_d = nc.inline_tensor(np.triu(np.ones((8, 8), dtype=BF), 1), "su8d").ap()
        su8 = sb(nc.sbuf_tensor("su8", [8, 8], BF16))
        dW.emit("sync", TH("dma_start", out=su8[:], in_=su8_d))
        id8_d = nc.inline_tensor(np.eye(8, dtype=BF), "id8d").ap()
        id8 = sb(nc.sbuf_tensor("id8", [8, 8], BF16))
        c_tok = dW.emit("sync", TH("dma_start", out=id8[:], in_=id8_d))

        o_accT = sb(nc.sbuf_tensor("o_accT", [128, 8, TW], BF16))
        spcr = sb(nc.sbuf_tensor("spcr", [128, 2], F32))
        spc_tok = [None]

        conv1_done = [None] * NE
        conv2_done = [None] * NE

        def gen_wconv(e):
            toks = []
            for pc in range(8):
                toks.append(dWB.emit("gpsimd", TH(
                    "dma_start", out=w1bf[e, pc],
                    in_=w1_h[e, pc * 128:(pc + 1) * 128, :])))
                yield
            conv1_done[e] = toks
            toks = []
            for pc in range(32):
                toks.append(dWB.emit("gpsimd", TH(
                    "dma_start", out=w2bf[e, pc],
                    in_=w2_h[e, pc * 128:(pc + 1) * 128, :])))
                yield
            conv2_done[e] = toks

        conv_q = []

        def pump_conv(n):
            while n > 0 and conv_q:
                try:
                    next(conv_q[0])
                    n -= 1
                except StopIteration:
                    conv_q.pop(0)

        def spacer():
            spc_tok[0] = EM("vector", TH("memset", spcr[:, 0:1], 0.0),
                            [spc_tok[0]], sV, 1)

        ctxA = ExitStack()
        sa = ctxA.enter_context

        wq = sa(nc.sbuf_tensor("wqb", [128, 8, D], BF16))
        wk = sa(nc.sbuf_tensor("wkb", [128, 8, D], BF16))
        wv = sa(nc.sbuf_tensor("wvb", [128, 8, D], BF16))
        wbg = sa(nc.sbuf_tensor("wbgb", [128, 8, 8], BF16))
        wmx = sa(nc.sbuf_tensor("wmxb", [128, 8, 4], BF16))
        cvq = sa(nc.sbuf_tensor("cvqs", [128, 8, 4], F32))
        cvk = sa(nc.sbuf_tensor("cvks", [128, 8, 4], F32))
        cvv = sa(nc.sbuf_tensor("cvvs", [128, 8, 4], F32))
        cvm = sa(nc.sbuf_tensor("cvms", [1, 4, 4], F32))
        mxb = sa(nc.sbuf_tensor("mxbs", [1, 4], F32))
        ctxw = ExitStack()
        wstgA = ctxw.enter_context(nc.sbuf_tensor("wstgA", [128, 4096], F32))

        tok = cst["tok"]
        for dst, src in ((wq, wq_h), (wk, wk_h), (wv, wv_h)):
            for pc in range(2):
                tok = dW.emit("sync", TH(
                    "dma_start",
                    out=wstgA[:].rearrange("p (a c) -> p a c", a=4),
                    in_=src[pc * 512:(pc + 1) * 512, :]
                    .rearrange("(a p) c -> p a c", p=128)), [tok])
                tok = EM("vector", TH(
                    "tensor_copy",
                    dst[:, pc * 4:(pc + 1) * 4, :].rearrange("p a c -> p (a c)"),
                    wstgA[:]), [tok], sV, 1)
        tok = dW.emit("sync", TH(
            "dma_start", out=wstgA[:, 0:64].rearrange("p (a c) -> p a c", a=8),
            in_=wbg_h), [tok])
        tok = EM("vector", TH("tensor_copy",
                              wbg[:].rearrange("p a c -> p (a c)"),
                              wstgA[:, 0:64]), [tok], sV, 1)
        tok = dW.emit("sync", TH(
            "dma_start", out=wstgA[:, 0:32].rearrange("p (a c) -> p a c", a=8),
            in_=wmx_h), [tok])
        tok = EM("vector", TH("tensor_copy",
                              wmx[:].rearrange("p a c -> p (a c)"),
                              wstgA[:, 0:32]), [tok], sV, 1)
        w_tok = tok
        for dst, src in ((cvq, cvq_h), (cvk, cvk_h), (cvv, cvv_h)):
            dW.emit("sync", TH("dma_start", out=dst[:], in_=src))
        dW.emit("sync", TH("dma_start", out=cvm[:], in_=cvm_h))
        cv_tok = dW.emit("sync", TH("dma_start", out=mxb[:], in_=mxb_h))
        ctxw.close()

        xstg = [sa(nc.sbuf_tensor(f"xstg{i}", [128, D], F32)) for i in range(2)]
        xT = sa(nc.sbuf_tensor("xT0", [128, 8, SP2], BF16))
        xbf = sa(nc.sbuf_tensor("xbf", [128, D], BF16))
        qraw = sa(nc.sbuf_tensor("qraw", [128, 2, SP2 + 3], BF16))
        kraw = sa(nc.sbuf_tensor("kraw", [128, 2, SP2 + 3], BF16))
        vraw = sa(nc.sbuf_tensor("vraw", [128, 2, SP2 + 3], BF16))
        mixraw = sa(nc.sbuf_tensor("mixraw", [1, SP2 + 3], BF16))
        hisq = sa(nc.sbuf_tensor("hisq", [128, 8, 3], BF16))
        hisk = sa(nc.sbuf_tensor("hisk", [128, 8, 3], BF16))
        hisv = sa(nc.sbuf_tensor("hisv", [128, 8, 3], BF16))
        hism = sa(nc.sbuf_tensor("hism", [1, 4, 3], F32))
        vcv = [sa(nc.sbuf_tensor(f"vcv{i}", [128, 2, SP2], BF16)) for i in range(2)]
        sqb = sa(nc.sbuf_tensor("sqb", [128, 2, SP2], BF16))
        mcv = sa(nc.sbuf_tensor("mcv", [1, 2, SP2], BF16))
        cvt_a = sa(nc.sbuf_tensor("cvt_a", [128, 2, SP2], BF16))
        cvt_b = sa(nc.sbuf_tensor("cvt_b", [128, 2, SP2], BF16))
        q3 = sa(nc.sbuf_tensor("q3", [128, 2, SP2], BF16))
        k3 = sa(nc.sbuf_tensor("k3", [128, 2, SP2], BF16))
        v3 = sa(nc.sbuf_tensor("v3", [128, 2, SP2], BF16))
        gam_row = sa(nc.sbuf_tensor("gam_row", [1, SP2], F32))
        beta_bf = sa(nc.sbuf_tensor("beta_bf", [1, SP2], BF16))
        mxs = sa(nc.sbuf_tensor("mxs", [1, SP2], F32))
        l2r = sa(nc.sbuf_tensor("l2r", [1, 2, 512], BF16))
        rr = sa(nc.sbuf_tensor("rr", [1, 2, 512], F32))
        rr_bf = sa(nc.sbuf_tensor("rr_bf", [1, 512], BF16))
        g_bf = sa(nc.sbuf_tensor("g_bf", [1, SP2], BF16))
        g_lo = sa(nc.sbuf_tensor("g_lo", [1, SP2], BF16))
        Rrk = sa(nc.sbuf_tensor("Rrk", [128, SP2], BF16))
        Rb = sa(nc.sbuf_tensor("Rb", [128, SP2], BF16))
        Rg = sa(nc.sbuf_tensor("Rg", [128, SP2], F32))
        Rg1m = sa(nc.sbuf_tensor("Rg1m", [128, SP2], BF16))
        khT = sa(nc.sbuf_tensor("khT", [128, 2, SP2], BF16))
        kbT = sa(nc.sbuf_tensor("kbT", [128, 2, SP2], BF16))
        vhT = sa(nc.sbuf_tensor("vhT", [128, 2, SP2], BF16))
        vgb = sa(nc.sbuf_tensor("vgb", [128, 2, SP2], BF16))
        ema_o = sa(nc.sbuf_tensor("ema_o", [128, 2, SP2], F32))
        ema_bf = sa(nc.sbuf_tensor("ema_bf", [128, 2, SP2], BF16))
        ema_st = sa(nc.sbuf_tensor("ema_st", [128, 8], F32))
        tokb = sa(nc.sbuf_tensor("tokb", [128, SPC2, 4 * DK], BF16))
        sc_rq = sa(nc.sbuf_tensor("sc_rq", [128, SPC2], F32))
        sc_mx = sa(nc.sbuf_tensor("sc_mx", [128, SPC2], F32))
        S_sb = sa(nc.sbuf_tensor("S_sb", [128, 4, 2, DK], BF16))
        S_f32 = sa(nc.sbuf_tensor("S_f32", [128, 4, 2, DK], F32))
        nt_b = sa(nc.sbuf_tensor("nt_b", [128, SPC2, 128], BF16))
        nn_b = sa(nc.sbuf_tensor("nn_b", [128, SPC2, 128], BF16))
        xg = [sa(nc.sbuf_tensor(f"xgv{i}", [128, SPC2, 128], BF16)) for i in range(2)]
        xgt = [sa(nc.sbuf_tensor(f"xgt{i}", [128, SPC2, 128], BF16)) for i in range(2)]
        ccb = [sa(nc.sbuf_tensor(f"ccv{i}", [128, SPC2, 128], BF16)) for i in range(2)]
        pt_b = sa(nc.sbuf_tensor("pt_b", [128, SPC2, 128], BF16))
        u_b = sa(nc.sbuf_tensor("u_b", [128, SPC2, DK], BF16))
        wt_b = sa(nc.sbuf_tensor("wt_b", [128, SPC2, 2, 128], BF16))
        uadj = sa(nc.sbuf_tensor("uadj", [128, DK], BF16))
        o1 = sa(nc.sbuf_tensor("o1b", [128, DK], F32))
        tgg = sa(nc.sbuf_tensor("tgg", [128, DK], F32))
        ogg = sa(nc.sbuf_tensor("ogg", [128, DK], F32))
        onb = sa(nc.sbuf_tensor("onb", [128, DK], BF16))
        stat = sa(nc.sbuf_tensor("stat", [128, 4], F32))

        S_dl = sa(nc.psum_tensor("S_dl", [128, 512], F32))
        p_tr = sa(nc.psum_tensor("p_tr", [128, 1024], BF16))
        p_a = sa(nc.psum_tensor("p_a", [128, 512], F32))
        p_b = sa(nc.psum_tensor("p_b", [128, 512], F32))
        p_c = sa(nc.psum_tensor("p_c", [128, 512], F32))
        p_d = sa(nc.psum_tensor("p_d", [128, 512], F32))
        p_sm = sa(nc.psum_tensor("p_sm", [2, 1024], F32))

        z0 = EM("vector", TH("memset",
                             hisq[:].rearrange("p a c -> p (a c)"), 0.0),
                [w_tok], sV, 1)
        EM("vector", TH("memset", spcr[:, 0:2], 0.0), (), sV, 1)
        EM("vector", TH("memset", hisk[:].rearrange("p a c -> p (a c)"), 0.0),
           (), sV, 1)
        EM("vector", TH("memset", hisv[:].rearrange("p a c -> p (a c)"), 0.0),
           (), sV, 1)
        EM("vector", TH("memset", hism[0:1].rearrange("p a c -> p (a c)"), 0.0),
           (), sV, 1)
        EM("vector", TH("memset", ema_st[:, :], 0.0), (), sV, 1)
        EM("vector", TH("memset",
                        S_sb[:].rearrange("p a b c -> p (a b c)"), 0.0),
           (), sV, 1)
        EM("vector", TH("memset",
                        S_f32[:].rearrange("p a b c -> p (a b c)"), 0.0),
           (), sV, 1)

        bar = prog.barrier()
        xstg_free = [None, None]
        prev_sv1 = [None]
        prev_hs = {"q": None, "k": None, "v": None}
        sadd_tok = [None]
        ot1_prev = [None]
        prev_l2 = [None]
        prev_pd = [None]

        for st in range(NST2):
            off = st * SP2
            own = st >= NST2 - NOWN
            woff = off - (L - TW)
            if st < NE:
                conv_q.append(gen_wconv(st))

            xt_evt = None
            for t in range(SP2 // 128):
                pp = t % 2
                tin = dX.emit("sync", TH(
                    "dma_start", out=xstg[pp][:],
                    in_=x_h[off + t * 128: off + (t + 1) * 128, :]),
                    [xstg_free[pp],
                     w_tok if (st == 0 and t < 2) else None])
                tc = EM("vector", TH("tensor_copy", xbf[:], xstg[pp][:]),
                        [tin, xt_evt, bar if (st == 0 and t == 0) else None],
                        sV, 1)
                xstg_free[pp] = tc
                trs = None
                for ds in range(8):
                    trs = EM("tensor", TH("transpose",
                                          p_tr[:, ds * 128:(ds + 1) * 128],
                                          xbf[:, ds * 128:(ds + 1) * 128], IDB),
                             [tc, cst["tok"] if st == 0 and t == 0 else None],
                             sT, 1)
                xt_evt = EM("scalar", TH("copy",
                                          xT[:, :, t * 128:(t + 1) * 128],
                                          p_tr[:].rearrange("p (a c) -> p a c", a=8)),
                             [trs], sA, 1)
            xt_done = xt_evt

            for h in range(4):
                hsl = slice(2 * h, 2 * h + 2)
                pump_conv(3)
                hq = hk = hv = hm = None
                if own:
                    hq = EM("vector", TH("tensor_copy", qraw[:, :, 0:3],
                                         hisq[:, hsl, :]), [bar], sV, 1)
                hk = EM("vector", TH("tensor_copy", kraw[:, :, 0:3],
                                     hisk[:, hsl, :]), [bar], sV, 1)
                hv = EM("vector", TH("tensor_copy", vraw[:, :, 0:3],
                                     hisv[:, hsl, :]), (), sV, 1)
                if own:
                    hm = EM("vector", TH("tensor_copy", mixraw[0:1, 0:3],
                                         hism[0:1, h, :]), (), sV, 1)

                def proj(dstraw, wgt, hist_tok, war_tok=None):
                    ev = None
                    for hf in range(2):
                        pp = p_a if hf == 0 else p_b
                        mmt = None
                        for ds in range(8):
                            mmt = EM("tensor", TH(
                                "matmul", pp[:],
                                wgt[:, ds, h * 256 + hf * 128:
                                    h * 256 + (hf + 1) * 128],
                                xT[:, ds, :],
                                start=(ds == 0), stop=(ds == 7)),
                                [xt_done if ds == 0 else None,
                                 w_tok if ds == 0 else None,
                                 ev if ds == 0 else None], sT, 1)
                        ev = EM("scalar", TH(
                            "copy", dstraw[:, hf, 3:3 + SP2],
                            pp[:]), [mmt, hist_tok, war_tok if hf == 0 else None],
                            sA, 1)
                    return ev
                pk = proj(kraw, wk, hk, prev_hs["k"])
                pv = proj(vraw, wv, hv, prev_hs["v"])
                pq = proj(qraw, wq, hq, prev_hs["q"]) if own else None

                bgmm = None
                for ds in range(8):
                    bgmm = EM("tensor", TH(
                        "matmul", p_sm[0:1, 0:512],
                        wbg[:, ds, 2 * h:2 * h + 1], xT[:, ds, :],
                        start=(ds == 0), stop=(ds == 7)),
                        [xt_done, prev_l2[0] if ds == 0 else None], sT, 1)
                for ds in range(8):
                    bgmm = EM("tensor", TH(
                        "matmul", p_sm[0:1, 512:1024],
                        wbg[:, ds, 2 * h + 1:2 * h + 2], xT[:, ds, :],
                        start=(ds == 0), stop=(ds == 7)), (), sT, 1)
                bg_ev = EM("scalar", TH("activation", out=beta_bf[0:1, :],
                                        in_=p_sm[0:1, 0:512], func=AF.Sigmoid),
                           [bgmm], sA, 1)
                g_ev = EM("scalar", TH("activation", out=gam_row[0:1, :],
                                       in_=p_sm[0:1, 512:1024], func=AF.Sigmoid),
                          (), sA, 1)
                if own:
                    mxmm = None
                    for ds in range(8):
                        mxmm = EM("tensor", TH(
                            "matmul", p_sm[0:1, 0:512],
                            wmx[:, ds, h:h + 1], xT[:, ds, :],
                            start=(ds == 0), stop=(ds == 7)), [bg_ev], sT, 1)
                    mx_ev = EM("scalar", TH("copy", mixraw[0:1, 3:3 + SP2],
                                            p_sm[0:1, 0:512]), [mxmm, hm], sA, 1)

                def conv(raw, cw, ev_tok, sa_=None, sb_=None):
                    ca = sa_ if sa_ is not None else cvt_a
                    cb = sb_ if sb_ is not None else cvt_b
                    t_last = None
                    for hf in range(2):
                        t0 = EM("vector", TH(
                            "tensor_scalar", out=ca[:, hf, :],
                            in0=raw[:, hf, 0:SP2],
                            scalar1=cw[:, 2 * h + hf, 0:1], scalar2=None,
                            op0=ALU.mult), [ev_tok, cv_tok], sV, 1)
                        t1 = EM("vector", TH(
                            "scalar_tensor_tensor", out=cb[:, hf, :],
                            in0=raw[:, hf, 1:1 + SP2],
                            scalar=cw[:, 2 * h + hf, 1:2],
                            in1=ca[:, hf, :], op0=ALU.mult, op1=ALU.add),
                            [t0], sV, 1)
                        t2 = EM("vector", TH(
                            "scalar_tensor_tensor", out=ca[:, hf, :],
                            in0=raw[:, hf, 2:2 + SP2],
                            scalar=cw[:, 2 * h + hf, 2:3],
                            in1=cb[:, hf, :], op0=ALU.mult, op1=ALU.add),
                            [t1], sV, 1)
                        t_last = EM("vector", TH(
                            "scalar_tensor_tensor", out=cb[:, hf, :],
                            in0=raw[:, hf, 3:3 + SP2],
                            scalar=cw[:, 2 * h + hf, 3:4],
                            in1=ca[:, hf, :], op0=ALU.mult, op1=ALU.add),
                            [t2], sV, 1)
                    return t_last

                hsk = EM("gpsimd", TH("tensor_copy", hisk[:, hsl, :],
                                      kraw[:, :, SP2:SP2 + 3]), [pk], sG, 1)
                prev_hs["k"] = hsk
                hsv = EM("gpsimd", TH("tensor_copy", hisv[:, hsl, :],
                                      vraw[:, :, SP2:SP2 + 3]), [pv], sG, 1)
                prev_hs["v"] = hsv
                if own:
                    hsq2 = EM("gpsimd", TH("tensor_copy", hisq[:, hsl, :],
                                           qraw[:, :, SP2:SP2 + 3]), [pq], sG, 1)
                    prev_hs["q"] = hsq2

                cvk_t = conv(kraw, cvk, pk)
                sk1 = EM("scalar", TH("activation",
                                      out=cvt_a[:].rearrange("p a c -> p (a c)"),
                                      in_=cvt_b[:].rearrange("p a c -> p (a c)"),
                                      func=AF.Silu), [cvk_t], sA, 1)
                sk2 = EM("scalar", TH("activation",
                                      out=k3[:].rearrange("p a c -> p (a c)"),
                                      in_=cvt_a[:].rearrange("p a c -> p (a c)"),
                                      func=AF.Silu), [sk1], sA, 1)
                cvv_t = conv(vraw, cvv, [pv, prev_sv1[0]],
                             sa_=vcv[0], sb_=vcv[1])
                sv1 = EM("scalar", TH("activation",
                                      out=v3[:].rearrange("p a c -> p (a c)"),
                                      in_=vcv[1][:].rearrange("p a c -> p (a c)"),
                                      func=AF.Silu), [cvv_t], sA, 1)
                prev_sv1[0] = sv1
                if own:
                    cvq_t = conv(qraw, cvq, sv1)
                    sq1 = EM("scalar", TH("activation",
                                          out=cvt_a[:].rearrange("p a c -> p (a c)"),
                                          in_=cvt_b[:].rearrange("p a c -> p (a c)"),
                                          func=AF.Silu), [cvq_t], sA, 1)
                    sq2t = EM("scalar", TH("activation",
                                           out=q3[:].rearrange("p a c -> p (a c)"),
                                           in_=cvt_a[:].rearrange("p a c -> p (a c)"),
                                           func=AF.Silu), [sq1], sA, 1)
                    m0 = EM("vector", TH("tensor_scalar", out=mcv[0:1, 0, :],
                                         in0=mixraw[0:1, 0:SP2],
                                         scalar1=cvm[0:1, h, 0:1],
                                         scalar2=None, op0=ALU.mult),
                            [mx_ev], sV, 1)
                    m1 = EM("vector", TH("scalar_tensor_tensor",
                                         out=mcv[0:1, 1, :],
                                         in0=mixraw[0:1, 1:1 + SP2],
                                         scalar=cvm[0:1, h, 1:2],
                                         in1=mcv[0:1, 0, :],
                                         op0=ALU.mult, op1=ALU.add), [m0], sV, 1)
                    m2 = EM("vector", TH("scalar_tensor_tensor",
                                         out=mcv[0:1, 0, :],
                                         in0=mixraw[0:1, 2:2 + SP2],
                                         scalar=cvm[0:1, h, 2:3],
                                         in1=mcv[0:1, 1, :],
                                         op0=ALU.mult, op1=ALU.add), [m1], sV, 1)
                    m3 = EM("vector", TH("scalar_tensor_tensor",
                                         out=mcv[0:1, 1, :],
                                         in0=mixraw[0:1, 3:3 + SP2],
                                         scalar=cvm[0:1, h, 3:4],
                                         in1=mcv[0:1, 0, :],
                                         op0=ALU.mult, op1=ALU.add), [m2], sV, 1)
                    hsm = EM("gpsimd", TH("tensor_copy", hism[0:1, h, :],
                                          mixraw[0:1, SP2:SP2 + 3]), [m3], sG, 1)
                    m4 = EM("scalar", TH("activation", out=mcv[0:1, 0, :],
                                         in_=mcv[0:1, 1, :], func=AF.Silu),
                            [m3], sA, 1)
                    mx_t = EM("scalar", TH("activation", out=mxs[0:1, :],
                                           in_=mcv[0:1, 0, :], func=AF.Sigmoid,
                                           bias=mxb[0:1, h:h + 1], scale=1.0),
                             [m4], sA, 1)

                pump_conv(3)
                sq = sqb
                sqk = EM("scalar", TH("activation",
                                      out=sq[:].rearrange("p a c -> p (a c)"),
                                      in_=k3[:].rearrange("p a c -> p (a c)"),
                                      func=AF.Square),
                         [sk2,
                          bar if (st > 0 or h > 0) else None], sA, 1)
                EM("tensor", TH("matmul", p_sm[0:1, 0:512],
                                cst["ones_col"][:], sq[:, 0, :],
                                start=True, stop=False),
                   [sqk, mx_ev if own else g_ev], sT, 1)
                ssk = EM("tensor", TH("matmul", p_sm[0:1, 0:512],
                                      cst["ones_col"][:], sq[:, 1, :],
                                      start=False, stop=True), (), sT, 1)
                if own:
                    sqq = EM("scalar", TH("activation",
                                          out=sq[:].rearrange("p a c -> p (a c)"),
                                          in_=q3[:].rearrange("p a c -> p (a c)"),
                                          func=AF.Square), [sq2t, ssk], sA, 1)
                    EM("tensor", TH("matmul", p_sm[0:1, 512:1024],
                                    cst["ones_col"][:], sq[:, 0, :],
                                    start=True, stop=False), [sqq, g_ev], sT, 1)
                    ssq1 = EM("tensor", TH("matmul", p_sm[0:1, 512:1024],
                                           cst["ones_col"][:], sq[:, 1, :],
                                           start=False, stop=True), (), sT, 1)
                srk = EM("scalar", TH("activation", out=l2r[0:1, 0, :],
                                      in_=p_sm[0:1, 0:512], func=AF.Sqrt,
                                      bias=cst["eps"][0:1, 0:1], scale=1.0),
                         [ssk], sA, 1)
                prev_l2[0] = srk
                if own:
                    srq = EM("scalar", TH("activation", out=l2r[0:1, 1, :],
                                          in_=p_sm[0:1, 512:1024], func=AF.Sqrt,
                                          bias=cst["eps"][0:1, 0:1], scale=1.0),
                             [ssq1], sA, 1)
                    prev_l2[0] = srq
                rcp = EM("vector", TH("reciprocal",
                                      rr[0:1].rearrange("p a c -> p (a c)")
                                      if own else rr[0:1, 0, :],
                                      l2r[0:1].rearrange("p a c -> p (a c)")
                                      if own else l2r[0:1, 0, :]),
                         [srk, srq if own else None], sV, 1)
                spacer()
                rkb = EM("vector", TH("tensor_copy", rr_bf[0:1, :],
                                      rr[0:1, 0, :]), [rcp], sV, 1)
                gb = EM("vector", TH("tensor_copy", g_bf[0:1, :],
                                     gam_row[0:1, :]), [g_ev], sV, 1)
                spacer()
                gl = EM("vector", TH("tensor_tensor", g_lo[0:1, :],
                                     gam_row[0:1, :], g_bf[0:1, :],
                                     ALU.subtract), [gb], sV, 1)
                bgb = EM("vector", TH("tensor_copy", spcr[:, 1:2],
                                      spcr[:, 1:2]), [gl, bg_ev], sV, 1)

                if st == NST2 - NOWN - 1:
                    qt = None
                    for hf in range(2):
                        for ds in range(8):
                            qt = EM("tensor", TH(
                                "matmul", p_d[0:128, 16 + hf * 3:19 + hf * 3],
                                wq[:, ds, h * 256 + hf * 128:
                                   h * 256 + (hf + 1) * 128],
                                xT[:, ds, SP2 - 3:SP2],
                                start=(ds == 0), stop=(ds == 7)),
                                [prev_pd[0] if (hf == 0 and ds == 0)
                                 else None], sT, 1)
                    tq_fill = EM("scalar", TH("copy", hisq[:, hsl, :].rearrange(
                        "p a c -> p (a c)"), p_d[0:128, 16:22]), [qt], sA, 1)
                    mt2 = None
                    for ds in range(8):
                        mt2 = EM("tensor", TH(
                            "matmul", p_d[0:1, 22:25],
                            wmx[:, ds, h:h + 1], xT[:, ds, SP2 - 3:SP2],
                            start=(ds == 0), stop=(ds == 7)),
                            [tq_fill if ds == 0 else None], sT, 1)
                    hq_fill = EM("scalar", TH("copy", hism[0:1, h, :],
                                              p_d[0:1, 22:25]), [mt2], sA, 1)
                    prev_pd[0] = hq_fill

                t1 = EM("tensor", TH("matmul", p_c[:], cst["ones_row"][0:1, 0:128],
                                     rr_bf[0:1, :], start=True, stop=True),
                        [rkb], sT, 1)
                ev_rrk = EM("scalar", TH("copy", Rrk[:, :], p_c[:]), [t1], sA, 1)
                t1 = EM("tensor", TH("matmul", p_d[:], cst["ones_row"][0:1, 0:128],
                                     beta_bf[0:1, :], start=True, stop=True),
                        [bgb, prev_pd[0]], sT, 1)
                ev_rb = EM("scalar", TH("copy", Rb[:, :], p_d[:]), [t1], sA, 1)
                prev_pd[0] = ev_rb
                EM("tensor", TH("matmul", p_c[:], cst["ones_row"][0:1, 0:128],
                                g_bf[0:1, :], start=True, stop=False),
                   [bgb, ev_rrk], sT, 1)
                t1 = EM("tensor", TH("matmul", p_c[:], cst["ones_row"][0:1, 0:128],
                                     g_lo[0:1, :], start=False, stop=True),
                        (), sT, 1)
                ev_rg = EM("scalar", TH("copy", Rg[:, :], p_c[:]), [t1], sA, 1)
                rg1 = EM("vector", TH("tensor_scalar", out=Rg1m[:, :],
                                      in0=Rg[:, :], scalar1=-1.0, scalar2=1.0,
                                      op0=ALU.mult, op1=ALU.add), [ev_rg], sV, 1)

                nk = nb = nv = ng = None
                for hf in range(2):
                    nk = EM("vector", TH("tensor_tensor", khT[:, hf, :],
                                         k3[:, hf, :], Rrk[:, :], ALU.mult),
                            [ev_rrk, sk2], sV, 1)
                    nb = EM("vector", TH("tensor_tensor", kbT[:, hf, :],
                                         khT[:, hf, :], Rb[:, :], ALU.mult),
                            [ev_rb, nk], sV, 1)
                    nv = EM("gpsimd", TH("tensor_tensor", vhT[:, hf, :],
                                         v3[:, hf, :], Rb[:, :], ALU.mult),
                            [sv1 if hf == 0 else None,
                             ev_rb if hf == 0 else None], sG, 1)
                    ng = EM("gpsimd", TH("tensor_tensor", vgb[:, hf, :],
                                         v3[:, hf, :], Rg1m[:, :], ALU.mult),
                            [rg1 if hf == 0 else None], sG, 1)

                em0 = EM("vector", TH("tensor_tensor_scan", out=ema_o[:, 0, :],
                                      data0=Rg[:, :], data1=vgb[:, 0, :],
                                      initial=ema_st[:, 2 * h:2 * h + 1],
                                      op0=ALU.mult, op1=ALU.add),
                         [ng, ev_rg], sV, 1)
                em1 = EM("vector", TH("tensor_tensor_scan", out=ema_o[:, 1, :],
                                      data0=Rg[:, :], data1=vgb[:, 1, :],
                                      initial=ema_st[:, 2 * h + 1:2 * h + 2],
                                      op0=ALU.mult, op1=ALU.add), (), sV, 1)
                es0 = EM("vector", TH("tensor_copy", ema_st[:, 2 * h:2 * h + 1],
                                      ema_o[:, 0, SP2 - 1:SP2]), [em1], sV, 1)
                es1 = EM("vector", TH("tensor_copy",
                                      ema_st[:, 2 * h + 1:2 * h + 2],
                                      ema_o[:, 1, SP2 - 1:SP2]), (), sV, 1)
                if own:
                    emb = EM("vector", TH("tensor_copy",
                                          ema_bf[:].rearrange("p a c -> p (a c)"),
                                          ema_o[:].rearrange("p a c -> p (a c)")),
                             [es1], sV, 1)

                if own:
                    sc_mm = None
                    for c in range(SPC2):
                        sc_mm = EM("tensor", TH(
                            "transpose", p_c[:, c:c + 1],
                            rr[0:1, 1, c * 128:(c + 1) * 128],
                            mEYE[0:1, 0:1]),
                            [rcp if c == 0 else None,
                             ev_rg if c == 0 else None], sT, 1)
                    for c in range(SPC2):
                        sc_mm = EM("tensor", TH(
                            "transpose", p_c[:, 8 + c:9 + c],
                            mxs[0:1, c * 128:(c + 1) * 128],
                            mEYE[0:1, 0:1]),
                            [mx_t if c == 0 else None], sT, 1)
                    scv1 = EM("vector", TH("tensor_copy", sc_rq[:, :],
                                           p_c[:, 0:SPC2]), [sc_mm], sV, 1)
                    scv2 = EM("vector", TH("tensor_copy", sc_mx[:, :],
                                           p_c[:, 8:8 + SPC2]), (), sV, 1)
                else:
                    scv1 = scv2 = es1

                tok_evs = []
                nsrc = 4 if own else 3
                for c in range(SPC2):
                    cw = slice(c * 128, (c + 1) * 128)
                    trs = None
                    srcs = (khT, kbT, vhT, ema_bf)
                    deps = (nk, nb, nv, emb if own else None)
                    for j in range(nsrc):
                        for hf in range(2):
                            trs = EM("tensor", TH(
                                "transpose",
                                p_tr[:, j * 256 + hf * 128:
                                     j * 256 + (hf + 1) * 128],
                                srcs[j][:, hf, cw], IDB),
                                [deps[j] if hf == 0 else None,
                                 tok_evs[c - 1] if (j == 0 and hf == 0 and c > 0)
                                 else None,
                                 scv2 if (j == 0 and hf == 0 and c == 0)
                                 else None], sT, 1)
                    ev = EM("scalar", TH("copy",
                                         tokb[:, c, 0:nsrc * 256],
                                         p_tr[:, 0:nsrc * 256]), [trs], sA, 1)
                    tok_evs.append(ev)

                pump_conv(3)
                nt_toks = [None] * SPC2
                nn_toks = [None] * SPC2
                mm_last = None
                for c in range(SPC2):
                    cw = slice(c * 128, (c + 1) * 128)
                    EM("tensor", TH("matmul", p_a[:, c * 128:(c + 1) * 128],
                                    khT[:, 0, cw], kbT[:, 0, cw],
                                    start=True, stop=False),
                       [nb, ssq1 if (own and c == 0) else
                        (ssk if c == 0 else None)], sT, 1)
                    EM("tensor", TH("matmul", p_a[:, c * 128:(c + 1) * 128],
                                    khT[:, 1, cw], kbT[:, 1, cw],
                                    start=False, stop=True), (), sT, 1)
                    EM("tensor", TH("matmul", p_b[:, c * 128:(c + 1) * 128],
                                    kbT[:, 0, cw], khT[:, 0, cw],
                                    start=True, stop=False), (), sT, 1)
                    mm_last = EM("tensor", TH("matmul",
                                              p_b[:, c * 128:(c + 1) * 128],
                                              kbT[:, 1, cw], khT[:, 1, cw],
                                              start=False, stop=True), (), sT, 1)
                ntt = EM("vector", TH(
                    "tensor_tensor", nt_b[:].rearrange("p a c -> p (a c)"),
                    p_a[:, 0:512], mTRIUS4, ALU.mult), [mm_last], sV, 1)
                nnt = EM("vector", TH(
                    "tensor_tensor", nn_b[:].rearrange("p a c -> p (a c)"),
                    p_b[:, 0:512], mTRILS4, ALU.mult), [mm_last], sV, 1)
                nt_toks = [ntt] * SPC2
                nn_toks = [nnt] * SPC2
                cct = EM("vector", TH(
                    "tensor_tensor", ccb[0][:].rearrange("p a c -> p (a c)"),
                    mEYE4, nt_b[:].rearrange("p a c -> p (a c)"),
                    ALU.subtract), [ntt], sV, 1)
                cc_tok = [cct] * SPC2

                x_cur, xt_cur = nt_b, nn_b
                x_tok, xt_tok = nt_toks, nn_toks
                cgen = 0
                prev_ncc_last = scv2
                for lvl in range(6):
                    xng = xg[lvl % 2]
                    xtg = xgt[lvl % 2]
                    nx_tok = [None] * SPC2
                    nxt_tok = [None] * SPC2
                    ncc_tok = [None] * SPC2
                    mm_last = None
                    for c in range(SPC2):
                        guard = (nn_toks[SPC2 - 1] if lvl == 0
                                 else xt_tok[SPC2 - 1]) if c == 0 else None
                        if lvl < 5:
                            EM("tensor", TH(
                                "matmul", p_a[:, c * 128:(c + 1) * 128],
                                xt_cur[:, c, :], x_cur[:, c, :],
                                start=True, stop=True),
                                [x_tok[c], xt_tok[c], guard], sT, 1)
                        mm_last = EM("tensor", TH(
                            "matmul", p_b[:, c * 128:(c + 1) * 128],
                            x_cur[:, c, :], xt_cur[:, c, :],
                            start=True, stop=True),
                            [x_tok[c], xt_tok[c], guard], sT, 1)
                    if lvl < 5:
                        nxa = EM("scalar", TH(
                            "copy", xng[:].rearrange("p a c -> p (a c)"),
                            p_a[:, 0:512]), [mm_last], sA, 1)
                        nx_tok = [nxa] * SPC2
                    nxta = EM("scalar", TH(
                        "copy", xtg[:].rearrange("p a c -> p (a c)"),
                        p_b[:, 0:512]), [mm_last], sA, 1)
                    nxt_tok = [nxta] * SPC2
                    mm_cu_last = None
                    for c in range(SPC2):
                        mm_cu_last = EM("tensor", TH(
                            "matmul", p_c[:, c * 128:(c + 1) * 128],
                            xtg[:, c, :], ccb[cgen][:, c, :],
                            start=True, stop=True),
                            [nxt_tok[c], cc_tok[c],
                             prev_ncc_last if c == 0 else None], sT, 1)
                    ncca = EM("vector", TH(
                        "tensor_tensor",
                        ccb[1 - cgen][:].rearrange("p a c -> p (a c)"),
                        p_c[:, 0:512],
                        ccb[cgen][:].rearrange("p a c -> p (a c)"),
                        ALU.add), [mm_cu_last], sV, 1)
                    ncc_tok = [ncca] * SPC2
                    prev_ncc_last = ncca
                    x_cur, xt_cur = xng, xtg
                    x_tok, xt_tok = nx_tok, nxt_tok
                    cc_tok = ncc_tok
                    cgen = 1 - cgen
                CCF = ccb[cgen]
                cc_all = cc_tok[SPC2 - 1]

                u_toks = [None] * SPC2
                wt_toks = [None] * SPC2
                pt_toks = [None] * SPC2
                for pc in range(SPC2 // 2):
                    mm_u = mm_w = mm_p = None
                    for sl in range(2):
                        c = 2 * pc + sl
                        cw = slice(c * 128, (c + 1) * 128)
                        mm_u = EM("tensor", TH(
                            "matmul", p_a[:, sl * 256:(sl + 1) * 256],
                            CCF[:, c, :], tokb[:, c, 512:768],
                            start=True, stop=True),
                            [cc_all if sl == 0 else None,
                             tok_evs[c],
                             u_toks[2 * pc - 1] if (pc >= 1 and sl == 0)
                             else None], sT, 1)
                        for hf in range(2):
                            mm_w = EM("tensor", TH(
                                "matmul",
                                p_b[:, sl * 256 + hf * 128:
                                    sl * 256 + (hf + 1) * 128],
                                tokb[:, c, 256 + hf * 128:256 + (hf + 1) * 128],
                                CCF[:, c, :], start=True, stop=True),
                                [wt_toks[2 * pc - 1]
                                 if (pc >= 1 and sl == 0 and hf == 0)
                                 else None], sT, 1)
                        if own:
                            EM("tensor", TH(
                                "matmul", p_c[:, sl * 128:(sl + 1) * 128],
                                khT[:, 0, cw], q3[:, 0, cw],
                                start=True, stop=False),
                                [sq2t if sl == 0 else None,
                                 pt_toks[2 * pc - 1] if (pc >= 1 and sl == 0)
                                 else None,
                                 prev_ncc_last if (pc == 0 and sl == 0)
                                 else None], sT, 1)
                            mm_p = EM("tensor", TH(
                                "matmul", p_c[:, sl * 128:(sl + 1) * 128],
                                khT[:, 1, cw], q3[:, 1, cw],
                                start=False, stop=True), (), sT, 1)
                    ut = EM("scalar", TH(
                        "copy", u_b[:, 2 * pc:2 * pc + 2, :]
                        .rearrange("p a c -> p (a c)"),
                        p_a[:, 0:512]), [mm_u], sA, 1)
                    u_toks[2 * pc] = u_toks[2 * pc + 1] = ut
                    wtt = EM("scalar", TH(
                        "copy",
                        wt_b[:, 2 * pc:2 * pc + 2, :, :]
                        .rearrange("p a b c -> p (a b c)"),
                        p_b[:, 0:512]), [mm_w], sA, 1)
                    wt_toks[2 * pc] = wt_toks[2 * pc + 1] = wtt
                    if own:
                        ptt = EM("vector", TH(
                            "tensor_tensor",
                            pt_b[:, 2 * pc:2 * pc + 2, :]
                            .rearrange("p a c -> p (a c)"),
                            p_c[:, 0:256], mTRIUI4[:, 0:256], ALU.mult),
                            [mm_p], sV, 1)
                        pt_toks[2 * pc] = pt_toks[2 * pc + 1] = ptt

                pump_conv(3)
                ssb_tok = None
                for c in range(SPC2):
                    cw = slice(c * 128, (c + 1) * 128)
                    EM("tensor", TH("matmul", p_c[:, 0:256], wt_b[:, c, 0, :],
                                    S_sb[:, h, 0, :], start=True, stop=False),
                       [wt_toks[c], ssb_tok,
                        (pt_toks[SPC2 - 1] if own else prev_ncc_last)
                        if c == 0 else None], sT, 1)
                    ws1 = EM("tensor", TH("matmul", p_c[:, 0:256],
                                          wt_b[:, c, 1, :], S_sb[:, h, 1, :],
                                          start=False, stop=True), (), sT, 1)
                    ua = EM("vector", TH("tensor_tensor", uadj[:, :],
                                         u_b[:, c, :], p_c[:, 0:256],
                                         ALU.subtract), [ws1, u_toks[c]], sV, 1)
                    if own:
                        EM("tensor", TH("matmul", p_d[:, 0:256], q3[:, 0, cw],
                                        S_sb[:, h, 0, :],
                                        start=True, stop=False),
                           [ua, ev_rg if c == 0 else None, ot1_prev[0]], sT, 1)
                        EM("tensor", TH("matmul", p_d[:, 0:256], q3[:, 1, cw],
                                        S_sb[:, h, 1, :],
                                        start=False, stop=False), (), sT, 1)
                        om = EM("tensor", TH("matmul", p_d[:, 0:256],
                                             pt_b[:, c, :], uadj[:, :],
                                             start=False, stop=True),
                                [pt_toks[c]], sT, 1)
                    EM("tensor", TH("matmul", S_dl[:, 0:256],
                                    tokb[:, c, 0:128], uadj[:, :],
                                    start=True, stop=False),
                       [sadd_tok[0], ua], sT, 1)
                    su = EM("tensor", TH("matmul", S_dl[:, 256:512],
                                         tokb[:, c, 128:256], uadj[:, :],
                                         start=False, stop=True), (), sT, 1)
                    sadd_tok[0] = EM("vector", TH(
                        "tensor_tensor",
                        S_f32[:, h].rearrange("p a c -> p (a c)"),
                        S_f32[:, h].rearrange("p a c -> p (a c)"),
                        S_dl[:, :], ALU.add), [su], sV, 1)
                    ssb_tok = EM("scalar", TH(
                        "copy", S_sb[:, h].rearrange("p a c -> p (a c)"),
                        S_f32[:, h].rearrange("p a c -> p (a c)")),
                        [sadd_tok[0]], sA, 1)
                    if not own:
                        continue
                    ot1 = EM("vector", TH("tensor_scalar", out=o1[:, :],
                                          in0=p_d[:, 0:256],
                                          scalar1=sc_rq[:, c:c + 1],
                                          scalar2=None, op0=ALU.mult),
                             [om, scv1], sV, 1)
                    ot1_prev[0] = ot1
                    ot2 = EM("vector", TH("tensor_tensor", tgg[:, :],
                                          tokb[:, c, 768:1024], o1[:, :],
                                          ALU.subtract), [ot1], sV, 1)
                    ot3 = EM("vector", TH("scalar_tensor_tensor", out=ogg[:, :],
                                          in0=tgg[:, :],
                                          scalar=sc_mx[:, c:c + 1],
                                          in1=o1[:, :], op0=ALU.mult,
                                          op1=ALU.add), [ot2, scv2], sV, 1)
                    sq2e = EM("scalar", TH("activation", out=tgg[:, :],
                                           in_=ogg[:, :], func=AF.Square),
                              [ot3], sA, 1)
                    rs = EM("vector", TH("reduce_sum", stat[:, 0:1], tgg[:, :],
                                         axis=AX.X), [sq2e], sV, 1)
                    stt = EM("scalar", TH("activation", out=stat[:, 1:2],
                                          in_=stat[:, 0:1], func=AF.Sqrt,
                                          bias=cst["eps"][:, 1:2],
                                          scale=1.0 / DK), [rs], sA, 1)
                    rc = EM("vector", TH("reciprocal", stat[:, 2:3],
                                         stat[:, 1:2]), [stt], sV, 1)
                    spacer()
                    onb_t = EM("vector", TH("tensor_scalar", out=onb[:, :],
                                            in0=ogg[:, :], scalar1=stat[:, 2:3],
                                            scalar2=None, op0=ALU.mult),
                               [rc], sV, 1)
                    EM("tensor", TH("transpose", p_tr[:, 0:128],
                                    onb[:, 0:128], IDB), [onb_t], sT, 1)
                    tr1 = EM("tensor", TH("transpose", p_tr[:, 128:256],
                                          onb[:, 128:256], IDB), (), sT, 1)
                    EM("vector", TH(
                        "tensor_copy",
                        o_accT[:, 2 * h:2 * h + 2, woff + cw.start:
                               woff + cw.stop],
                        p_tr[:, 0:256].rearrange("p (a c) -> p a c", a=2)),
                        [tr1], sV, 1)

            pump_conv(1 << 30) if st == NST2 - 1 else None
            bar = prog.barrier()

        ph_bar = prog.barrier(("tensor", "vector", "scalar", "gpsimd"))
        ph_gate = [ph_bar] + dX.total_tok() + dW.total_tok()
        ctxA.close()

        sbB = ctx.enter_context
        x1T = sbB(nc.sbuf_tensor("x1T", [128, 8, TW], BF16))
        out_acc = sbB(nc.sbuf_tensor("out_acc", [128, NT, D], BF16))
        maskp = sbB(nc.sbuf_tensor("maskp", [128, NT, NE], F32))
        lgs_all = sbB(nc.sbuf_tensor("lgs_all", [128, NT, NE], F32))
        b1t = sbB(nc.sbuf_tensor("b1t", [128, 32], F32))
        w1sl = [sbB(nc.sbuf_tensor(f"w1sl{i}", [128, 8, 256], BF16)) for i in range(2)]
        rmsc = sbB(nc.sbuf_tensor("rmscs", [128, 8], F32))
        swb = sbB(nc.sbuf_tensor("swwb", [128, 8, NE], BF16))
        swbb = sbB(nc.sbuf_tensor("swbbs", [1, NE], BF16))
        b2b = sbB(nc.sbuf_tensor("b2bs", [8, D], BF16))
        xt_s = [sbB(nc.sbuf_tensor(f"xts{i}", [128, D], F32)) for i in range(2)]
        ysq = sbB(nc.sbuf_tensor("ysq", [128, D], F32))
        lst = sbB(nc.sbuf_tensor("lst", [128, 8], F32))
        expo = sbB(nc.sbuf_tensor("expo", [128, NE], F32))
        eqb = sbB(nc.sbuf_tensor("eqb", [128, NE], BF16))
        eqT = sbB(nc.sbuf_tensor("eqT", [8, 128], BF16))
        zrow = sbB(nc.sbuf_tensor("zrow", [8, 128], BF16))
        mTt = sbB(nc.sbuf_tensor("mTt", [8, 128], BF16))
        msk = sbB(nc.sbuf_tensor("msk", [128, NE], BF16))
        ctxW2 = ExitStack()
        wob = ctxW2.enter_context(nc.sbuf_tensor("wob", [128, 8, D], BF16))
        ybuf2 = ctxW2.enter_context(nc.sbuf_tensor("ybuf2", [128, 2, 512], F32))

        p_y = [sbB(nc.psum_tensor(f"p_y{i}", [128, 512], F32)) for i in range(7)]
        ctx2 = ExitStack()
        p_trb = ctx2.enter_context(nc.psum_tensor("p_trb", [128, 1024], BF16))

        wstg = [ctxW2.enter_context(nc.sbuf_tensor(f"wstgB{i}", [128, 1024], F32)) for i in range(2)]
        t_rm = dI.emit("sync", TH("dma_start", out=rmsc[:], in_=rms_h), [ph_gate])
        wo_tok = None
        wo_toks = []
        for pc in range(8):
            stg = wstg[pc % 2]
            t_wo = dI.emit("sync", TH(
                "dma_start", out=stg[:],
                in_=wo_h[pc * 128:(pc + 1) * 128, :]),
                [ph_gate, wo_toks[pc - 2] if pc >= 2 else None])
            wo_tok = EM("vector", TH(
                "tensor_scalar", out=wob[:, pc, :], in0=stg[:],
                scalar1=rmsc[:, pc:pc + 1], scalar2=None,
                op0=ALU.mult), [t_wo, t_rm, ph_bar], sV, 1)
            wo_toks.append(wo_tok)
        t_sw = dI.emit("sync", TH(
            "dma_start", out=ysq[:, 0:64].rearrange("p (a c) -> p a c", a=8),
            in_=sw_h.rearrange("(a p) c -> p a c", p=128)), [ph_gate])
        t_sw = EM("vector", TH("tensor_copy",
                               swb[:].rearrange("p a c -> p (a c)"),
                               ysq[:, 0:64]), [t_sw, ph_bar], sV, 1)
        t_swb = dI.emit("sync", TH("dma_start", out=lst[0:1, 0:8], in_=swb_h),
                        [ph_gate])
        t_swb = EM("vector", TH("tensor_copy", swbb[:], lst[0:1, 0:8]),
                   [t_swb], sV, 1)
        t_b2 = dI.emit("sync", TH("dma_start", out=xt_s[0][0:8, :], in_=b2_h),
                       [ph_gate])
        t_b2 = EM("vector", TH("tensor_copy", b2b[:, :], xt_s[0][0:8, :]),
                  [t_b2, t_sw], sV, 1)


        xt_free = [None, t_b2]
        p_free = {}
        lgs_toks = [None] * NT
        x1T_tok = [None] * NT
        oa_init = [None] * NT
        srt_prev = None
        s2_prev = None
        for t in range(NT):
            pp = t % 2
            tx = dI.emit("sync", TH("dma_start", out=xt_s[pp][:],
                                    in_=x_h[L - TW + t * 128:
                                            L - TW + (t + 1) * 128, :]),
                         [xt_free[pp] if t > 1 else
                          (t_b2 if pp == 0 else ph_gate)])
            yv = {}
            for dh in range(2):
                mm = None
                for ds in range(8):
                    mm = EM("tensor", TH(
                        "matmul", p_y[dh][:],
                        o_accT[:, ds, t * 128:(t + 1) * 128],
                        wob[:, ds, dh * 512:(dh + 1) * 512],
                        start=(ds == 0), stop=(ds == 7)),
                        [wo_toks[-1] if ds == 0 else None,
                         p_free.get(dh) if ds == 0 else None,
                         ph_bar if (t == 0 and ds == 0) else None], sT, 1)
                yv[dh] = EM("vector", TH(
                    "tensor_tensor", ybuf2[:, dh, :], p_y[dh][:],
                    xt_s[pp][:, dh * 512:(dh + 1) * 512], ALU.add),
                    [mm, tx, srt_prev if dh == 0 else None], sV, 1)
                p_free[dh] = yv[dh]
            yflat = ybuf2[:].rearrange("p a c -> p (a c)")
            s1 = EM("vector", TH("reduce_sum", lst[:, 0:1], yflat, axis=AX.X),
                    [yv[1], srt_prev], sV, 1)
            sqy = EM("scalar", TH("activation", out=ysq[:, :], in_=yflat,
                                  func=AF.Square), [yv[1], t_b2, s2_prev], sA, 1)
            s2 = EM("vector", TH("reduce_sum", lst[:, 1:2], ysq[:, :], axis=AX.X),
                    [sqy], sV, 1)
            s2_prev = s2
            mn = EM("scalar", TH("mul", lst[:, 2:3], lst[:, 0:1], 1.0 / D),
                    [s1], sA, 1)
            msq = EM("scalar", TH("activation", out=lst[:, 3:4], in_=lst[:, 2:3],
                                  func=AF.Square), [mn], sA, 1)
            var = EM("vector", TH("tensor_scalar", out=lst[:, 4:5], in0=lst[:, 1:2],
                                  scalar1=1.0 / D, scalar2=lst[:, 3:4],
                                  op0=ALU.mult, op1=ALU.subtract),
                     [s2, msq], sV, 1)
            srt = EM("scalar", TH("activation", out=lst[:, 5:6], in_=lst[:, 4:5],
                                  func=AF.Sqrt, bias=cst["eps"][:, 1:2],
                                  scale=1.0), [var], sA, 1)
            srt_prev = srt
            rst = EM("vector", TH("reciprocal", lst[:, 6:7], lst[:, 5:6]),
                     [srt], sV, 1)
            spacer()
            x1t = EM("vector", TH("tensor_scalar", out=out_acc[:, t, :], in0=yflat,
                                  scalar1=lst[:, 2:3], scalar2=lst[:, 6:7],
                                  op0=ALU.subtract, op1=ALU.mult),
                     [rst], sV, 1)
            oa_init[t] = x1t
            xt_free[pp] = x1t
            trx = None
            for ds in range(8):
                trx = EM("tensor", TH("transpose",
                                      p_trb[:, ds * 128:(ds + 1) * 128],
                                      out_acc[:, t, ds * 128:(ds + 1) * 128], IDB),
                         [x1t if ds == 0 else None,
                          x1T_tok[t - 1] if (ds == 0 and t > 0) else None], sT, 1)
            x1T_tok[t] = EM("vector", TH("tensor_copy",
                                         x1T[:, :, t * 128:(t + 1) * 128],
                                         p_trb[:].rearrange("p (a c) -> p a c", a=8)),
                            [trx], sV, 1)
            lmm = None
            for ds in range(8):
                lmm = EM("tensor", TH("matmul", p_y[2][:, 0:NE],
                                      x1T[:, ds, t * 128:(t + 1) * 128],
                                      swb[:, ds, :], start=(ds == 0), stop=False),
                         [x1T_tok[t] if ds == 0 else None,
                          t_sw if ds == 0 else None,
                          lgs_toks[t - 1] if (ds == 0 and t > 0) else None], sT, 1)
            lmm = EM("tensor", TH("matmul", p_y[2][:, 0:NE],
                                  cst["ones_row"][0:1, 0:128], swbb[0:1, :],
                                  start=False, stop=True), [t_swb], sT, 1)
            lgs_toks[t] = EM("vector", TH("tensor_copy", lgs_all[:, t, :],
                                          p_y[2][:, 0:NE]), [lmm], sV, 1)

        route_tok = [None] * NT
        mtt_tok = None
        ex_prev = srt_prev
        for t in range(NT):
            lm = EM("vector", TH("reduce_max", lst[:, 0:1], lgs_all[:, t, :],
                                 axis=AX.X),
                    [lgs_toks[NT - 1] if t == 0 else None, ex_prev], sV, 1)
            nlm = EM("scalar", TH("mul", lst[:, 1:2], lst[:, 0:1], -1.0),
                     [lm], sA, 1)
            ex = EM("scalar", TH("activation", out=expo[:, :], in_=lgs_all[:, t, :],
                                 func=AF.Exp, bias=lst[:, 1:2], scale=1.0),
                    [nlm], sA, 1)
            ex_prev = ex
            sm = EM("vector", TH("reduce_sum", lst[:, 2:3], expo[:, :], axis=AX.X),
                    [ex], sV, 1)
            eq = EM("vector", TH("tensor_single_scalar", eqb[:, :],
                                 lgs_all[:, t, :], lst[:, 0:1], ALU.is_ge),
                    [sm], sV, 1)
            rpm = EM("vector", TH("reciprocal", lst[:, 3:4], lst[:, 2:3]),
                     [eq], sV, 1)
            tr_eq = EM("tensor", TH("transpose", p_trb[0:8, 0:128], eqb[:, :], IDB),
                       [eq, mtt_tok], sT, 1)
            eqt = EM("vector", TH("tensor_copy", eqT[:, :], p_trb[0:8, 0:128]),
                     [tr_eq], sV, 1)
            cmm = EM("tensor", TH("matmul", p_y[3][0:8, 0:128], su8[:, :],
                                  eqT[:, :], start=True, stop=True),
                     [eqt, route_tok[t - 1] if t > 0 else None,
                      c_tok if t == 0 else None], sT, 1)
            zr = EM("vector", TH("tensor_single_scalar", zrow[:, :],
                                 p_y[3][0:8, 0:128], 0.0, ALU.is_equal),
                    [cmm], sV, 1)
            mt = EM("vector", TH("tensor_tensor", mTt[:, :], eqT[:, :],
                                 zrow[:, :], ALU.mult), [zr], sV, 1)
            tr_m = EM("tensor", TH("transpose", p_trb[0:128, 128:136], mTt[:, :],
                                   id8[:, :]), [mt], sT, 1)
            mk = EM("vector", TH("tensor_copy", msk[:, :], p_trb[0:128, 128:136]),
                    [tr_m], sV, 1)
            mpq = EM("vector", TH("tensor_scalar", out=maskp[:, t, :],
                                  in0=msk[:, :], scalar1=lst[:, 3:4], scalar2=None,
                                  op0=ALU.mult), [mk, rpm], sV, 1)
            ini = None
            for dh in range(2):
                bmm = EM("tensor", TH("matmul", p_y[4 + dh][:], mTt[:, :],
                                      b2b[:, dh * 512:(dh + 1) * 512],
                                      start=True, stop=True),
                         [mt, t_b2, route_tok[t - 1] if t >= 1 else None], sT, 1)
                ini = EM("vector", TH("scalar_tensor_tensor",
                                      out=out_acc[:, t, dh * 512:(dh + 1) * 512],
                                      in0=p_y[4 + dh][:], scalar=lst[:, 3:4],
                                      in1=out_acc[:, t, dh * 512:(dh + 1) * 512],
                                      op0=ALU.mult, op1=ALU.add),
                         [bmm, mpq if dh == 0 else None, oa_init[t]], sV, 1)
            route_tok[t] = ini
            mtt_tok = ini

        ctx2.close()
        p_y.append(sbB(nc.psum_tensor("p_y7", [128, 512], F32)))
        bar_b2 = prog.barrier()
        ctxW2.close()

        w2res = sbB(nc.sbuf_tensor("w2res", [128, 32, D], BF16))

        def hTs(fs, c0, c1):
            return o_accT[:, fs // 4, (fs % 4) * 512 + c0:(fs % 4) * 512 + c1]

        w1sl_free = [None, None]
        w2res_free = None
        w2res_done = []
        py_free = [None] * 8
        hT_rd = [None] * 32
        oa_tok = {t: route_tok[t] for t in range(NT)}
        b1_tok = None
        prev_h_all = None
        h_psum_free = [None] * 4
        for e in range(NE):
            b1_tok = dI.emit("sync", TH("dma_start", out=b1t[:], in_=b1_h[e]),
                             [b1_tok, prev_h_all,
                              bar_b2 if e == 0 else None])
            for q in range(4):
                qs = slice(q * 512, (q + 1) * 512)
                hts = []
                for g in range(16):
                    gp = g % 2
                    t_w1 = dSL.emit("sync", TH(
                        "dma_start", out=w1sl[gp][:],
                        in_=w1bf[e, :, :, g * 256:(g + 1) * 256]
                        .rearrange("a p c -> p a c")),
                        [w1sl_free[gp], conv1_done[e] if g <= 1 else None])
                    if q == 0:
                        if g == 0:
                            w2res_done = []
                        w2res_done.append(dSL.emit("sync", TH(
                            "dma_start", out=w2res[:, 2 * g:2 * g + 2, :],
                            in_=w2bf[e, 2 * g:(g + 1) * 2]
                            .rearrange("s p d -> p s d")),
                            [w2res_free if g == 0 else None,
                             conv2_done[e] if g == 0 else None]))
                    for i in range(2):
                        fs = g * 2 + i
                        pj = fs % 4
                        mm = None
                        for ds in range(8):
                            mm = EM("tensor", TH(
                                "matmul", p_y[pj][:],
                                w1sl[gp][:, ds, i * 128:(i + 1) * 128],
                                x1T[:, ds, qs],
                                start=(ds == 0), stop=(ds == 7)),
                                [t_w1 if ds == 0 else None,
                                 h_psum_free[pj] if ds == 0 else None,
                                 bar_b2 if (e == 0 and q == 0 and fs == 0
                                            and ds == 0) else None], sT, 1)
                        ht = EM("scalar", TH("activation",
                                             out=hTs(fs, 0, 512),
                                             in_=p_y[pj][:],
                                             func=AF.Gelu_apprx_tanh,
                                             bias=b1t[:, fs:fs + 1],
                                             scale=1.0),
                                [mm, b1_tok, hT_rd[fs]], sA, 1)
                        h_psum_free[pj] = ht
                        hts.append(ht)
                    w1sl_free[gp] = hts[-1]
                h_all = hts[-1]
                mmy = None
                for fs in range(32):
                    for tt in range(4):
                        for dh in range(2):
                            pj = tt * 2 + dh
                            mmy = EM("tensor", TH(
                                "matmul", p_y[pj][:],
                                hTs(fs, tt * 128, (tt + 1) * 128),
                                w2res[:, fs, dh * 512:(dh + 1) * 512],
                                start=(fs == 0), stop=(fs == 31)),
                                [w2res_done if (fs == 0 and tt == 0 and dh == 0)
                                 else None,
                                 h_all if (fs == 0 and tt == 0 and dh == 0)
                                 else None,
                                 py_free[pj] if fs == 0 else None], sT, 1)
                    hT_rd[fs] = mmy
                y_done = mmy
                for tt in range(4):
                    tile = q * 4 + tt
                    for dh in range(2):
                        pj = tt * 2 + dh
                        oa_tok[tile] = EM("vector", TH(
                            "scalar_tensor_tensor",
                            out=out_acc[:, tile, dh * 512:(dh + 1) * 512],
                            in0=p_y[pj][:], scalar=maskp[:, tile, e:e + 1],
                            in1=out_acc[:, tile, dh * 512:(dh + 1) * 512],
                            op0=ALU.mult, op1=ALU.add),
                            [y_done, oa_tok[tile]], sV, 1)
                        py_free[pj] = oa_tok[tile]
                for pj in range(4):
                    h_psum_free[pj] = py_free[pj]
            prev_h_all = h_all
            w2res_free = y_done

        osb = xt_s
        osb_free = [None, None]
        do_tok = None
        srt_prev = None
        s2_prev = None
        oa_last = oa_tok[NT - 1]
        for t in range(NT):
            pp = t % 2
            oaf = out_acc[:, t, :]
            s1 = EM("vector", TH("reduce_sum", lst[:, 0:1], oaf, axis=AX.X),
                    [oa_tok[t], srt_prev,
                     ex_prev if t == 0 else None], sV, 1)
            sqy = EM("scalar", TH("activation", out=ysq[:, :], in_=oaf,
                                  func=AF.Square), [oa_tok[t], s2_prev], sA, 1)
            s2 = EM("vector", TH("reduce_sum", lst[:, 1:2], ysq[:, :], axis=AX.X),
                    [sqy], sV, 1)
            s2_prev = s2
            mn = EM("scalar", TH("mul", lst[:, 2:3], lst[:, 0:1], 1.0 / D),
                    [s1], sA, 1)
            msq = EM("scalar", TH("activation", out=lst[:, 3:4], in_=lst[:, 2:3],
                                  func=AF.Square), [mn], sA, 1)
            var = EM("vector", TH("tensor_scalar", out=lst[:, 4:5], in0=lst[:, 1:2],
                                  scalar1=1.0 / D, scalar2=lst[:, 3:4],
                                  op0=ALU.mult, op1=ALU.subtract),
                     [s2, msq], sV, 1)
            srt = EM("scalar", TH("activation", out=lst[:, 5:6], in_=lst[:, 4:5],
                                  func=AF.Sqrt, bias=cst["eps"][:, 1:2],
                                  scale=1.0), [var], sA, 1)
            srt_prev = srt
            rst = EM("vector", TH("reciprocal", lst[:, 6:7], lst[:, 5:6]),
                     [srt], sV, 1)
            spacer()
            ot = EM("vector", TH("tensor_scalar", out=osb[pp][:, :], in0=oaf,
                                 scalar1=lst[:, 2:3], scalar2=lst[:, 6:7],
                                 op0=ALU.subtract, op1=ALU.mult),
                    [rst, osb_free[pp],
                     oa_last if t == 0 else None,
                     xt_free[pp] if t < 2 else None], sV, 1)
            do_tok = dO.emit("gpsimd", TH("dma_start",
                                          out=out_h[t * 128:(t + 1) * 128, :],
                                          in_=osb[pp][:, :]), [ot])
            osb_free[pp] = do_tok
        EM("gpsimd", TH("nop"), [dO.total_tok()])

        with nc.Block() as block:
            prog.run(block)
    return nc


_CACHE = {}


def kernel(**inp):
    x = np.ascontiguousarray(inp["L0_attn_hidden_states"], np.float32)
    Wq = np.ascontiguousarray(inp["L0_attn_Wq"], np.float32)
    Wk = np.ascontiguousarray(inp["L0_attn_Wk"], np.float32)
    Wv = np.ascontiguousarray(inp["L0_attn_Wv"], np.float32)
    conv_q = np.ascontiguousarray(inp["L0_attn_conv_q"], np.float32)
    conv_k = np.ascontiguousarray(inp["L0_attn_conv_k"], np.float32)
    conv_v = np.ascontiguousarray(inp["L0_attn_conv_v"], np.float32)
    Wb = np.ascontiguousarray(inp["L0_attn_Wb"], np.float32)
    Wdec = np.ascontiguousarray(inp["L0_attn_Wdec"], np.float32)
    Wmix = np.ascontiguousarray(inp["L0_attn_Wmix"], np.float32)
    conv_mix = np.ascontiguousarray(inp["L0_attn_conv_mix"], np.float32)
    mix_bias = np.ascontiguousarray(inp["L0_attn_mix_bias"], np.float32)
    rms_w = np.ascontiguousarray(inp["L0_attn_rms_w"], np.float32)
    Wo = np.ascontiguousarray(inp["L0_attn_Wo"], np.float32)
    sw_w = np.ascontiguousarray(inp["L0_ffn_switch_w"], np.float32)
    sw_b = np.ascontiguousarray(inp["L0_ffn_switch_b"], np.float32)
    w1 = np.ascontiguousarray(inp["L0_ffn_w1"], np.float32)
    b1 = np.ascontiguousarray(inp["L0_ffn_b1"], np.float32)
    w2 = np.ascontiguousarray(inp["L0_ffn_w2"], np.float32)
    b2 = np.ascontiguousarray(inp["L0_ffn_b2"], np.float32)

    if "fused" not in _CACHE:
        _CACHE["fused"] = build_fused()
    ncf = _CACHE["fused"]

    def slab3(a):
        return np.ascontiguousarray(
            a.reshape(8, 128, a.shape[1]).transpose(1, 0, 2))

    wbg_host = slab3(np.stack(
        [col for h in range(H) for col in (Wb[:, h], Wdec[:, h])], axis=1))
    rmsc = np.ascontiguousarray(np.tile(rms_w, H).reshape(8, 128).T)
    in_maps = []
    for c in range(8):
        b, w = divmod(c, 4)
        xin = np.zeros((L, D), np.float32)
        pre = w * TW
        if pre:
            xin[L - TW - pre:L - TW] = x[b, 0:pre]
        xin[L - TW:] = x[b, pre:pre + TW]
        in_maps.append(dict(
            x=xin, wq=Wq, wk=Wk, wv=Wv,
            wbg=wbg_host,
            wmx=slab3(Wmix),
            cvq=slab3(conv_q), cvk=slab3(conv_k), cvv=slab3(conv_v),
            cvm=np.ascontiguousarray(conv_mix.reshape(1, 4, 4)),
            mxb=np.ascontiguousarray(mix_bias.reshape(1, 4)),
            wo=Wo, rmsc=rmsc, sww=sw_w,
            swbi=np.ascontiguousarray(sw_b.reshape(1, NE)),
            w1=w1, b1r=np.ascontiguousarray(
                b1.reshape(NE, 32, 128).transpose(0, 2, 1)),
            w2=w2, b2=b2,
        ))
    res = run_bass_kernel_spmd(ncf, in_maps, core_ids=list(range(8)))
    out = np.zeros((2, L, D), np.float32)
    for c in range(8):
        b, w = divmod(c, 4)
        out[b, w * TW:(w + 1) * TW] = res.results[c]["out"]
    return out


# version 20 (final)
# speedup vs baseline: 1.0056x; 1.0056x over previous
import numpy as np
import ml_dtypes
import concourse.bass as bass
import concourse.mybir as mybir
from concourse.bass_utils import run_bass_kernel_spmd
from contextlib import ExitStack

F32 = mybir.dt.float32
BF16 = mybir.dt.bfloat16
AF = mybir.ActivationFunctionType
ALU = mybir.AluOpType
AX = mybir.AxisListType
BF = ml_dtypes.bfloat16

ENG = ("sync", "tensor", "vector", "scalar", "gpsimd")

L = 8192
D = 1024
H = 4
DK = 256
CH = 128
STRIP = 1024
NSTRIP = L // STRIP
SPC = STRIP // CH
NE = 8
FF = 4096
TW = 2048
NT = TW // 128
C1 = float(2.0 * np.sqrt(2.0 / np.pi))
C3 = float(0.044715 * 2.0 * np.sqrt(2.0 / np.pi))


def TH(name, *a, **k):
    return lambda h: getattr(h, name)(*a, **k)


class Sem:
    __slots__ = ("h", "n")


class Prog:

    def __init__(self, nc, ctx):
        self.nc = nc
        self.ctx = ctx
        self.streams = {e: [] for e in ENG}
        self.sat = {e: {} for e in ENG}
        self._n = 0

    def sem(self, name=None):
        s = Sem()
        s.h = self.ctx.enter_context(self.nc.semaphore(name or f"s{self._n}"))
        self._n += 1
        s.n = 0
        return s

    def emit(self, eng, fn, waits=(), inc=None, k=1):
        wl = []
        sat = self.sat[eng]
        def add(w):
            if w is None:
                return
            if isinstance(w, list):
                for ww in w:
                    add(ww)
                return
            s, c = w
            if c > sat.get(s, 0):
                wl.append((s, c))
                sat[s] = c
        for w in waits:
            add(w)
        incs = []
        tok = None
        if inc is not None:
            inc.n += k
            incs.append((inc, k))
            tok = (inc, inc.n)
        self.streams[eng].append([wl, fn, incs, tok])
        return tok

    def barrier(self, engines=("tensor", "vector", "scalar")):
        toks = []
        for e in engines:
            if not self.streams[e]:
                continue
            rec = self.streams[e][-1]
            if rec[3] is not None:
                toks.append(rec[3])
            else:
                b = self.sem()
                rec[2].append((b, 1))
                b.n += 1
                rec[3] = (b, b.n)
                toks.append((b, b.n))
        return toks

    def run(self, block):
        def mk(recs):
            def go(h):
                for wl, fn, incs, _t in recs:
                    for s, c in wl:
                        h.wait_ge(s.h, c)
                    ins = fn(h)
                    for s, k in incs:
                        ins = ins.then_inc(s.h, k)
            return go
        block.sync(mk(self.streams["sync"]))
        block.tensor(mk(self.streams["tensor"]))
        block.vector(mk(self.streams["vector"]))
        block.scalar(mk(self.streams["scalar"]))
        block.gpsimd(mk(self.streams["gpsimd"]))


class DmaStream:

    def __init__(self, prog, k=1):
        self.prog = prog
        self.sems = [prog.sem() for _ in range(k)]
        self.k = k
        self.i = 0

    def emit(self, eng, fn, waits=()):
        sem = self.sems[self.i % self.k]
        self.i += 1
        w = list(waits) + [(sem, sem.n)]
        return self.prog.emit(eng, fn, w, sem, 16)

    def total_tok(self):
        return [(sem, sem.n) for sem in self.sems]


def make_consts(nc, ctx, prog, dsem, masks=True):
    c = {}
    sb = ctx.enter_context
    ident_d = nc.inline_tensor(np.eye(128, dtype=BF), "identd").ap()
    ones_row_d = nc.inline_tensor(np.ones((1, 512), dtype=BF), "onesrowd").ap()
    onescol_d = nc.inline_tensor(np.ones((128, 1), dtype=BF), "onescold").ap()
    eps_d = nc.inline_tensor(
        np.tile(np.array([[1e-6, 1e-5, 1.0, 0.0]], np.float32), (128, 1)), "epsd").ap()
    c["ident"] = sb(nc.sbuf_tensor("c_ident", [128, 128], BF16))
    c["ones_row"] = sb(nc.sbuf_tensor("c_onesrow", [1, 512], BF16))
    c["ones_col"] = sb(nc.sbuf_tensor("c_onescol", [128, 1], BF16))
    c["eps"] = sb(nc.sbuf_tensor("c_eps", [128, 4], F32))
    dsem.emit("sync", TH("dma_start", out=c["ident"][:], in_=ident_d))
    dsem.emit("sync", TH("dma_start", out=c["ones_row"][:], in_=ones_row_d))
    dsem.emit("sync", TH("dma_start", out=c["ones_col"][:], in_=onescol_d))
    tok = dsem.emit("sync", TH("dma_start", out=c["eps"][:], in_=eps_d))
    if masks:
        tril_s = np.tril(np.ones((128, 128), np.float32), -1)
        mdat = np.stack([tril_s, tril_s.T,
                         np.triu(np.ones((128, 128), np.float32)),
                         np.eye(128, dtype=np.float32)], axis=1)
        m_d = nc.inline_tensor(mdat, "masksd").ap()
        c["masks"] = sb(nc.sbuf_tensor("c_masks", [128, 4, 128], F32))
        tok = dsem.emit("sync", TH("dma_start", out=c["masks"][:], in_=m_d))
        m4dat = np.stack([np.tile(tril_s.T.astype(BF), (1, 4)),
                          np.tile(tril_s.astype(BF), (1, 4)),
                          np.tile(np.eye(128, dtype=BF), (1, 4)),
                          np.tile(np.triu(np.ones((128, 128), BF)), (1, 4))],
                         axis=1)
        m4_d = nc.inline_tensor(m4dat, "masks4d").ap()
        c["masks4"] = sb(nc.sbuf_tensor("c_masks4", [128, 4, 512], BF16))
        tok = dsem.emit("sync", TH("dma_start", out=c["masks4"][:], in_=m4_d))
    c["tok"] = tok
    return c


def build_attn():
    nc = bass.Bass("TRN2", target_bir_lowering=False, debug=False)
    x_h = nc.dram_tensor("x", [L, D], F32, kind="ExternalInput").ap()
    wq_h = nc.dram_tensor("wq", [D, DK], F32, kind="ExternalInput").ap()
    wk_h = nc.dram_tensor("wk", [D, DK], F32, kind="ExternalInput").ap()
    wv_h = nc.dram_tensor("wv", [D, DK], F32, kind="ExternalInput").ap()
    wbg_h = nc.dram_tensor("wbg", [128, 8, 2], F32, kind="ExternalInput").ap()
    wmx_h = nc.dram_tensor("wmx", [128, 8, 1], F32, kind="ExternalInput").ap()
    cvq_h = nc.dram_tensor("cvq", [2, 128, 4], F32, kind="ExternalInput").ap()
    cvk_h = nc.dram_tensor("cvk", [2, 128, 4], F32, kind="ExternalInput").ap()
    cvv_h = nc.dram_tensor("cvv", [2, 128, 4], F32, kind="ExternalInput").ap()
    cvm_h = nc.dram_tensor("cvm", [1, 4], F32, kind="ExternalInput").ap()
    mxb_h = nc.dram_tensor("mxb", [1, 1], F32, kind="ExternalInput").ap()
    onT_h = nc.dram_tensor("onT", [2, 128, L], BF16, kind="ExternalOutput").ap()

    ctx = ExitStack()
    with ctx:
        prog = Prog(nc, ctx)
        sb = ctx.enter_context
        EM = prog.emit

        dX = DmaStream(prog, 4)
        dW = DmaStream(prog, 1)
        dO = DmaStream(prog, 2)
        sT = prog.sem("sT")
        sV = prog.sem("sV")
        sA = prog.sem("sA")

        cst = make_consts(nc, ctx, prog, dW)
        mTRILS = cst["masks"][:, 0, :]
        mTRIUS = cst["masks"][:, 1, :]
        mTRIUI = cst["masks"][:, 2, :]
        mEYE = cst["masks"][:, 3, :]
        IDB = cst["ident"][:]

        wq = sb(nc.sbuf_tensor("wqb", [128, 8, DK], BF16))
        wk = sb(nc.sbuf_tensor("wkb", [128, 8, DK], BF16))
        wv = sb(nc.sbuf_tensor("wvb", [128, 8, DK], BF16))
        wbg = sb(nc.sbuf_tensor("wbgb", [128, 8, 2], BF16))
        wmx = sb(nc.sbuf_tensor("wmxb", [128, 8, 1], BF16))
        cvq = sb(nc.sbuf_tensor("cvqs", [128, 2, 4], F32))
        cvk = sb(nc.sbuf_tensor("cvks", [128, 2, 4], F32))
        cvv = sb(nc.sbuf_tensor("cvvs", [128, 2, 4], F32))
        cvm = sb(nc.sbuf_tensor("cvms", [1, 4], F32))
        mxb = sb(nc.sbuf_tensor("mxbs", [1, 1], F32))
        ctxw = ExitStack()
        wstg = ctxw.enter_context(nc.sbuf_tensor("wstg", [128, 2048], F32))

        tok = cst["tok"]
        for dst, src, cols in ((wq, wq_h, DK), (wk, wk_h, DK), (wv, wv_h, DK),
                               (wbg, wbg_h, 2), (wmx, wmx_h, 1)):
            src_ap = (src.rearrange("(a p) c -> p a c", p=128)
                      if cols == DK else src)
            tok = dW.emit("sync", TH("dma_start",
                                      out=wstg[:, 0:8 * cols].rearrange(
                                          "p (a c) -> p a c", a=8),
                                      in_=src_ap),
                          [tok])
            tok = EM("vector", TH("tensor_copy",
                                  dst[:].rearrange("p a c -> p (a c)"),
                                  wstg[:, 0:8 * cols]), [tok], sV, 1)
        w_tok = tok
        for dst, src in ((cvq, cvq_h), (cvk, cvk_h), (cvv, cvv_h)):
            dW.emit("sync", TH("dma_start", out=dst[:],
                               in_=src.rearrange("a p c -> p a c")))
        dW.emit("sync", TH("dma_start", out=cvm[:], in_=cvm_h))
        cv_tok = dW.emit("sync", TH("dma_start", out=mxb[:], in_=mxb_h))
        ctxw.close()

        xstg = [sb(nc.sbuf_tensor(f"xstg{i}", [128, D], F32)) for i in range(2)]
        xT = [sb(nc.sbuf_tensor("xT0", [128, 8, STRIP], BF16))] * 2
        xbf = sb(nc.sbuf_tensor("xbf", [128, D], BF16))
        qraw = sb(nc.sbuf_tensor("qraw", [128, 2, STRIP + 3], BF16))
        kraw = sb(nc.sbuf_tensor("kraw", [128, 2, STRIP + 3], BF16))
        vraw = sb(nc.sbuf_tensor("vraw", [128, 2, STRIP + 3], BF16))
        mixraw = sb(nc.sbuf_tensor("mixraw", [1, STRIP + 3], F32))
        cvt_a = sb(nc.sbuf_tensor("cvt_a", [128, 2, STRIP], BF16))
        cvt_b = sb(nc.sbuf_tensor("cvt_b", [128, 2, STRIP], BF16))
        q3 = sb(nc.sbuf_tensor("q3", [128, 2, STRIP], BF16))
        k3 = sb(nc.sbuf_tensor("k3", [128, 2, STRIP], BF16))
        v3 = sb(nc.sbuf_tensor("v3", [128, 2, STRIP], BF16))
        beta_row = sb(nc.sbuf_tensor("beta_row", [1, STRIP], F32))
        gam_row = sb(nc.sbuf_tensor("gam_row", [1, STRIP], F32))
        beta_bf = sb(nc.sbuf_tensor("beta_bf", [1, STRIP], BF16))
        mxs = sb(nc.sbuf_tensor("mxs", [1, STRIP], F32))
        l2r = sb(nc.sbuf_tensor("l2r", [1, 4, 512], F32))
        rr = sb(nc.sbuf_tensor("rr", [1, 4, 512], F32))
        rr_bf = sb(nc.sbuf_tensor("rr_bf", [1, 2, 512], BF16))
        g_bf = sb(nc.sbuf_tensor("g_bf", [1, STRIP], BF16))
        g_lo = sb(nc.sbuf_tensor("g_lo", [1, STRIP], BF16))
        Rrk = sb(nc.sbuf_tensor("Rrk", [128, STRIP], BF16))
        Rb = sb(nc.sbuf_tensor("Rb", [128, STRIP], BF16))
        Rg = sb(nc.sbuf_tensor("Rg", [128, STRIP], F32))
        Rg1m = sb(nc.sbuf_tensor("Rg1m", [128, STRIP], F32))
        khT = sb(nc.sbuf_tensor("khT", [128, 2, STRIP], BF16))
        kbT = sb(nc.sbuf_tensor("kbT", [128, 2, STRIP], BF16))
        vhT = sb(nc.sbuf_tensor("vhT", [128, 2, STRIP], BF16))
        vgb = sb(nc.sbuf_tensor("vgb", [128, 2, STRIP], BF16))
        ema_o = sb(nc.sbuf_tensor("ema_o", [128, 2, STRIP], F32))
        ema_bf = sb(nc.sbuf_tensor("ema_bf", [128, 2, STRIP], BF16))
        ema_st = sb(nc.sbuf_tensor("ema_st", [128, 2], F32))
        tokb = sb(nc.sbuf_tensor("tokb", [128, SPC, 4 * DK], BF16))
        sc_rq = sb(nc.sbuf_tensor("sc_rq", [128, SPC], F32))
        sc_mx = sb(nc.sbuf_tensor("sc_mx", [128, SPC], F32))
        S_sb = sb(nc.sbuf_tensor("S_sb", [128, 2, DK], BF16))
        S_f32 = sb(nc.sbuf_tensor("S_f32", [128, 2, DK], F32))
        nt_b = sb(nc.sbuf_tensor("nt_b", [128, SPC, 128], BF16))
        nn_b = sb(nc.sbuf_tensor("nn_b", [128, SPC, 128], BF16))
        xg = [sb(nc.sbuf_tensor(f"xgv{i}", [128, SPC, 128], BF16)) for i in range(2)]
        xgt = [sb(nc.sbuf_tensor(f"xgt{i}", [128, SPC, 128], BF16)) for i in range(2)]
        ccb = [sb(nc.sbuf_tensor(f"ccv{i}", [128, SPC, 128], BF16)) for i in range(2)]
        pt_b = sb(nc.sbuf_tensor("pt_b", [128, SPC, 128], BF16))
        u_b = sb(nc.sbuf_tensor("u_b", [128, SPC, DK], BF16))
        wt_b = sb(nc.sbuf_tensor("wt_b", [128, SPC, 2, 128], BF16))
        uadj = sb(nc.sbuf_tensor("uadj", [128, DK], BF16))
        o1 = sb(nc.sbuf_tensor("o1b", [128, DK], F32))
        tgg = sb(nc.sbuf_tensor("tgg", [128, DK], F32))
        ogg = sb(nc.sbuf_tensor("ogg", [128, DK], F32))
        onb = sb(nc.sbuf_tensor("onb", [128, DK], BF16))
        stat = sb(nc.sbuf_tensor("stat", [128, 4], F32))
        spcr = sb(nc.sbuf_tensor("spcr", [128, 2], F32))
        onT_sb = [sb(nc.sbuf_tensor("onTs0", [128, 2, STRIP], BF16))] * 2

        S_dl = sb(nc.psum_tensor("S_dl", [128, 512], F32))
        p_tr = sb(nc.psum_tensor("p_tr", [128, 1024], BF16))
        p_a = sb(nc.psum_tensor("p_a", [128, 512], F32))
        p_b = sb(nc.psum_tensor("p_b", [128, 512], F32))
        p_c = sb(nc.psum_tensor("p_c", [128, 512], F32))
        p_d = sb(nc.psum_tensor("p_d", [128, 512], F32))
        p_sm = sb(nc.psum_tensor("p_sm", [2, 1024], F32))

        spc_tok = [None]

        def spacer():
            spc_tok[0] = EM("vector", TH("memset", spcr[:, 0:1], 0.0),
                            [spc_tok[0]], sV, 1)

        z0 = EM("vector", TH("memset", qraw[:, :, 0:3], 0.0), [w_tok], sV, 1)
        EM("vector", TH("memset", kraw[:, :, 0:3], 0.0), (), sV, 1)
        EM("vector", TH("memset", vraw[:, :, 0:3], 0.0), (), sV, 1)
        EM("vector", TH("memset", mixraw[0:1, 0:3], 0.0), (), sV, 1)
        EM("vector", TH("memset", ema_st[:, :], 0.0), (), sV, 1)
        EM("vector", TH("memset", S_sb[:].rearrange("p a c -> p (a c)"), 0.0),
           (), sV, 1)
        EM("vector", TH("memset", S_f32[:].rearrange("p a c -> p (a c)"), 0.0),
           (), sV, 1)

        bar = prog.barrier()
        onT_free = [None, None]
        xstg_free = [None, None]
        sadd_tok = [None]

        for s in range(NSTRIP):
            off = s * STRIP
            par = s % 2
            xTs = xT[par]

            xt_evt = None
            for t in range(STRIP // 128):
                pp = t % 2
                tin = dX.emit("sync", TH(
                    "dma_start", out=xstg[pp][:],
                    in_=x_h[off + t * 128: off + (t + 1) * 128, :]),
                    [xstg_free[pp],
                     w_tok if (s == 0 and t < 2) else None])
                tc = EM("vector", TH("tensor_copy", xbf[:], xstg[pp][:]),
                        [tin, xt_evt], sV, 1)
                xstg_free[pp] = tc
                trs = None
                for ds in range(8):
                    trs = EM("tensor", TH("transpose",
                                          p_tr[:, ds * 128:(ds + 1) * 128],
                                          xbf[:, ds * 128:(ds + 1) * 128], IDB),
                             [tc, cst["tok"] if s == 0 and t == 0 else None],
                             sT, 1)
                xt_evt = EM("vector", TH("tensor_copy",
                                         xTs[:, :, t * 128:(t + 1) * 128],
                                         p_tr[:].rearrange("p (a c) -> p a c", a=8)),
                            [trs], sV, 1)
            xt_done = xt_evt

            if s == 0:
                hq = hk = hv = hm = z0
            else:
                hq = EM("vector", TH("tensor_copy", qraw[:, :, 0:3],
                                     qraw[:, :, STRIP:STRIP + 3]),
                        [bar], sV, 1)
                hk = EM("vector", TH("tensor_copy", kraw[:, :, 0:3],
                                     kraw[:, :, STRIP:STRIP + 3]), (), sV, 1)
                hv = EM("vector", TH("tensor_copy", vraw[:, :, 0:3],
                                     vraw[:, :, STRIP:STRIP + 3]), (), sV, 1)
                hm = EM("vector", TH("tensor_copy", mixraw[0:1, 0:3],
                                     mixraw[0:1, STRIP:STRIP + 3]), (), sV, 1)

            def proj(dstraw, wgt, hist_tok):
                ev = None
                for hf in range(2):
                    for cq in range(2):
                        pp = p_a if (hf * 2 + cq) % 2 == 0 else p_b
                        mmt = None
                        for ds in range(8):
                            mmt = EM("tensor", TH(
                                "matmul", pp[:],
                                wgt[:, ds, hf * 128:(hf + 1) * 128],
                                xTs[:, ds, cq * 512:(cq + 1) * 512],
                                start=(ds == 0), stop=(ds == 7)),
                                [xt_done if ds == 0 else None,
                                 w_tok if ds == 0 else None,
                                 ev if ds == 0 else None], sT, 1)
                        ev = EM("scalar", TH(
                            "copy", dstraw[:, hf, 3 + cq * 512:3 + (cq + 1) * 512],
                            pp[:]), [mmt, hist_tok], sA, 1)
                return ev
            pk = proj(kraw, wk, hk)
            pv = proj(vraw, wv, hv)
            pq = proj(qraw, wq, hq)

            bgmm = None
            for cq in range(2):
                for ds in range(8):
                    bgmm = EM("tensor", TH(
                        "matmul", p_sm[0:1, cq * 512:(cq + 1) * 512],
                        wbg[:, ds, 0:1], xTs[:, ds, cq * 512:(cq + 1) * 512],
                        start=(ds == 0), stop=(ds == 7)),
                        [xt_done, bar if s > 0 else None], sT, 1)
            gmm = []
            for cq, pp in ((0, p_c), (1, p_d)):
                t1 = None
                for ds in range(8):
                    t1 = EM("tensor", TH(
                        "matmul", pp[0:1, 0:512],
                        wbg[:, ds, 1:2], xTs[:, ds, cq * 512:(cq + 1) * 512],
                        start=(ds == 0), stop=(ds == 7)),
                        [xt_done, bar if s > 0 else None], sT, 1)
                gmm.append(t1)
            bg_ev = EM("scalar", TH("activation", out=beta_row[0:1, :],
                                    in_=p_sm[0:1, 0:1024], func=AF.Sigmoid),
                       [bgmm], sA, 1)
            g_ev0 = EM("scalar", TH("activation", out=gam_row[0:1, 0:512],
                                    in_=p_c[0:1, 0:512], func=AF.Sigmoid),
                       [gmm[0]], sA, 1)
            g_ev = EM("scalar", TH("activation", out=gam_row[0:1, 512:1024],
                                   in_=p_d[0:1, 0:512], func=AF.Sigmoid),
                      [gmm[1]], sA, 1)
            mxmm = None
            for cq in range(2):
                for ds in range(8):
                    mxmm = EM("tensor", TH(
                        "matmul", p_sm[0:1, cq * 512:(cq + 1) * 512],
                        wmx[:, ds, :], xTs[:, ds, cq * 512:(cq + 1) * 512],
                        start=(ds == 0), stop=(ds == 7)), [bg_ev], sT, 1)
            mx_ev = EM("scalar", TH("copy", mixraw[0:1, 3:3 + STRIP],
                                    p_sm[0:1, 0:1024]), [mxmm, hm], sA, 1)

            def conv(raw, cw, ev_tok):
                t_last = None
                for hf in range(2):
                    t0 = EM("vector", TH(
                        "tensor_scalar", out=cvt_a[:, hf, :], in0=raw[:, hf, 0:STRIP],
                        scalar1=cw[:, hf, 0:1], scalar2=None, op0=ALU.mult),
                        [ev_tok, cv_tok], sV, 1)
                    t1 = EM("vector", TH(
                        "scalar_tensor_tensor", out=cvt_b[:, hf, :],
                        in0=raw[:, hf, 1:1 + STRIP], scalar=cw[:, hf, 1:2],
                        in1=cvt_a[:, hf, :], op0=ALU.mult, op1=ALU.add),
                        [t0], sV, 1)
                    t2 = EM("vector", TH(
                        "scalar_tensor_tensor", out=cvt_a[:, hf, :],
                        in0=raw[:, hf, 2:2 + STRIP], scalar=cw[:, hf, 2:3],
                        in1=cvt_b[:, hf, :], op0=ALU.mult, op1=ALU.add),
                        [t1], sV, 1)
                    t_last = EM("vector", TH(
                        "scalar_tensor_tensor", out=cvt_b[:, hf, :],
                        in0=raw[:, hf, 3:3 + STRIP], scalar=cw[:, hf, 3:4],
                        in1=cvt_a[:, hf, :], op0=ALU.mult, op1=ALU.add),
                        [t2], sV, 1)
                return t_last

            cvk_t = conv(kraw, cvk, pk)
            sk1 = EM("scalar", TH("activation",
                                  out=cvt_a[:].rearrange("p a c -> p (a c)"),
                                  in_=cvt_b[:].rearrange("p a c -> p (a c)"),
                                  func=AF.Silu), [cvk_t], sA, 1)
            sk2 = EM("scalar", TH("activation",
                                  out=k3[:].rearrange("p a c -> p (a c)"),
                                  in_=cvt_a[:].rearrange("p a c -> p (a c)"),
                                  func=AF.Silu), [sk1], sA, 1)
            cvv_t = conv(vraw, cvv, sk2)
            sv1 = EM("scalar", TH("activation",
                                  out=v3[:].rearrange("p a c -> p (a c)"),
                                  in_=cvt_b[:].rearrange("p a c -> p (a c)"),
                                  func=AF.Silu), [cvv_t], sA, 1)
            cvq_t = conv(qraw, cvq, sv1)
            sq1 = EM("scalar", TH("activation",
                                  out=cvt_a[:].rearrange("p a c -> p (a c)"),
                                  in_=cvt_b[:].rearrange("p a c -> p (a c)"),
                                  func=AF.Silu), [cvq_t], sA, 1)
            sq2t = EM("scalar", TH("activation",
                                   out=q3[:].rearrange("p a c -> p (a c)"),
                                   in_=cvt_a[:].rearrange("p a c -> p (a c)"),
                                   func=AF.Silu), [sq1], sA, 1)
            m0 = EM("vector", TH("tensor_scalar", out=cvt_a[0:1, 0, :],
                                 in0=mixraw[0:1, 0:STRIP], scalar1=cvm[0:1, 0:1],
                                 scalar2=None, op0=ALU.mult), [mx_ev, sq2t], sV, 1)
            m1 = EM("vector", TH("scalar_tensor_tensor", out=cvt_b[0:1, 0, :],
                                 in0=mixraw[0:1, 1:1 + STRIP], scalar=cvm[0:1, 1:2],
                                 in1=cvt_a[0:1, 0, :], op0=ALU.mult, op1=ALU.add),
                    [m0], sV, 1)
            m2 = EM("vector", TH("scalar_tensor_tensor", out=cvt_a[0:1, 0, :],
                                 in0=mixraw[0:1, 2:2 + STRIP], scalar=cvm[0:1, 2:3],
                                 in1=cvt_b[0:1, 0, :], op0=ALU.mult, op1=ALU.add),
                    [m1], sV, 1)
            m3 = EM("vector", TH("scalar_tensor_tensor", out=cvt_b[0:1, 0, :],
                                 in0=mixraw[0:1, 3:3 + STRIP], scalar=cvm[0:1, 3:4],
                                 in1=cvt_a[0:1, 0, :], op0=ALU.mult, op1=ALU.add),
                    [m2], sV, 1)
            m4 = EM("scalar", TH("activation", out=cvt_a[0:1, 0, :],
                                 in_=cvt_b[0:1, 0, :], func=AF.Silu),
                    [m3], sA, 1)
            mx_t = EM("scalar", TH("activation", out=mxs[0:1, :],
                                   in_=cvt_a[0:1, 0, :], func=AF.Sigmoid,
                                   bias=mxb[0:1, 0:1], scale=1.0), [m4], sA, 1)

            sq = cvt_a
            sqk = EM("scalar", TH("activation",
                                  out=sq[:].rearrange("p a c -> p (a c)"),
                                  in_=k3[:].rearrange("p a c -> p (a c)"),
                                  func=AF.Square),
                     [sk2, mx_t, bar if s > 0 else None], sA, 1)
            ssk = None
            for cq in range(2):
                EM("tensor", TH("matmul", p_sm[0:1, cq * 512:(cq + 1) * 512],
                                cst["ones_col"][:], sq[:, 0, cq * 512:(cq + 1) * 512],
                                start=True, stop=False), [sqk, mx_ev], sT, 1)
                ssk = EM("tensor", TH("matmul", p_sm[0:1, cq * 512:(cq + 1) * 512],
                                      cst["ones_col"][:], sq[:, 1, cq * 512:(cq + 1) * 512],
                                      start=False, stop=True), (), sT, 1)
            sqq = EM("scalar", TH("activation",
                                  out=sq[:].rearrange("p a c -> p (a c)"),
                                  in_=q3[:].rearrange("p a c -> p (a c)"),
                                  func=AF.Square), [sq2t, ssk], sA, 1)
            ssq = []
            for cq, pp in ((0, p_a), (1, p_b)):
                EM("tensor", TH("matmul", pp[0:1, 0:512], cst["ones_col"][:],
                                sq[:, 0, cq * 512:(cq + 1) * 512],
                                start=True, stop=False), [sqq], sT, 1)
                t1 = EM("tensor", TH("matmul", pp[0:1, 0:512], cst["ones_col"][:],
                                     sq[:, 1, cq * 512:(cq + 1) * 512],
                                     start=False, stop=True), (), sT, 1)
                ssq.append(t1)
            srk = EM("scalar", TH("activation", out=l2r[0:1, 0:2, :],
                                  in_=p_sm[0:1, 0:1024], func=AF.Sqrt,
                                  bias=cst["eps"][0:1, 0:1], scale=1.0),
                     [ssk], sA, 1)
            srq0 = EM("scalar", TH("activation", out=l2r[0:1, 2, :],
                                   in_=p_a[0:1, 0:512], func=AF.Sqrt,
                                   bias=cst["eps"][0:1, 0:1], scale=1.0),
                      [ssq[0]], sA, 1)
            srq1 = EM("scalar", TH("activation", out=l2r[0:1, 3, :],
                                   in_=p_b[0:1, 0:512], func=AF.Sqrt,
                                   bias=cst["eps"][0:1, 0:1], scale=1.0),
                      [ssq[1]], sA, 1)
            rcp = EM("vector", TH("reciprocal",
                                  rr[0:1].rearrange("p a c -> p (a c)"),
                                  l2r[0:1].rearrange("p a c -> p (a c)")),
                     [srk, srq0, srq1], sV, 1)
            spacer()
            rkb = EM("vector", TH("tensor_copy",
                                  rr_bf[0:1].rearrange("p a c -> p (a c)"),
                                  rr[0:1, 0:2, :].rearrange("p a c -> p (a c)")),
                     [rcp], sV, 1)
            gb = EM("vector", TH("tensor_copy", g_bf[0:1, :], gam_row[0:1, :]),
                    [g_ev, g_ev0], sV, 1)
            spacer()
            gl = EM("vector", TH("tensor_tensor", g_lo[0:1, :], gam_row[0:1, :],
                                 g_bf[0:1, :], ALU.subtract), [gb], sV, 1)
            bgb = EM("vector", TH("tensor_copy", beta_bf[0:1, :], beta_row[0:1, :]),
                     [gl, bg_ev], sV, 1)

            ev_rrk = None
            for cq in range(2):
                pp = p_c if cq == 0 else p_d
                t1 = EM("tensor", TH("matmul", pp[:], cst["ones_row"][0:1, 0:128],
                                     rr_bf[0:1, cq, :], start=True, stop=True),
                        [rkb], sT, 1)
                ev_rrk = EM("scalar", TH("copy", Rrk[:, cq * 512:(cq + 1) * 512],
                                         pp[:]), [t1], sA, 1)
            ev_rb = None
            for cq in range(2):
                pp = p_c if cq == 0 else p_d
                t1 = EM("tensor", TH("matmul", pp[:], cst["ones_row"][0:1, 0:128],
                                     beta_bf[0:1, cq * 512:(cq + 1) * 512],
                                     start=True, stop=True), [bgb, ev_rrk], sT, 1)
                ev_rb = EM("scalar", TH("copy", Rb[:, cq * 512:(cq + 1) * 512],
                                        pp[:]), [t1], sA, 1)
            ev_rg = None
            for cq in range(2):
                pp = p_c if cq == 0 else p_d
                EM("tensor", TH("matmul", pp[:], cst["ones_row"][0:1, 0:128],
                                g_bf[0:1, cq * 512:(cq + 1) * 512],
                                start=True, stop=False), [bgb, ev_rb], sT, 1)
                t1 = EM("tensor", TH("matmul", pp[:], cst["ones_row"][0:1, 0:128],
                                     g_lo[0:1, cq * 512:(cq + 1) * 512],
                                     start=False, stop=True), (), sT, 1)
                ev_rg = EM("scalar", TH("copy", Rg[:, cq * 512:(cq + 1) * 512],
                                        pp[:]), [t1], sA, 1)
            rg1 = EM("vector", TH("tensor_scalar", out=Rg1m[:, :], in0=Rg[:, :],
                                  scalar1=-1.0, scalar2=1.0,
                                  op0=ALU.mult, op1=ALU.add), [ev_rg], sV, 1)

            nk = nb = nv = ng = None
            for hf in range(2):
                nk = EM("vector", TH("tensor_tensor", khT[:, hf, :], k3[:, hf, :],
                                     Rrk[:, :], ALU.mult), [ev_rrk, sk2], sV, 1)
                nb = EM("vector", TH("tensor_tensor", kbT[:, hf, :], khT[:, hf, :],
                                     Rb[:, :], ALU.mult), [ev_rb, nk], sV, 1)
                nv = EM("vector", TH("tensor_tensor", vhT[:, hf, :], v3[:, hf, :],
                                     Rb[:, :], ALU.mult), [sv1], sV, 1)
                ng = EM("vector", TH("tensor_tensor", vgb[:, hf, :], v3[:, hf, :],
                                     Rg1m[:, :], ALU.mult), [rg1], sV, 1)

            em0 = EM("vector", TH("tensor_tensor_scan", out=ema_o[:, 0, :],
                                  data0=Rg[:, :], data1=vgb[:, 0, :],
                                  initial=ema_st[:, 0:1], op0=ALU.mult,
                                  op1=ALU.add), [ng, ev_rg], sV, 1)
            em1 = EM("vector", TH("tensor_tensor_scan", out=ema_o[:, 1, :],
                                  data0=Rg[:, :], data1=vgb[:, 1, :],
                                  initial=ema_st[:, 1:2], op0=ALU.mult,
                                  op1=ALU.add), (), sV, 1)
            es0 = EM("vector", TH("tensor_copy", ema_st[:, 0:1],
                                  ema_o[:, 0, STRIP - 1:STRIP]), [em1], sV, 1)
            es1 = EM("vector", TH("tensor_copy", ema_st[:, 1:2],
                                  ema_o[:, 1, STRIP - 1:STRIP]), (), sV, 1)
            emb = EM("vector", TH("tensor_copy",
                                  ema_bf[:].rearrange("p a c -> p (a c)"),
                                  ema_o[:].rearrange("p a c -> p (a c)")),
                     [es1], sV, 1)

            rr_flat = rr[0:1].rearrange("p a c -> p (a c)")
            sc_mm = None
            for c in range(SPC):
                sc_mm = EM("tensor", TH("transpose", p_c[:, c:c + 1],
                                        rr_flat[:, 1024 + c * 128:1024 + (c + 1) * 128],
                                        mEYE[0:1, 0:1]),
                           [rcp if c == 0 else None, ev_rg if c == 0 else None],
                           sT, 1)
            for c in range(SPC):
                sc_mm = EM("tensor", TH("transpose", p_c[:, 8 + c:9 + c],
                                        mxs[0:1, c * 128:(c + 1) * 128],
                                        mEYE[0:1, 0:1]),
                           [mx_t if c == 0 else None], sT, 1)
            scv1 = EM("vector", TH("tensor_copy", sc_rq[:, :], p_c[:, 0:8]),
                      [sc_mm], sV, 1)
            scv2 = EM("vector", TH("tensor_copy", sc_mx[:, :], p_c[:, 8:16]),
                      (), sV, 1)

            tok_evs = []
            for c in range(SPC):
                cw = slice(c * 128, (c + 1) * 128)
                trs = None
                srcs = (khT, kbT, vhT, ema_bf)
                deps = (nk, nb, nv, emb)
                for j in range(4):
                    for hf in range(2):
                        trs = EM("tensor", TH("transpose",
                                              p_tr[:, j * 256 + hf * 128:
                                                   j * 256 + (hf + 1) * 128],
                                              srcs[j][:, hf, cw], IDB),
                                 [deps[j] if hf == 0 else None,
                                  tok_evs[c - 1] if (j == 0 and hf == 0 and c > 0)
                                  else None,
                                  scv2 if (j == 0 and hf == 0 and c == 0) else None],
                                 sT, 1)
                ev = EM("vector", TH("tensor_copy", tokb[:, c, :], p_tr[:, :]),
                        [trs], sV, 1)
                tok_evs.append(ev)

            nt_toks = [None] * SPC
            nn_toks = [None] * SPC
            for wv_i in range(2):
                mm_last = None
                for ci in range(4):
                    c = wv_i * 4 + ci
                    cw = slice(c * 128, (c + 1) * 128)
                    EM("tensor", TH("matmul", p_a[:, ci * 128:(ci + 1) * 128],
                                    khT[:, 0, cw], kbT[:, 0, cw],
                                    start=True, stop=False),
                       [nb, nn_toks[3] if (wv_i == 1 and ci == 0) else None,
                        ssq[1] if (wv_i == 0 and ci == 0) else None], sT, 1)
                    EM("tensor", TH("matmul",
                                    p_a[:, ci * 128:(ci + 1) * 128],
                                    khT[:, 1, cw], kbT[:, 1, cw],
                                    start=False, stop=True), (), sT, 1)
                    EM("tensor", TH("matmul", p_b[:, ci * 128:(ci + 1) * 128],
                                    kbT[:, 0, cw], khT[:, 0, cw],
                                    start=True, stop=False), (), sT, 1)
                    mm_last = EM("tensor", TH("matmul",
                                              p_b[:, ci * 128:(ci + 1) * 128],
                                              kbT[:, 1, cw], khT[:, 1, cw],
                                              start=False, stop=True), (), sT, 1)
                for ci in range(4):
                    c = wv_i * 4 + ci
                    nt_toks[c] = EM("vector", TH(
                        "tensor_tensor", nt_b[:, c, :],
                        p_a[:, ci * 128:(ci + 1) * 128], mTRIUS, ALU.mult),
                        [mm_last], sV, 1)
                    nn_toks[c] = EM("vector", TH(
                        "tensor_tensor", nn_b[:, c, :],
                        p_b[:, ci * 128:(ci + 1) * 128], mTRILS, ALU.mult),
                        [mm_last], sV, 1)
            cc_tok = [None] * SPC
            for c in range(SPC):
                cc_tok[c] = EM("vector", TH("tensor_tensor", ccb[0][:, c, :],
                                            mEYE, nt_b[:, c, :], ALU.subtract),
                               [nt_toks[c]], sV, 1)

            x_cur, xt_cur = nt_b, nn_b
            x_tok, xt_tok = nt_toks, nn_toks
            cgen = 0
            prev_ncc_last = scv2
            for lvl in range(6):
                xng = xg[lvl % 2]
                xtg = xgt[lvl % 2]
                nx_tok = [None] * SPC
                nxt_tok = [None] * SPC
                ncc_tok = [None] * SPC
                for wv_i in range(2):
                    mm_last = None
                    for ci in range(4):
                        c = wv_i * 4 + ci
                        if wv_i == 0 and ci == 0:
                            guard = (nn_toks[SPC - 1] if lvl == 0
                                     else xt_tok[SPC - 1])
                        elif wv_i == 1 and ci == 0:
                            guard = nxt_tok[3]
                        else:
                            guard = None
                        if lvl < 5:
                            EM("tensor", TH(
                                "matmul", p_a[:, ci * 128:(ci + 1) * 128],
                                xt_cur[:, c, :], x_cur[:, c, :],
                                start=True, stop=True),
                                [x_tok[c], xt_tok[c], guard], sT, 1)
                        mm_last = EM("tensor", TH(
                            "matmul", p_b[:, ci * 128:(ci + 1) * 128],
                            x_cur[:, c, :], xt_cur[:, c, :],
                            start=True, stop=True),
                            [x_tok[c], xt_tok[c], guard], sT, 1)
                    for ci in range(4):
                        c = wv_i * 4 + ci
                        if lvl < 5:
                            nx_tok[c] = EM("vector", TH(
                                "tensor_copy", xng[:, c, :],
                                p_a[:, ci * 128:(ci + 1) * 128]), [mm_last], sV, 1)
                        nxt_tok[c] = EM("vector", TH(
                            "tensor_copy", xtg[:, c, :],
                            p_b[:, ci * 128:(ci + 1) * 128]), [mm_last], sV, 1)
                    mm_cu_last = None
                    for ci in range(4):
                        c = wv_i * 4 + ci
                        mm_cu_last = EM("tensor", TH(
                            "matmul", p_c[:, ci * 128:(ci + 1) * 128],
                            xtg[:, c, :], ccb[cgen][:, c, :],
                            start=True, stop=True),
                            [nxt_tok[c], cc_tok[c],
                             prev_ncc_last if ci == 0 else None], sT, 1)
                    for ci in range(4):
                        c = wv_i * 4 + ci
                        ncc_tok[c] = EM("vector", TH(
                            "tensor_tensor", ccb[1 - cgen][:, c, :],
                            p_c[:, ci * 128:(ci + 1) * 128],
                            ccb[cgen][:, c, :], ALU.add), [mm_cu_last], sV, 1)
                    prev_ncc_last = ncc_tok[wv_i * 4 + 3]
                x_cur, xt_cur = xng, xtg
                x_tok, xt_tok = nx_tok, nxt_tok
                cc_tok = ncc_tok
                cgen = 1 - cgen
            CCF = ccb[cgen]
            cc_all = cc_tok[SPC - 1]

            u_toks = [None] * SPC
            wt_toks = [None] * SPC
            pt_toks = [None] * SPC
            for c in range(SPC):
                cw = slice(c * 128, (c + 1) * 128)
                sl = c % 2
                mm_u = EM("tensor", TH(
                    "matmul", p_a[:, sl * 256:(sl + 1) * 256],
                    CCF[:, c, :], tokb[:, c, 512:768],
                    start=True, stop=True),
                    [cc_all, tok_evs[c],
                     u_toks[c - 1] if c >= 1 else None], sT, 1)
                mm_w = None
                for hf in range(2):
                    mm_w = EM("tensor", TH(
                        "matmul",
                        p_b[:, sl * 256 + hf * 128:sl * 256 + (hf + 1) * 128],
                        tokb[:, c, 256 + hf * 128:256 + (hf + 1) * 128],
                        CCF[:, c, :], start=True, stop=True),
                        [wt_toks[c - 1] if (c >= 1 and hf == 0) else None],
                        sT, 1)
                EM("tensor", TH(
                    "matmul", p_c[:, sl * 128:(sl + 1) * 128],
                    khT[:, 0, cw], q3[:, 0, cw], start=True, stop=False),
                    [sq2t, pt_toks[c - 1] if c >= 1 else None,
                     prev_ncc_last if c == 0 else None], sT, 1)
                mm_p = EM("tensor", TH(
                    "matmul", p_c[:, sl * 128:(sl + 1) * 128],
                    khT[:, 1, cw], q3[:, 1, cw], start=False, stop=True),
                    (), sT, 1)
                u_toks[c] = EM("vector", TH(
                    "tensor_copy", u_b[:, c, :],
                    p_a[:, sl * 256:(sl + 1) * 256]), [mm_u], sV, 1)
                wt_toks[c] = EM("vector", TH(
                    "tensor_copy", wt_b[:, c, :, :].rearrange("p a c -> p (a c)"),
                    p_b[:, sl * 256:(sl + 1) * 256]), [mm_w], sV, 1)
                pt_toks[c] = EM("vector", TH(
                    "tensor_tensor", pt_b[:, c, :],
                    p_c[:, sl * 128:(sl + 1) * 128], mTRIUI, ALU.mult),
                    [mm_p], sV, 1)

            ssb_tok = None
            ot1_prev = [None]
            for c in range(SPC):
                cw = slice(c * 128, (c + 1) * 128)
                slot = s * SPC + c
                EM("tensor", TH("matmul", p_c[:, 0:256], wt_b[:, c, 0, :],
                                S_sb[:, 0, :], start=True, stop=False),
                   [wt_toks[c], ssb_tok,
                    pt_toks[SPC - 1] if c == 0 else None], sT, 1)
                ws1 = EM("tensor", TH("matmul", p_c[:, 0:256], wt_b[:, c, 1, :],
                                      S_sb[:, 1, :], start=False, stop=True),
                         (), sT, 1)
                ua = EM("vector", TH("tensor_tensor", uadj[:, :], u_b[:, c, :],
                                     p_c[:, 0:256], ALU.subtract),
                        [ws1, u_toks[c]], sV, 1)
                EM("tensor", TH("matmul", p_d[:, 0:256], q3[:, 0, cw],
                                S_sb[:, 0, :], start=True, stop=False),
                   [ua, ev_rg if c == 0 else None, ot1_prev[0]], sT, 1)
                EM("tensor", TH("matmul", p_d[:, 0:256], q3[:, 1, cw],
                                S_sb[:, 1, :], start=False, stop=False),
                   (), sT, 1)
                om = EM("tensor", TH("matmul", p_d[:, 0:256], pt_b[:, c, :],
                                     uadj[:, :], start=False, stop=True),
                        (), sT, 1)
                EM("tensor", TH("matmul", S_dl[:, 0:256],
                                tokb[:, c, 0:128], uadj[:, :],
                                start=True, stop=False),
                   [sadd_tok[0]], sT, 1)
                su = EM("tensor", TH("matmul", S_dl[:, 256:512],
                                     tokb[:, c, 128:256], uadj[:, :],
                                     start=False, stop=True), (), sT, 1)
                sadd_tok[0] = EM("vector", TH(
                    "tensor_tensor", S_f32[:].rearrange("p a c -> p (a c)"),
                    S_f32[:].rearrange("p a c -> p (a c)"), S_dl[:, :],
                    ALU.add), [su], sV, 1)
                ssb_tok = EM("scalar", TH("copy",
                                          S_sb[:].rearrange("p a c -> p (a c)"),
                                          S_f32[:].rearrange("p a c -> p (a c)")),
                            [sadd_tok[0]], sA, 1)
                ot1 = EM("vector", TH("tensor_scalar", out=o1[:, :],
                                      in0=p_d[:, 0:256], scalar1=sc_rq[:, c:c + 1],
                                      scalar2=None, op0=ALU.mult),
                         [om, scv1], sV, 1)
                ot1_prev[0] = ot1
                ot2 = EM("vector", TH("tensor_tensor", tgg[:, :],
                                      tokb[:, c, 768:1024], o1[:, :],
                                      ALU.subtract), [ot1], sV, 1)
                ot3 = EM("vector", TH("scalar_tensor_tensor", out=ogg[:, :],
                                      in0=tgg[:, :], scalar=sc_mx[:, c:c + 1],
                                      in1=o1[:, :], op0=ALU.mult, op1=ALU.add),
                         [ot2, scv2], sV, 1)
                sq2e = EM("scalar", TH("activation", out=tgg[:, :], in_=ogg[:, :],
                                       func=AF.Square), [ot3], sA, 1)
                rs = EM("vector", TH("reduce_sum", stat[:, 0:1], tgg[:, :],
                                     axis=AX.X), [sq2e], sV, 1)
                st = EM("scalar", TH("activation", out=stat[:, 1:2],
                                     in_=stat[:, 0:1], func=AF.Sqrt,
                                     bias=cst["eps"][:, 1:2], scale=1.0 / DK),
                        [rs], sA, 1)
                rc = EM("vector", TH("reciprocal", stat[:, 2:3], stat[:, 1:2]),
                        [st], sV, 1)
                spacer()
                onb_t = EM("vector", TH("tensor_scalar", out=onb[:, :],
                                        in0=ogg[:, :], scalar1=stat[:, 2:3],
                                        scalar2=None, op0=ALU.mult), [rc], sV, 1)
                EM("tensor", TH("transpose", p_tr[:, 0:128],
                                onb[:, 0:128], IDB), [onb_t], sT, 1)
                tr1 = EM("tensor", TH("transpose", p_tr[:, 128:256],
                                      onb[:, 128:256], IDB), (), sT, 1)
                EM("vector", TH("tensor_copy",
                                onT_sb[par][:, :, cw],
                                p_tr[:, 0:256].rearrange("p (a c) -> p a c", a=2)),
                   [tr1, onT_free[0] if c == 0 else None], sV, 1)
            out_ready = prog.barrier(("vector",))
            onT_free[0] = dO.emit("gpsimd", TH(
                "dma_start",
                out=onT_h[:, :, off:off + STRIP].rearrange("s p t -> p s t"),
                in_=onT_sb[par][:, :, :]), [out_ready])

            bar = prog.barrier()

        EM("gpsimd", TH("nop"), [dO.total_tok()])

        with nc.Block() as block:
            prog.run(block)
    return nc


def build_ffn():
    nc = bass.Bass("TRN2", target_bir_lowering=False, debug=False)
    oc_h = nc.dram_tensor("oc", [D, TW], BF16, kind="ExternalInput").ap()
    xr_h = nc.dram_tensor("xr", [TW, D], F32, kind="ExternalInput").ap()
    wo_h = nc.dram_tensor("wo", [D, D], F32, kind="ExternalInput").ap()
    rms_h = nc.dram_tensor("rmsc", [128, 8], F32, kind="ExternalInput").ap()
    sw_h = nc.dram_tensor("sww", [D, NE], F32, kind="ExternalInput").ap()
    swb_h = nc.dram_tensor("swbi", [1, NE], F32, kind="ExternalInput").ap()
    w1_h = nc.dram_tensor("w1", [NE, D, FF], F32, kind="ExternalInput").ap()
    b1_h = nc.dram_tensor("b1r", [NE, 1, FF], F32, kind="ExternalInput").ap()
    w2_h = nc.dram_tensor("w2", [NE, FF, D], F32, kind="ExternalInput").ap()
    b2_h = nc.dram_tensor("b2", [NE, D], F32, kind="ExternalInput").ap()
    out_h = nc.dram_tensor("out", [TW, D], F32, kind="ExternalOutput").ap()
    w1bf = nc.dram_tensor("w1bf", [NE, 8, 128, FF], BF16, kind="Internal").ap()
    w2bf = nc.dram_tensor("w2bf", [NE, 32, 128, D], BF16, kind="Internal").ap()

    ctx = ExitStack()
    with ctx:
        prog = Prog(nc, ctx)
        sb = ctx.enter_context
        EM = prog.emit

        dI = DmaStream(prog, 4)
        dWF = DmaStream(prog, 2)
        dWB = DmaStream(prog, 4)
        dSL = DmaStream(prog, 8)
        dO = DmaStream(prog, 2)
        sT = prog.sem("sT")
        sV = prog.sem("sV")
        sA = prog.sem("sA")
        sG = prog.sem("sG")

        cst = make_consts(nc, ctx, prog, dI, masks=False)
        IDB = cst["ident"][:]
        su8_d = nc.inline_tensor(np.triu(np.ones((8, 8), dtype=BF), 1), "su8d").ap()
        su8 = sb(nc.sbuf_tensor("su8", [8, 8], BF16))
        dI.emit("sync", TH("dma_start", out=su8[:], in_=su8_d))
        id8_d = nc.inline_tensor(np.eye(8, dtype=BF), "id8d").ap()
        id8 = sb(nc.sbuf_tensor("id8", [8, 8], BF16))
        c_tok = dI.emit("sync", TH("dma_start", out=id8[:], in_=id8_d))

        x1T = sb(nc.sbuf_tensor("x1T", [128, 8, TW], BF16))
        out_acc = sb(nc.sbuf_tensor("out_acc", [128, NT, D], BF16))
        maskp = sb(nc.sbuf_tensor("maskp", [128, NT, NE], F32))
        lgs_all = sb(nc.sbuf_tensor("lgs_all", [128, NT, NE], F32))
        b1rb = sb(nc.sbuf_tensor("b1rb", [1, FF], BF16))
        wstg = [sb(nc.sbuf_tensor(f"wstg{i}", [128, 2048], F32)) for i in range(2)]
        w2cv = sb(nc.sbuf_tensor("w2cv", [128, 2, D], BF16))
        w1cv = sb(nc.sbuf_tensor("w1cv", [128, 2048], BF16))
        w1sl = [sb(nc.sbuf_tensor(f"w1sl{i}", [128, 8, 512], BF16)) for i in range(2)]
        w2sl = [sb(nc.sbuf_tensor(f"w2sl{i}", [128, 4, D], BF16)) for i in range(2)]
        rmsc = sb(nc.sbuf_tensor("rmscs", [128, 8], F32))
        swb = sb(nc.sbuf_tensor("swwb", [128, 8, NE], BF16))
        swbb = sb(nc.sbuf_tensor("swbbs", [1, NE], BF16))
        b2b = sb(nc.sbuf_tensor("b2bs", [8, D], BF16))
        xt_s = [sb(nc.sbuf_tensor(f"xts{i}", [128, D], F32)) for i in range(2)]
        ybuf = sb(nc.sbuf_tensor("ybuf", [128, 2, 512], F32))
        ysq = sb(nc.sbuf_tensor("ysq", [128, D], F32))
        lst = sb(nc.sbuf_tensor("lst", [128, 8], F32))
        expo = sb(nc.sbuf_tensor("expo", [128, NE], F32))
        eqb = sb(nc.sbuf_tensor("eqb", [128, NE], BF16))
        eqT = sb(nc.sbuf_tensor("eqT", [8, 128], BF16))
        zrow = sb(nc.sbuf_tensor("zrow", [8, 128], BF16))
        mTt = sb(nc.sbuf_tensor("mTt", [8, 128], BF16))
        msk = sb(nc.sbuf_tensor("msk", [128, NE], BF16))
        gbx = [sb(nc.sbuf_tensor(f"gbx{i}", [128, 512], BF16)) for i in range(3)]
        gb2 = [sb(nc.sbuf_tensor(f"gb2{i}", [128, 512], BF16)) for i in range(3)]
        gb3 = [sb(nc.sbuf_tensor(f"gb3{i}", [128, 512], BF16)) for i in range(3)]
        spcr = sb(nc.sbuf_tensor("spcr", [128, 2], F32))

        p_y = [sb(nc.psum_tensor(f"p_y{i}", [128, 512], F32)) for i in range(7)]
        ctx2 = ExitStack()
        p_trb = ctx2.enter_context(nc.psum_tensor("p_trb", [128, 1024], BF16))
        ctxo = ExitStack()
        oc = ctxo.enter_context(nc.sbuf_tensor("ocb", [128, 8, TW], BF16))
        wob = ctxo.enter_context(nc.sbuf_tensor("wob", [128, 8, D], BF16))

        spc_tok = [None]

        def spacer():
            spc_tok[0] = EM("vector", TH("memset", spcr[:, 0:1], 0.0),
                            [spc_tok[0]], sV, 1)

        t_oc = dI.emit("sync", TH("dma_start", out=oc[:],
                                   in_=oc_h.rearrange("(a p) t -> p a t", p=128)))
        t_rm = dI.emit("sync", TH("dma_start", out=rmsc[:], in_=rms_h))
        wo_tok = None
        wo_toks = []
        for pc in range(4):
            stg = wstg[pc % 2]
            t_wo = dI.emit("sync", TH(
                "dma_start",
                out=stg[:].rearrange("p (a c) -> p a c", a=2),
                in_=wo_h[pc * 256:(pc + 1) * 256, :]
                .rearrange("(a p) c -> p a c", p=128)),
                [wo_toks[pc - 2] if pc >= 2 else None])
            for ds in range(2):
                wo_tok = EM("vector", TH(
                    "tensor_scalar", out=wob[:, pc * 2 + ds, :],
                    in0=stg[:, ds * 1024:(ds + 1) * 1024],
                    scalar1=rmsc[:, pc * 2 + ds:pc * 2 + ds + 1], scalar2=None,
                    op0=ALU.mult), [t_wo, t_rm], sV, 1)
            wo_toks.append(wo_tok)
        t_sw = dI.emit("sync", TH(
            "dma_start", out=ysq[:, 0:64].rearrange("p (a c) -> p a c", a=8),
            in_=sw_h.rearrange("(a p) c -> p a c", p=128)))
        t_sw = EM("vector", TH("tensor_copy",
                               swb[:].rearrange("p a c -> p (a c)"),
                               ysq[:, 0:64]), [t_sw], sV, 1)
        t_swb = dI.emit("sync", TH("dma_start", out=lst[0:1, 0:8], in_=swb_h))
        t_swb = EM("vector", TH("tensor_copy", swbb[:], lst[0:1, 0:8]),
                   [t_swb], sV, 1)
        t_b2 = dI.emit("sync", TH("dma_start", out=xt_s[0][0:8, :], in_=b2_h))
        t_b2 = EM("vector", TH("tensor_copy", b2b[:, :], xt_s[0][0:8, :]),
                  [t_b2, t_sw], sV, 1)

        stg_free = [None, None]
        conv1_done = [None] * NE
        conv2_done = [None] * NE
        w1cv_free = [None]
        w2cv_free = [None]

        def emit_wconv(e, first=False):
            lastw = None
            for pc in range(8):
                for hf in range(2):
                    j = pc * 2 + hf
                    pp = j % 2
                    ti = dWF.emit("sync", TH(
                        "dma_start", out=wstg[pp][:],
                        in_=w1_h[e, pc * 128:(pc + 1) * 128,
                                 hf * 2048:(hf + 1) * 2048]),
                        [stg_free[pp],
                         wo_toks[-1] if (first and j <= 1) else None])
                    tc = EM("vector", TH("tensor_copy", w1cv[:, :], wstg[pp][:]),
                            [ti, w1cv_free[0]], sV, 1)
                    stg_free[pp] = tc
                    lastw = dWB.emit("gpsimd", TH(
                        "dma_start",
                        out=w1bf[e, pc, :, hf * 2048:(hf + 1) * 2048],
                        in_=w1cv[:, :]), [tc])
                    w1cv_free[0] = lastw
            conv1_done[e] = lastw
            lastw = None
            for pc in range(16):
                pp = pc % 2
                ti = dWF.emit("sync", TH(
                    "dma_start",
                    out=wstg[pp][:].rearrange("p (a d) -> p a d", a=2),
                    in_=w2_h[e, pc * 256:(pc + 1) * 256, :]
                    .rearrange("(a p) d -> p a d", p=128)),
                    [stg_free[pp]])
                tc = EM("scalar", TH("copy",
                                     w2cv[:].rearrange("p a d -> p (a d)"),
                                     wstg[pp][:]), [ti, w2cv_free[0]], sA, 1)
                stg_free[pp] = tc
                lastw = dWB.emit("gpsimd", TH(
                    "dma_start",
                    out=w2bf[e, pc * 2:(pc + 1) * 2].rearrange("s p d -> p s d"),
                    in_=w2cv[:]), [tc])
                w2cv_free[0] = lastw
            conv2_done[e] = lastw

        emit_wconv(0, first=True)
        emit_wconv(1)

        xt_free = [None, t_b2]
        p_free = {}
        lgs_toks = [None] * NT
        x1T_tok = [None] * NT
        oa_init = [None] * NT
        srt_prev = None
        s2_prev = None
        for t in range(NT):
            pp = t % 2
            tx = dI.emit("sync", TH("dma_start", out=xt_s[pp][:],
                                     in_=xr_h[t * 128:(t + 1) * 128, :]),
                         [xt_free[pp] if t > 1 else (t_b2 if pp == 0 else None)])
            yv = {}
            for dh in range(2):
                mm = None
                for ds in range(8):
                    mm = EM("tensor", TH(
                        "matmul", p_y[dh][:], oc[:, ds, t * 128:(t + 1) * 128],
                        wob[:, ds, dh * 512:(dh + 1) * 512],
                        start=(ds == 0), stop=(ds == 7)),
                        [t_oc if ds == 0 else None,
                         wo_toks[-1] if ds == 0 else None,
                         p_free.get(dh) if ds == 0 else None,
                         c_tok if (t == 0 and ds == 0) else None], sT, 1)
                yv[dh] = EM("vector", TH(
                    "tensor_tensor", ybuf[:, dh, :], p_y[dh][:],
                    xt_s[pp][:, dh * 512:(dh + 1) * 512], ALU.add),
                    [mm, tx, srt_prev if dh == 0 else None], sV, 1)
                p_free[dh] = yv[dh]
            yflat = ybuf[:].rearrange("p a c -> p (a c)")
            s1 = EM("vector", TH("reduce_sum", lst[:, 0:1], yflat, axis=AX.X),
                    [yv[1], srt_prev], sV, 1)
            sqy = EM("scalar", TH("activation", out=ysq[:, :], in_=yflat,
                                  func=AF.Square), [yv[1], t_b2, s2_prev], sA, 1)
            s2 = EM("vector", TH("reduce_sum", lst[:, 1:2], ysq[:, :], axis=AX.X),
                    [sqy], sV, 1)
            s2_prev = s2
            mn = EM("scalar", TH("mul", lst[:, 2:3], lst[:, 0:1], 1.0 / D),
                    [s1], sA, 1)
            msq = EM("scalar", TH("activation", out=lst[:, 3:4], in_=lst[:, 2:3],
                                  func=AF.Square), [mn], sA, 1)
            var = EM("vector", TH("tensor_scalar", out=lst[:, 4:5], in0=lst[:, 1:2],
                                  scalar1=1.0 / D, scalar2=lst[:, 3:4],
                                  op0=ALU.mult, op1=ALU.subtract),
                     [s2, msq], sV, 1)
            srt = EM("scalar", TH("activation", out=lst[:, 5:6], in_=lst[:, 4:5],
                                  func=AF.Sqrt, bias=cst["eps"][:, 1:2],
                                  scale=1.0), [var], sA, 1)
            srt_prev = srt
            rst = EM("vector", TH("reciprocal", lst[:, 6:7], lst[:, 5:6]),
                     [srt], sV, 1)
            spacer()
            x1t = EM("vector", TH("tensor_scalar", out=out_acc[:, t, :], in0=yflat,
                                  scalar1=lst[:, 2:3], scalar2=lst[:, 6:7],
                                  op0=ALU.subtract, op1=ALU.mult),
                     [rst], sV, 1)
            oa_init[t] = x1t
            xt_free[pp] = x1t
            trx = None
            for ds in range(8):
                trx = EM("tensor", TH("transpose",
                                      p_trb[:, ds * 128:(ds + 1) * 128],
                                      out_acc[:, t, ds * 128:(ds + 1) * 128], IDB),
                         [x1t if ds == 0 else None,
                          x1T_tok[t - 1] if (ds == 0 and t > 0) else None], sT, 1)
            x1T_tok[t] = EM("vector", TH("tensor_copy",
                                         x1T[:, :, t * 128:(t + 1) * 128],
                                         p_trb[:].rearrange("p (a c) -> p a c", a=8)),
                            [trx], sV, 1)
            lmm = None
            for ds in range(8):
                lmm = EM("tensor", TH("matmul", p_y[2][:, 0:NE],
                                      x1T[:, ds, t * 128:(t + 1) * 128],
                                      swb[:, ds, :], start=(ds == 0), stop=False),
                         [x1T_tok[t] if ds == 0 else None,
                          t_sw if ds == 0 else None,
                          lgs_toks[t - 1] if (ds == 0 and t > 0) else None], sT, 1)
            lmm = EM("tensor", TH("matmul", p_y[2][:, 0:NE],
                                  cst["ones_row"][0:1, 0:128], swbb[0:1, :],
                                  start=False, stop=True), [t_swb], sT, 1)
            lgs_toks[t] = EM("vector", TH("tensor_copy", lgs_all[:, t, :],
                                          p_y[2][:, 0:NE]), [lmm], sV, 1)

        route_tok = [None] * NT
        mtt_tok = None
        ex_prev = srt_prev
        for t in range(NT):
            lm = EM("vector", TH("reduce_max", lst[:, 0:1], lgs_all[:, t, :],
                                 axis=AX.X),
                    [lgs_toks[NT - 1] if t == 0 else None, ex_prev], sV, 1)
            nlm = EM("scalar", TH("mul", lst[:, 1:2], lst[:, 0:1], -1.0),
                     [lm], sA, 1)
            ex = EM("scalar", TH("activation", out=expo[:, :], in_=lgs_all[:, t, :],
                                 func=AF.Exp, bias=lst[:, 1:2], scale=1.0),
                    [nlm], sA, 1)
            ex_prev = ex
            sm = EM("vector", TH("reduce_sum", lst[:, 2:3], expo[:, :], axis=AX.X),
                    [ex], sV, 1)
            eq = EM("vector", TH("tensor_single_scalar", eqb[:, :],
                                 lgs_all[:, t, :], lst[:, 0:1], ALU.is_ge),
                    [sm], sV, 1)
            rpm = EM("vector", TH("reciprocal", lst[:, 3:4], lst[:, 2:3]),
                     [eq], sV, 1)
            tr_eq = EM("tensor", TH("transpose", p_trb[0:8, 0:128], eqb[:, :], IDB),
                       [eq, mtt_tok], sT, 1)
            eqt = EM("vector", TH("tensor_copy", eqT[:, :], p_trb[0:8, 0:128]),
                     [tr_eq], sV, 1)
            cmm = EM("tensor", TH("matmul", p_y[3][0:8, 0:128], su8[:, :],
                                  eqT[:, :], start=True, stop=True),
                     [eqt, route_tok[t - 1] if t > 0 else None], sT, 1)
            zr = EM("vector", TH("tensor_single_scalar", zrow[:, :],
                                 p_y[3][0:8, 0:128], 0.0, ALU.is_equal),
                    [cmm], sV, 1)
            mt = EM("vector", TH("tensor_tensor", mTt[:, :], eqT[:, :],
                                 zrow[:, :], ALU.mult), [zr], sV, 1)
            tr_m = EM("tensor", TH("transpose", p_trb[0:128, 128:136], mTt[:, :],
                                   id8[:, :]), [mt], sT, 1)
            mk = EM("vector", TH("tensor_copy", msk[:, :], p_trb[0:128, 128:136]),
                    [tr_m], sV, 1)
            mpq = EM("vector", TH("tensor_scalar", out=maskp[:, t, :],
                                  in0=msk[:, :], scalar1=lst[:, 3:4], scalar2=None,
                                  op0=ALU.mult), [mk, rpm], sV, 1)
            ini = None
            for dh in range(2):
                bmm = EM("tensor", TH("matmul", p_y[4 + dh][:], mTt[:, :],
                                      b2b[:, dh * 512:(dh + 1) * 512],
                                      start=True, stop=True),
                         [mt, t_b2, route_tok[t - 1] if t >= 1 else None], sT, 1)
                ini = EM("vector", TH("scalar_tensor_tensor",
                                      out=out_acc[:, t, dh * 512:(dh + 1) * 512],
                                      in0=p_y[4 + dh][:], scalar=lst[:, 3:4],
                                      in1=out_acc[:, t, dh * 512:(dh + 1) * 512],
                                      op0=ALU.mult, op1=ALU.add),
                         [bmm, mpq if dh == 0 else None, oa_init[t]], sV, 1)
            route_tok[t] = ini
            mtt_tok = ini

        ctx2.close()
        p_y.append(sb(nc.psum_tensor("p_y7", [128, 512], F32)))
        bar_b2 = prog.barrier()
        ctxo.close()
        hT = sb(nc.sbuf_tensor("hT", [128, 32, 512], BF16))

        w1sl_free = [None, None]
        w2sl_free = [None, None]
        py_free = [None] * 8
        hT_rd = [None] * 32
        oa_tok = {t: route_tok[t] for t in range(NT)}
        b1_tok = None
        prev_h_all = None
        gel_free = [None] * 3
        h_psum_free = [None] * 4
        for e in range(NE):
            if e + 2 < NE:
                emit_wconv(e + 2)
            for j in range(4):
                b1_tok = dI.emit("sync", TH(
                    "dma_start", out=xt_s[0][0:1, :],
                    in_=b1_h[e, :, j * 1024:(j + 1) * 1024]),
                    [b1_tok, bar_b2 if (e == 0 and j == 0) else None])
                b1_tok = EM("vector", TH("tensor_copy",
                                         b1rb[0:1, j * 1024:(j + 1) * 1024],
                                         xt_s[0][0:1, :]),
                            [b1_tok, prev_h_all if j == 0 else None], sV, 1)
            for q in range(4):
                qs = slice(q * 512, (q + 1) * 512)
                hts = []
                for g in range(8):
                    gp = g % 2
                    t_w1 = dSL.emit("scalar", TH(
                        "dma_start", out=w1sl[gp][:],
                        in_=w1bf[e, :, :, g * 512:(g + 1) * 512]
                        .rearrange("a p c -> p a c")),
                        [w1sl_free[gp], conv1_done[e] if g <= 1 else None])
                    for i in range(4):
                        fs = g * 4 + i
                        pj = fs % 4
                        mm = None
                        for ds in range(8):
                            mm = EM("tensor", TH(
                                "matmul", p_y[pj][:],
                                w1sl[gp][:, ds, i * 128:(i + 1) * 128],
                                x1T[:, ds, qs],
                                start=(ds == 0), stop=False),
                                [t_w1 if ds == 0 else None,
                                 h_psum_free[pj] if ds == 0 else None,
                                 bar_b2 if (e == 0 and q == 0 and fs == 0
                                            and ds == 0) else None], sT, 1)
                        mm = EM("tensor", TH(
                            "matmul", p_y[pj][:],
                            b1rb[0:1, fs * 128:(fs + 1) * 128],
                            cst["ones_row"][0:1, :],
                            start=False, stop=True), [b1_tok], sT, 1)
                        gj = fs % 3
                        gx = EM("vector", TH("tensor_copy", gbx[gj][:, :],
                                             p_y[pj][:]),
                                [mm, gel_free[gj]], sV, 1)
                        h_psum_free[pj] = gx
                        g2t = EM("scalar", TH("activation", out=gb2[gj][:, :],
                                              in_=gbx[gj][:, :],
                                              func=AF.Square), [gx], sA, 1)
                        g3t = EM("vector", TH("tensor_scalar", out=gb3[gj][:, :],
                                              in0=gb2[gj][:, :], scalar1=C3,
                                              scalar2=C1, op0=ALU.mult,
                                              op1=ALU.add), [g2t], sV, 1)
                        g4t = EM("gpsimd", TH("tensor_mul", gb2[gj][:, :],
                                              gb3[gj][:, :], gbx[gj][:, :]),
                                 [g3t], sG, 1)
                        g5t = EM("scalar", TH("activation", out=gb3[gj][:, :],
                                              in_=gb2[gj][:, :],
                                              func=AF.Sigmoid), [g4t], sA, 1)
                        ht = EM("gpsimd", TH("tensor_mul", hT[:, fs, :],
                                             gb3[gj][:, :], gbx[gj][:, :]),
                                [g5t, hT_rd[fs]], sG, 1)
                        gel_free[gj] = ht
                        hts.append(ht)
                    w1sl_free[gp] = hts[-1]
                h_all = hts[-1]
                for g in range(8):
                    gp = g % 2
                    t_w2 = dSL.emit("scalar", TH(
                        "dma_start", out=w2sl[gp][:],
                        in_=w2bf[e, g * 4:(g + 1) * 4].rearrange("s p d -> p s d")),
                        [w2sl_free[gp], conv2_done[e] if g <= 1 else None])
                    mmy = None
                    for i in range(4):
                        fs = g * 4 + i
                        for tt in range(4):
                            for dh in range(2):
                                pj = tt * 2 + dh
                                mmy = EM("tensor", TH(
                                    "matmul", p_y[pj][:],
                                    hT[:, fs, tt * 128:(tt + 1) * 128],
                                    w2sl[gp][:, i, dh * 512:(dh + 1) * 512],
                                    start=(fs == 0), stop=(fs == 31)),
                                    [t_w2 if (i == 0 and tt == 0 and dh == 0)
                                     else None,
                                     h_all if (fs == 0 and tt == 0 and dh == 0)
                                     else None,
                                     py_free[pj] if fs == 0 else None], sT, 1)
                        hT_rd[fs] = mmy
                    w2sl_free[gp] = mmy
                y_done = mmy
                for tt in range(4):
                    tile = q * 4 + tt
                    for dh in range(2):
                        pj = tt * 2 + dh
                        oa_tok[tile] = EM("vector", TH(
                            "scalar_tensor_tensor",
                            out=out_acc[:, tile, dh * 512:(dh + 1) * 512],
                            in0=p_y[pj][:], scalar=maskp[:, tile, e:e + 1],
                            in1=out_acc[:, tile, dh * 512:(dh + 1) * 512],
                            op0=ALU.mult, op1=ALU.add),
                            [y_done, oa_tok[tile]], sV, 1)
                        py_free[pj] = oa_tok[tile]
                for pj in range(4):
                    h_psum_free[pj] = py_free[pj]
            prev_h_all = h_all

        osb = xt_s
        osb_free = [None, None]
        do_tok = None
        srt_prev = None
        s2_prev = None
        oa_last = oa_tok[NT - 1]
        for t in range(NT):
            pp = t % 2
            oaf = out_acc[:, t, :]
            s1 = EM("vector", TH("reduce_sum", lst[:, 0:1], oaf, axis=AX.X),
                    [oa_tok[t], srt_prev,
                     ex_prev if t == 0 else None], sV, 1)
            sqy = EM("scalar", TH("activation", out=ysq[:, :], in_=oaf,
                                  func=AF.Square), [oa_tok[t], s2_prev], sA, 1)
            s2 = EM("vector", TH("reduce_sum", lst[:, 1:2], ysq[:, :], axis=AX.X),
                    [sqy], sV, 1)
            s2_prev = s2
            mn = EM("scalar", TH("mul", lst[:, 2:3], lst[:, 0:1], 1.0 / D),
                    [s1], sA, 1)
            msq = EM("scalar", TH("activation", out=lst[:, 3:4], in_=lst[:, 2:3],
                                  func=AF.Square), [mn], sA, 1)
            var = EM("vector", TH("tensor_scalar", out=lst[:, 4:5], in0=lst[:, 1:2],
                                  scalar1=1.0 / D, scalar2=lst[:, 3:4],
                                  op0=ALU.mult, op1=ALU.subtract),
                     [s2, msq], sV, 1)
            srt = EM("scalar", TH("activation", out=lst[:, 5:6], in_=lst[:, 4:5],
                                  func=AF.Sqrt, bias=cst["eps"][:, 1:2],
                                  scale=1.0), [var], sA, 1)
            srt_prev = srt
            rst = EM("vector", TH("reciprocal", lst[:, 6:7], lst[:, 5:6]),
                     [srt], sV, 1)
            spacer()
            ot = EM("vector", TH("tensor_scalar", out=osb[pp][:, :], in0=oaf,
                                 scalar1=lst[:, 2:3], scalar2=lst[:, 6:7],
                                 op0=ALU.subtract, op1=ALU.mult),
                    [rst, osb_free[pp], oa_last if t == 0 else None], sV, 1)
            do_tok = dO.emit("gpsimd", TH("dma_start",
                                          out=out_h[t * 128:(t + 1) * 128, :],
                                          in_=osb[pp][:, :]), [ot])
            osb_free[pp] = do_tok
        EM("gpsimd", TH("nop"), [dO.total_tok()])

        with nc.Block() as block:
            prog.run(block)
    return nc


SP2 = 512
SPC2 = SP2 // CH
NST2 = L // SP2
NOWN = TW // SP2


def build_fused():
    nc = bass.Bass("TRN2", target_bir_lowering=False, debug=False)
    x_h = nc.dram_tensor("x", [L, D], F32, kind="ExternalInput").ap()
    wq_h = nc.dram_tensor("wq", [D, D], F32, kind="ExternalInput").ap()
    wk_h = nc.dram_tensor("wk", [D, D], F32, kind="ExternalInput").ap()
    wv_h = nc.dram_tensor("wv", [D, D], F32, kind="ExternalInput").ap()
    wbg_h = nc.dram_tensor("wbg", [128, 8, 8], F32, kind="ExternalInput").ap()
    wmx_h = nc.dram_tensor("wmx", [128, 8, 4], F32, kind="ExternalInput").ap()
    cvq_h = nc.dram_tensor("cvq", [128, 8, 4], F32, kind="ExternalInput").ap()
    cvk_h = nc.dram_tensor("cvk", [128, 8, 4], F32, kind="ExternalInput").ap()
    cvv_h = nc.dram_tensor("cvv", [128, 8, 4], F32, kind="ExternalInput").ap()
    cvm_h = nc.dram_tensor("cvm", [1, 4, 4], F32, kind="ExternalInput").ap()
    mxb_h = nc.dram_tensor("mxb", [1, 4], F32, kind="ExternalInput").ap()
    wo_h = nc.dram_tensor("wo", [D, D], F32, kind="ExternalInput").ap()
    rms_h = nc.dram_tensor("rmsc", [128, 8], F32, kind="ExternalInput").ap()
    sw_h = nc.dram_tensor("sww", [D, NE], F32, kind="ExternalInput").ap()
    swb_h = nc.dram_tensor("swbi", [1, NE], F32, kind="ExternalInput").ap()
    w1_h = nc.dram_tensor("w1", [NE, D, FF], F32, kind="ExternalInput").ap()
    b1_h = nc.dram_tensor("b1r", [NE, 128, 32], F32, kind="ExternalInput").ap()
    w2_h = nc.dram_tensor("w2", [NE, FF, D], F32, kind="ExternalInput").ap()
    b2_h = nc.dram_tensor("b2", [NE, D], F32, kind="ExternalInput").ap()
    out_h = nc.dram_tensor("out", [TW, D], F32, kind="ExternalOutput").ap()
    w1bf = nc.dram_tensor("w1bf", [NE, 8, 128, FF], BF16, kind="Internal").ap()
    w2bf = nc.dram_tensor("w2bf", [NE, 32, 128, D], BF16, kind="Internal").ap()

    ctx = ExitStack()
    with ctx:
        prog = Prog(nc, ctx)
        sb = ctx.enter_context
        EM = prog.emit

        dX = DmaStream(prog, 4)
        dW = DmaStream(prog, 1)
        dI = DmaStream(prog, 4)
        dWF = DmaStream(prog, 2)
        dWB = DmaStream(prog, 12)
        dSL = DmaStream(prog, 8)
        dO = DmaStream(prog, 2)
        sT = prog.sem("sT")
        sV = prog.sem("sV")
        sA = prog.sem("sA")
        sG = prog.sem("sG")

        cst = make_consts(nc, ctx, prog, dW)
        mTRILS = cst["masks"][:, 0, :]
        mTRIUS = cst["masks"][:, 1, :]
        mTRIUI = cst["masks"][:, 2, :]
        mEYE = cst["masks"][:, 3, :]
        mTRIUS4 = cst["masks4"][:, 0, :]
        mTRILS4 = cst["masks4"][:, 1, :]
        mEYE4 = cst["masks4"][:, 2, :]
        mTRIUI4 = cst["masks4"][:, 3, :]
        IDB = cst["ident"][:]
        su8_d = nc.inline_tensor(np.triu(np.ones((8, 8), dtype=BF), 1), "su8d").ap()
        su8 = sb(nc.sbuf_tensor("su8", [8, 8], BF16))
        dW.emit("sync", TH("dma_start", out=su8[:], in_=su8_d))
        id8_d = nc.inline_tensor(np.eye(8, dtype=BF), "id8d").ap()
        id8 = sb(nc.sbuf_tensor("id8", [8, 8], BF16))
        c_tok = dW.emit("sync", TH("dma_start", out=id8[:], in_=id8_d))

        o_accT = sb(nc.sbuf_tensor("o_accT", [128, 8, TW], BF16))
        spcr = sb(nc.sbuf_tensor("spcr", [128, 2], F32))
        spc_tok = [None]

        conv1_done = [None] * NE
        conv2_done = [None] * NE

        def gen_wconv(e):
            toks = []
            for pc in range(8):
                toks.append(dWB.emit("gpsimd", TH(
                    "dma_start", out=w1bf[e, pc],
                    in_=w1_h[e, pc * 128:(pc + 1) * 128, :])))
                yield
            conv1_done[e] = toks
            toks = []
            for pc in range(32):
                toks.append(dWB.emit("gpsimd", TH(
                    "dma_start", out=w2bf[e, pc],
                    in_=w2_h[e, pc * 128:(pc + 1) * 128, :])))
                yield
            conv2_done[e] = toks

        conv_q = []

        def pump_conv(n):
            while n > 0 and conv_q:
                try:
                    next(conv_q[0])
                    n -= 1
                except StopIteration:
                    conv_q.pop(0)

        def spacer():
            spc_tok[0] = EM("vector", TH("memset", spcr[:, 0:1], 0.0),
                            [spc_tok[0]], sV, 1)

        ctxA = ExitStack()
        sa = ctxA.enter_context

        wq = sa(nc.sbuf_tensor("wqb", [128, 8, D], BF16))
        wk = sa(nc.sbuf_tensor("wkb", [128, 8, D], BF16))
        wv = sa(nc.sbuf_tensor("wvb", [128, 8, D], BF16))
        wbg = sa(nc.sbuf_tensor("wbgb", [128, 8, 8], BF16))
        wmx = sa(nc.sbuf_tensor("wmxb", [128, 8, 4], BF16))
        cvq = sa(nc.sbuf_tensor("cvqs", [128, 8, 4], F32))
        cvk = sa(nc.sbuf_tensor("cvks", [128, 8, 4], F32))
        cvv = sa(nc.sbuf_tensor("cvvs", [128, 8, 4], F32))
        cvm = sa(nc.sbuf_tensor("cvms", [1, 4, 4], F32))
        mxb = sa(nc.sbuf_tensor("mxbs", [1, 4], F32))
        ctxw = ExitStack()
        wstgA = ctxw.enter_context(nc.sbuf_tensor("wstgA", [128, 4096], F32))

        tok = cst["tok"]
        for dst, src in ((wq, wq_h), (wk, wk_h), (wv, wv_h)):
            for pc in range(2):
                tok = dW.emit("sync", TH(
                    "dma_start",
                    out=wstgA[:].rearrange("p (a c) -> p a c", a=4),
                    in_=src[pc * 512:(pc + 1) * 512, :]
                    .rearrange("(a p) c -> p a c", p=128)), [tok])
                tok = EM("vector", TH(
                    "tensor_copy",
                    dst[:, pc * 4:(pc + 1) * 4, :].rearrange("p a c -> p (a c)"),
                    wstgA[:]), [tok], sV, 1)
        tok = dW.emit("sync", TH(
            "dma_start", out=wstgA[:, 0:64].rearrange("p (a c) -> p a c", a=8),
            in_=wbg_h), [tok])
        tok = EM("vector", TH("tensor_copy",
                              wbg[:].rearrange("p a c -> p (a c)"),
                              wstgA[:, 0:64]), [tok], sV, 1)
        tok = dW.emit("sync", TH(
            "dma_start", out=wstgA[:, 0:32].rearrange("p (a c) -> p a c", a=8),
            in_=wmx_h), [tok])
        tok = EM("vector", TH("tensor_copy",
                              wmx[:].rearrange("p a c -> p (a c)"),
                              wstgA[:, 0:32]), [tok], sV, 1)
        w_tok = tok
        for dst, src in ((cvq, cvq_h), (cvk, cvk_h), (cvv, cvv_h)):
            dW.emit("sync", TH("dma_start", out=dst[:], in_=src))
        dW.emit("sync", TH("dma_start", out=cvm[:], in_=cvm_h))
        cv_tok = dW.emit("sync", TH("dma_start", out=mxb[:], in_=mxb_h))
        ctxw.close()

        xstg = [sa(nc.sbuf_tensor(f"xstg{i}", [128, D], F32)) for i in range(2)]
        xT = sa(nc.sbuf_tensor("xT0", [128, 8, SP2], BF16))
        xbf = sa(nc.sbuf_tensor("xbf", [128, D], BF16))
        qraw = sa(nc.sbuf_tensor("qraw", [128, 2, SP2 + 3], BF16))
        kraw = sa(nc.sbuf_tensor("kraw", [128, 2, SP2 + 3], BF16))
        vraw = sa(nc.sbuf_tensor("vraw", [128, 2, SP2 + 3], BF16))
        mixraw = sa(nc.sbuf_tensor("mixraw", [1, SP2 + 3], BF16))
        hisq = sa(nc.sbuf_tensor("hisq", [128, 8, 3], BF16))
        hisk = sa(nc.sbuf_tensor("hisk", [128, 8, 3], BF16))
        hisv = sa(nc.sbuf_tensor("hisv", [128, 8, 3], BF16))
        hism = sa(nc.sbuf_tensor("hism", [1, 4, 3], F32))
        vcv = [sa(nc.sbuf_tensor(f"vcv{i}", [128, 2, SP2], BF16)) for i in range(2)]
        sqb = sa(nc.sbuf_tensor("sqb", [128, 2, SP2], BF16))
        mcv = sa(nc.sbuf_tensor("mcv", [1, 2, SP2], BF16))
        cvt_a = sa(nc.sbuf_tensor("cvt_a", [128, 2, SP2], BF16))
        cvt_b = sa(nc.sbuf_tensor("cvt_b", [128, 2, SP2], BF16))
        q3 = sa(nc.sbuf_tensor("q3", [128, 2, SP2], BF16))
        k3 = sa(nc.sbuf_tensor("k3", [128, 2, SP2], BF16))
        v3 = sa(nc.sbuf_tensor("v3", [128, 2, SP2], BF16))
        gam_row = sa(nc.sbuf_tensor("gam_row", [1, SP2], F32))
        beta_bf = sa(nc.sbuf_tensor("beta_bf", [1, SP2], BF16))
        mxs = sa(nc.sbuf_tensor("mxs", [1, SP2], F32))
        l2r = sa(nc.sbuf_tensor("l2r", [1, 2, 512], BF16))
        rr = sa(nc.sbuf_tensor("rr", [1, 2, 512], F32))
        rr_bf = sa(nc.sbuf_tensor("rr_bf", [1, 512], BF16))
        g_bf = sa(nc.sbuf_tensor("g_bf", [1, SP2], BF16))
        g_lo = sa(nc.sbuf_tensor("g_lo", [1, SP2], BF16))
        Rrk = sa(nc.sbuf_tensor("Rrk", [128, SP2], BF16))
        Rb = sa(nc.sbuf_tensor("Rb", [128, SP2], BF16))
        Rg = sa(nc.sbuf_tensor("Rg", [128, SP2], F32))
        Rg1m = sa(nc.sbuf_tensor("Rg1m", [128, SP2], BF16))
        khT = sa(nc.sbuf_tensor("khT", [128, 2, SP2], BF16))
        kbT = sa(nc.sbuf_tensor("kbT", [128, 2, SP2], BF16))
        vhT = sa(nc.sbuf_tensor("vhT", [128, 2, SP2], BF16))
        vgb = sa(nc.sbuf_tensor("vgb", [128, 2, SP2], BF16))
        ema_o = sa(nc.sbuf_tensor("ema_o", [128, 2, SP2], F32))
        ema_bf = sa(nc.sbuf_tensor("ema_bf", [128, 2, SP2], BF16))
        ema_st = sa(nc.sbuf_tensor("ema_st", [128, 8], F32))
        tokb = sa(nc.sbuf_tensor("tokb", [128, SPC2, 4 * DK], BF16))
        sc_rq = sa(nc.sbuf_tensor("sc_rq", [128, SPC2], F32))
        sc_mx = sa(nc.sbuf_tensor("sc_mx", [128, SPC2], F32))
        S_sb = sa(nc.sbuf_tensor("S_sb", [128, 4, 2, DK], BF16))
        S_f32 = sa(nc.sbuf_tensor("S_f32", [128, 4, 2, DK], F32))
        nt_b = sa(nc.sbuf_tensor("nt_b", [128, SPC2, 128], BF16))
        nn_b = sa(nc.sbuf_tensor("nn_b", [128, SPC2, 128], BF16))
        xg = [sa(nc.sbuf_tensor(f"xgv{i}", [128, SPC2, 128], BF16)) for i in range(2)]
        xgt = [sa(nc.sbuf_tensor(f"xgt{i}", [128, SPC2, 128], BF16)) for i in range(2)]
        ccb = [sa(nc.sbuf_tensor(f"ccv{i}", [128, SPC2, 128], BF16)) for i in range(2)]
        pt_b = sa(nc.sbuf_tensor("pt_b", [128, SPC2, 128], BF16))
        u_b = sa(nc.sbuf_tensor("u_b", [128, SPC2, DK], BF16))
        wt_b = sa(nc.sbuf_tensor("wt_b", [128, SPC2, 2, 128], BF16))
        uadj = sa(nc.sbuf_tensor("uadj", [128, DK], BF16))
        o1 = sa(nc.sbuf_tensor("o1b", [128, DK], F32))
        tgg = sa(nc.sbuf_tensor("tgg", [128, DK], F32))
        ogg = sa(nc.sbuf_tensor("ogg", [128, DK], F32))
        onb = sa(nc.sbuf_tensor("onb", [128, DK], BF16))
        stat = sa(nc.sbuf_tensor("stat", [128, 4], F32))

        S_dl = sa(nc.psum_tensor("S_dl", [128, 512], F32))
        p_tr = sa(nc.psum_tensor("p_tr", [128, 1024], BF16))
        p_a = sa(nc.psum_tensor("p_a", [128, 512], F32))
        p_b = sa(nc.psum_tensor("p_b", [128, 512], F32))
        p_c = sa(nc.psum_tensor("p_c", [128, 512], F32))
        p_d = sa(nc.psum_tensor("p_d", [128, 512], F32))
        p_sm = sa(nc.psum_tensor("p_sm", [2, 1024], F32))

        z0 = EM("vector", TH("memset",
                             hisq[:].rearrange("p a c -> p (a c)"), 0.0),
                [w_tok], sV, 1)
        EM("vector", TH("memset", spcr[:, 0:2], 0.0), (), sV, 1)
        EM("vector", TH("memset", hisk[:].rearrange("p a c -> p (a c)"), 0.0),
           (), sV, 1)
        EM("vector", TH("memset", hisv[:].rearrange("p a c -> p (a c)"), 0.0),
           (), sV, 1)
        EM("vector", TH("memset", hism[0:1].rearrange("p a c -> p (a c)"), 0.0),
           (), sV, 1)
        EM("vector", TH("memset", ema_st[:, :], 0.0), (), sV, 1)
        EM("vector", TH("memset",
                        S_sb[:].rearrange("p a b c -> p (a b c)"), 0.0),
           (), sV, 1)
        EM("vector", TH("memset",
                        S_f32[:].rearrange("p a b c -> p (a b c)"), 0.0),
           (), sV, 1)

        bar = prog.barrier()
        xstg_free = [None, None]
        prev_sv1 = [None]
        prev_gev = [None]
        prev_hs = {"q": None, "k": None, "v": None}
        sadd_tok = [None]
        ot1_prev = [None]
        prev_l2 = [None]
        prev_pd = [None]

        for st in range(NST2):
            off = st * SP2
            own = st >= NST2 - NOWN
            woff = off - (L - TW)
            if st < NE:
                conv_q.append(gen_wconv(st))

            xt_evt = None
            for t in range(SP2 // 128):
                pp = t % 2
                tin = dX.emit("sync", TH(
                    "dma_start", out=xstg[pp][:],
                    in_=x_h[off + t * 128: off + (t + 1) * 128, :]),
                    [xstg_free[pp],
                     w_tok if (st == 0 and t < 2) else None])
                tc = EM("vector", TH("tensor_copy", xbf[:], xstg[pp][:]),
                        [tin, xt_evt, bar if (st == 0 and t == 0) else None],
                        sV, 1)
                xstg_free[pp] = tc
                trs = None
                for ds in range(8):
                    trs = EM("tensor", TH("transpose",
                                          p_tr[:, ds * 128:(ds + 1) * 128],
                                          xbf[:, ds * 128:(ds + 1) * 128], IDB),
                             [tc, cst["tok"] if st == 0 and t == 0 else None],
                             sT, 1)
                xt_evt = EM("scalar", TH("copy",
                                          xT[:, :, t * 128:(t + 1) * 128],
                                          p_tr[:].rearrange("p (a c) -> p a c", a=8)),
                             [trs], sA, 1)
            xt_done = xt_evt

            for h in range(4):
                hsl = slice(2 * h, 2 * h + 2)
                pump_conv(3)
                hq = hk = hv = hm = None
                if own:
                    hq = EM("vector", TH("tensor_copy", qraw[:, :, 0:3],
                                         hisq[:, hsl, :]), [bar], sV, 1)
                hk = EM("vector", TH("tensor_copy", kraw[:, :, 0:3],
                                     hisk[:, hsl, :]), [bar], sV, 1)
                hv = EM("vector", TH("tensor_copy", vraw[:, :, 0:3],
                                     hisv[:, hsl, :]), (), sV, 1)
                if own:
                    hm = EM("vector", TH("tensor_copy", mixraw[0:1, 0:3],
                                         hism[0:1, h, :]), (), sV, 1)

                def proj(dstraw, wgt, hist_tok, war_tok=None):
                    ev = None
                    for hf in range(2):
                        pp = p_a if hf == 0 else p_b
                        mmt = None
                        for ds in range(8):
                            mmt = EM("tensor", TH(
                                "matmul", pp[:],
                                wgt[:, ds, h * 256 + hf * 128:
                                    h * 256 + (hf + 1) * 128],
                                xT[:, ds, :],
                                start=(ds == 0), stop=(ds == 7)),
                                [xt_done if ds == 0 else None,
                                 w_tok if ds == 0 else None,
                                 ev if ds == 0 else None], sT, 1)
                        ev = EM("scalar", TH(
                            "copy", dstraw[:, hf, 3:3 + SP2],
                            pp[:]), [mmt, hist_tok, war_tok if hf == 0 else None],
                            sA, 1)
                    return ev
                pk = proj(kraw, wk, hk, prev_hs["k"])
                pv = proj(vraw, wv, hv, prev_hs["v"])
                pq = proj(qraw, wq, hq, prev_hs["q"]) if own else None

                bgmm = None
                for ds in range(8):
                    bgmm = EM("tensor", TH(
                        "matmul", p_sm[0:1, 0:512],
                        wbg[:, ds, 2 * h:2 * h + 1], xT[:, ds, :],
                        start=(ds == 0), stop=(ds == 7)),
                        [xt_done, prev_l2[0] if ds == 0 else None,
                         prev_gev[0] if ds == 0 else None], sT, 1)
                for ds in range(8):
                    bgmm = EM("tensor", TH(
                        "matmul", p_sm[0:1, 512:1024],
                        wbg[:, ds, 2 * h + 1:2 * h + 2], xT[:, ds, :],
                        start=(ds == 0), stop=(ds == 7)), (), sT, 1)
                bg_ev = EM("scalar", TH("activation", out=beta_bf[0:1, :],
                                        in_=p_sm[0:1, 0:512], func=AF.Sigmoid),
                           [bgmm], sA, 1)
                g_ev = EM("scalar", TH("activation", out=gam_row[0:1, :],
                                       in_=p_sm[0:1, 512:1024], func=AF.Sigmoid),
                          (), sA, 1)
                prev_gev[0] = g_ev
                if own:
                    mxmm = None
                    for ds in range(8):
                        mxmm = EM("tensor", TH(
                            "matmul", p_sm[0:1, 0:512],
                            wmx[:, ds, h:h + 1], xT[:, ds, :],
                            start=(ds == 0), stop=(ds == 7)), [bg_ev], sT, 1)
                    mx_ev = EM("scalar", TH("copy", mixraw[0:1, 3:3 + SP2],
                                            p_sm[0:1, 0:512]), [mxmm, hm], sA, 1)

                def conv(raw, cw, ev_tok, sa_=None, sb_=None):
                    ca = sa_ if sa_ is not None else cvt_a
                    cb = sb_ if sb_ is not None else cvt_b
                    t_last = None
                    for hf in range(2):
                        t0 = EM("vector", TH(
                            "tensor_scalar", out=ca[:, hf, :],
                            in0=raw[:, hf, 0:SP2],
                            scalar1=cw[:, 2 * h + hf, 0:1], scalar2=None,
                            op0=ALU.mult), [ev_tok, cv_tok], sV, 1)
                        t1 = EM("vector", TH(
                            "scalar_tensor_tensor", out=cb[:, hf, :],
                            in0=raw[:, hf, 1:1 + SP2],
                            scalar=cw[:, 2 * h + hf, 1:2],
                            in1=ca[:, hf, :], op0=ALU.mult, op1=ALU.add),
                            [t0], sV, 1)
                        t2 = EM("vector", TH(
                            "scalar_tensor_tensor", out=ca[:, hf, :],
                            in0=raw[:, hf, 2:2 + SP2],
                            scalar=cw[:, 2 * h + hf, 2:3],
                            in1=cb[:, hf, :], op0=ALU.mult, op1=ALU.add),
                            [t1], sV, 1)
                        t_last = EM("vector", TH(
                            "scalar_tensor_tensor", out=cb[:, hf, :],
                            in0=raw[:, hf, 3:3 + SP2],
                            scalar=cw[:, 2 * h + hf, 3:4],
                            in1=ca[:, hf, :], op0=ALU.mult, op1=ALU.add),
                            [t2], sV, 1)
                    return t_last

                hsk = EM("gpsimd", TH("tensor_copy", hisk[:, hsl, :],
                                      kraw[:, :, SP2:SP2 + 3]), [pk], sG, 1)
                prev_hs["k"] = hsk
                hsv = EM("gpsimd", TH("tensor_copy", hisv[:, hsl, :],
                                      vraw[:, :, SP2:SP2 + 3]), [pv], sG, 1)
                prev_hs["v"] = hsv
                if own:
                    hsq2 = EM("gpsimd", TH("tensor_copy", hisq[:, hsl, :],
                                           qraw[:, :, SP2:SP2 + 3]), [pq], sG, 1)
                    prev_hs["q"] = hsq2

                cvk_t = conv(kraw, cvk, pk)
                sk1 = EM("scalar", TH("activation",
                                      out=cvt_a[:].rearrange("p a c -> p (a c)"),
                                      in_=cvt_b[:].rearrange("p a c -> p (a c)"),
                                      func=AF.Silu), [cvk_t], sA, 1)
                sk2 = EM("scalar", TH("activation",
                                      out=k3[:].rearrange("p a c -> p (a c)"),
                                      in_=cvt_a[:].rearrange("p a c -> p (a c)"),
                                      func=AF.Silu), [sk1], sA, 1)
                cvv_t = conv(vraw, cvv, [pv, prev_sv1[0]],
                             sa_=vcv[0], sb_=vcv[1])
                sv1 = EM("scalar", TH("activation",
                                      out=v3[:].rearrange("p a c -> p (a c)"),
                                      in_=vcv[1][:].rearrange("p a c -> p (a c)"),
                                      func=AF.Silu), [cvv_t], sA, 1)
                prev_sv1[0] = sv1
                if own:
                    cvq_t = conv(qraw, cvq, sk2)
                    sq1 = EM("scalar", TH("activation",
                                          out=cvt_a[:].rearrange("p a c -> p (a c)"),
                                          in_=cvt_b[:].rearrange("p a c -> p (a c)"),
                                          func=AF.Silu), [cvq_t], sA, 1)
                    sq2t = EM("scalar", TH("activation",
                                           out=q3[:].rearrange("p a c -> p (a c)"),
                                           in_=cvt_a[:].rearrange("p a c -> p (a c)"),
                                           func=AF.Silu), [sq1], sA, 1)
                    m0 = EM("vector", TH("tensor_scalar", out=mcv[0:1, 0, :],
                                         in0=mixraw[0:1, 0:SP2],
                                         scalar1=cvm[0:1, h, 0:1],
                                         scalar2=None, op0=ALU.mult),
                            [mx_ev], sV, 1)
                    m1 = EM("vector", TH("scalar_tensor_tensor",
                                         out=mcv[0:1, 1, :],
                                         in0=mixraw[0:1, 1:1 + SP2],
                                         scalar=cvm[0:1, h, 1:2],
                                         in1=mcv[0:1, 0, :],
                                         op0=ALU.mult, op1=ALU.add), [m0], sV, 1)
                    m2 = EM("vector", TH("scalar_tensor_tensor",
                                         out=mcv[0:1, 0, :],
                                         in0=mixraw[0:1, 2:2 + SP2],
                                         scalar=cvm[0:1, h, 2:3],
                                         in1=mcv[0:1, 1, :],
                                         op0=ALU.mult, op1=ALU.add), [m1], sV, 1)
                    m3 = EM("vector", TH("scalar_tensor_tensor",
                                         out=mcv[0:1, 1, :],
                                         in0=mixraw[0:1, 3:3 + SP2],
                                         scalar=cvm[0:1, h, 3:4],
                                         in1=mcv[0:1, 0, :],
                                         op0=ALU.mult, op1=ALU.add), [m2], sV, 1)
                    hsm = EM("gpsimd", TH("tensor_copy", hism[0:1, h, :],
                                          mixraw[0:1, SP2:SP2 + 3]), [m3], sG, 1)
                    m4 = EM("scalar", TH("activation", out=mcv[0:1, 0, :],
                                         in_=mcv[0:1, 1, :], func=AF.Silu),
                            [m3], sA, 1)
                    mx_t = EM("scalar", TH("activation", out=mxs[0:1, :],
                                           in_=mcv[0:1, 0, :], func=AF.Sigmoid,
                                           bias=mxb[0:1, h:h + 1], scale=1.0),
                             [m4], sA, 1)

                pump_conv(3)
                sq = sqb
                sqk = EM("scalar", TH("activation",
                                      out=sq[:].rearrange("p a c -> p (a c)"),
                                      in_=k3[:].rearrange("p a c -> p (a c)"),
                                      func=AF.Square),
                         [sk2,
                          bar if (st > 0 or h > 0) else None], sA, 1)
                EM("tensor", TH("matmul", p_sm[0:1, 0:512],
                                cst["ones_col"][:], sq[:, 0, :],
                                start=True, stop=False),
                   [sqk, mx_ev if own else g_ev], sT, 1)
                ssk = EM("tensor", TH("matmul", p_sm[0:1, 0:512],
                                      cst["ones_col"][:], sq[:, 1, :],
                                      start=False, stop=True), (), sT, 1)
                if own:
                    sqq = EM("scalar", TH("activation",
                                          out=sq[:].rearrange("p a c -> p (a c)"),
                                          in_=q3[:].rearrange("p a c -> p (a c)"),
                                          func=AF.Square), [sq2t, ssk], sA, 1)
                    EM("tensor", TH("matmul", p_d[0:1, 0:512],
                                    cst["ones_col"][:], sq[:, 0, :],
                                    start=True, stop=False),
                       [sqq, ot1_prev[0], prev_pd[0]], sT, 1)
                    ssq1 = EM("tensor", TH("matmul", p_d[0:1, 0:512],
                                           cst["ones_col"][:], sq[:, 1, :],
                                           start=False, stop=True), (), sT, 1)
                srk = EM("scalar", TH("activation", out=l2r[0:1, 0, :],
                                      in_=p_sm[0:1, 0:512], func=AF.Sqrt,
                                      bias=cst["eps"][0:1, 0:1], scale=1.0),
                         [ssk], sA, 1)
                prev_l2[0] = srk
                if own:
                    srq = EM("scalar", TH("activation", out=l2r[0:1, 1, :],
                                          in_=p_d[0:1, 0:512], func=AF.Sqrt,
                                          bias=cst["eps"][0:1, 0:1], scale=1.0),
                             [ssq1], sA, 1)
                    prev_l2[0] = srk
                if not own:
                    srq = None
                rcp = EM("vector", TH("reciprocal",
                                      rr[0:1].rearrange("p a c -> p (a c)")
                                      if own else rr[0:1, 0, :],
                                      l2r[0:1].rearrange("p a c -> p (a c)")
                                      if own else l2r[0:1, 0, :]),
                         [srk, srq if own else None], sV, 1)
                spacer()
                rkb = EM("vector", TH("tensor_copy", rr_bf[0:1, :],
                                      rr[0:1, 0, :]), [rcp], sV, 1)
                gb = EM("vector", TH("tensor_copy", g_bf[0:1, :],
                                     gam_row[0:1, :]), [g_ev], sV, 1)
                spacer()
                gl = EM("vector", TH("tensor_tensor", g_lo[0:1, :],
                                     gam_row[0:1, :], g_bf[0:1, :],
                                     ALU.subtract), [gb], sV, 1)
                bgb = EM("vector", TH("tensor_copy", spcr[:, 1:2],
                                      spcr[:, 1:2]), [gl, bg_ev], sV, 1)

                if st == NST2 - NOWN - 1:
                    qt = None
                    for hf in range(2):
                        for ds in range(8):
                            qt = EM("tensor", TH(
                                "matmul", p_d[0:128, 16 + hf * 3:19 + hf * 3],
                                wq[:, ds, h * 256 + hf * 128:
                                   h * 256 + (hf + 1) * 128],
                                xT[:, ds, SP2 - 3:SP2],
                                start=(ds == 0), stop=(ds == 7)),
                                [prev_pd[0] if (hf == 0 and ds == 0)
                                 else None], sT, 1)
                    tq_fill = EM("scalar", TH("copy", hisq[:, hsl, :].rearrange(
                        "p a c -> p (a c)"), p_d[0:128, 16:22]), [qt], sA, 1)
                    mt2 = None
                    for ds in range(8):
                        mt2 = EM("tensor", TH(
                            "matmul", p_d[0:1, 22:25],
                            wmx[:, ds, h:h + 1], xT[:, ds, SP2 - 3:SP2],
                            start=(ds == 0), stop=(ds == 7)),
                            [tq_fill if ds == 0 else None], sT, 1)
                    hq_fill = EM("scalar", TH("copy", hism[0:1, h, :],
                                              p_d[0:1, 22:25]), [mt2], sA, 1)
                    prev_pd[0] = hq_fill

                t1 = EM("tensor", TH("matmul", p_c[:], cst["ones_row"][0:1, 0:128],
                                     rr_bf[0:1, :], start=True, stop=True),
                        [rkb], sT, 1)
                ev_rrk = EM("scalar", TH("copy", Rrk[:, :], p_c[:]), [t1], sA, 1)
                t1 = EM("tensor", TH("matmul", p_d[:], cst["ones_row"][0:1, 0:128],
                                     beta_bf[0:1, :], start=True, stop=True),
                        [bgb, prev_pd[0], srq if own else None], sT, 1)
                ev_rb = EM("scalar", TH("copy", Rb[:, :], p_d[:]), [t1], sA, 1)
                prev_pd[0] = ev_rb
                EM("tensor", TH("matmul", p_c[:], cst["ones_row"][0:1, 0:128],
                                g_bf[0:1, :], start=True, stop=False),
                   [bgb, ev_rrk], sT, 1)
                t1 = EM("tensor", TH("matmul", p_c[:], cst["ones_row"][0:1, 0:128],
                                     g_lo[0:1, :], start=False, stop=True),
                        (), sT, 1)
                ev_rg = EM("scalar", TH("copy", Rg[:, :], p_c[:]), [t1], sA, 1)
                rg1 = EM("vector", TH("tensor_scalar", out=Rg1m[:, :],
                                      in0=Rg[:, :], scalar1=-1.0, scalar2=1.0,
                                      op0=ALU.mult, op1=ALU.add), [ev_rg], sV, 1)

                nk = nb = nv = ng = None
                for hf in range(2):
                    nk = EM("vector", TH("tensor_tensor", khT[:, hf, :],
                                         k3[:, hf, :], Rrk[:, :], ALU.mult),
                            [ev_rrk, sk2], sV, 1)
                    nb = EM("vector", TH("tensor_tensor", kbT[:, hf, :],
                                         khT[:, hf, :], Rb[:, :], ALU.mult),
                            [ev_rb, nk], sV, 1)
                    nv = EM("gpsimd", TH("tensor_tensor", vhT[:, hf, :],
                                         v3[:, hf, :], Rb[:, :], ALU.mult),
                            [sv1 if hf == 0 else None,
                             ev_rb if hf == 0 else None], sG, 1)
                    ng = EM("gpsimd", TH("tensor_tensor", vgb[:, hf, :],
                                         v3[:, hf, :], Rg1m[:, :], ALU.mult),
                            [rg1 if hf == 0 else None], sG, 1)

                em0 = EM("vector", TH("tensor_tensor_scan", out=ema_o[:, 0, :],
                                      data0=Rg[:, :], data1=vgb[:, 0, :],
                                      initial=ema_st[:, 2 * h:2 * h + 1],
                                      op0=ALU.mult, op1=ALU.add),
                         [ng, ev_rg], sV, 1)
                em1 = EM("vector", TH("tensor_tensor_scan", out=ema_o[:, 1, :],
                                      data0=Rg[:, :], data1=vgb[:, 1, :],
                                      initial=ema_st[:, 2 * h + 1:2 * h + 2],
                                      op0=ALU.mult, op1=ALU.add), (), sV, 1)
                es0 = EM("vector", TH("tensor_copy", ema_st[:, 2 * h:2 * h + 1],
                                      ema_o[:, 0, SP2 - 1:SP2]), [em1], sV, 1)
                es1 = EM("vector", TH("tensor_copy",
                                      ema_st[:, 2 * h + 1:2 * h + 2],
                                      ema_o[:, 1, SP2 - 1:SP2]), (), sV, 1)
                if own:
                    emb = EM("vector", TH("tensor_copy",
                                          ema_bf[:].rearrange("p a c -> p (a c)"),
                                          ema_o[:].rearrange("p a c -> p (a c)")),
                             [es1], sV, 1)

                if own:
                    sc_mm = None
                    for c in range(SPC2):
                        sc_mm = EM("tensor", TH(
                            "transpose", p_c[:, c:c + 1],
                            rr[0:1, 1, c * 128:(c + 1) * 128],
                            mEYE[0:1, 0:1]),
                            [rcp if c == 0 else None,
                             ev_rg if c == 0 else None], sT, 1)
                    for c in range(SPC2):
                        sc_mm = EM("tensor", TH(
                            "transpose", p_c[:, 8 + c:9 + c],
                            mxs[0:1, c * 128:(c + 1) * 128],
                            mEYE[0:1, 0:1]),
                            [mx_t if c == 0 else None], sT, 1)
                    scv1 = EM("vector", TH("tensor_copy", sc_rq[:, :],
                                           p_c[:, 0:SPC2]), [sc_mm], sV, 1)
                    scv2 = EM("vector", TH("tensor_copy", sc_mx[:, :],
                                           p_c[:, 8:8 + SPC2]), (), sV, 1)
                else:
                    scv1 = scv2 = es1

                tok_evs = []
                nsrc = 4 if own else 3
                for c in range(SPC2):
                    cw = slice(c * 128, (c + 1) * 128)
                    trs = None
                    srcs = (khT, kbT, vhT, ema_bf)
                    deps = (nk, nb, nv, emb if own else None)
                    for j in range(nsrc):
                        for hf in range(2):
                            trs = EM("tensor", TH(
                                "transpose",
                                p_tr[:, j * 256 + hf * 128:
                                     j * 256 + (hf + 1) * 128],
                                srcs[j][:, hf, cw], IDB),
                                [deps[j] if hf == 0 else None,
                                 tok_evs[c - 1] if (j == 0 and hf == 0 and c > 0)
                                 else None,
                                 scv2 if (j == 0 and hf == 0 and c == 0)
                                 else None], sT, 1)
                    ev = EM("scalar", TH("copy",
                                         tokb[:, c, 0:nsrc * 256],
                                         p_tr[:, 0:nsrc * 256]), [trs], sA, 1)
                    tok_evs.append(ev)

                pump_conv(3)
                nt_toks = [None] * SPC2
                nn_toks = [None] * SPC2
                mm_last = None
                for c in range(SPC2):
                    cw = slice(c * 128, (c + 1) * 128)
                    EM("tensor", TH("matmul", p_a[:, c * 128:(c + 1) * 128],
                                    khT[:, 0, cw], kbT[:, 0, cw],
                                    start=True, stop=False),
                       [nb, ssq1 if (own and c == 0) else
                        (ssk if c == 0 else None)], sT, 1)
                    EM("tensor", TH("matmul", p_a[:, c * 128:(c + 1) * 128],
                                    khT[:, 1, cw], kbT[:, 1, cw],
                                    start=False, stop=True), (), sT, 1)
                    EM("tensor", TH("matmul", p_b[:, c * 128:(c + 1) * 128],
                                    kbT[:, 0, cw], khT[:, 0, cw],
                                    start=True, stop=False), (), sT, 1)
                    mm_last = EM("tensor", TH("matmul",
                                              p_b[:, c * 128:(c + 1) * 128],
                                              kbT[:, 1, cw], khT[:, 1, cw],
                                              start=False, stop=True), (), sT, 1)
                ntt = EM("vector", TH(
                    "tensor_tensor", nt_b[:].rearrange("p a c -> p (a c)"),
                    p_a[:, 0:512], mTRIUS4, ALU.mult), [mm_last], sV, 1)
                nnt = EM("vector", TH(
                    "tensor_tensor", nn_b[:].rearrange("p a c -> p (a c)"),
                    p_b[:, 0:512], mTRILS4, ALU.mult), [mm_last], sV, 1)
                nt_toks = [ntt] * SPC2
                nn_toks = [nnt] * SPC2
                cct = EM("vector", TH(
                    "tensor_tensor", ccb[0][:].rearrange("p a c -> p (a c)"),
                    mEYE4, nt_b[:].rearrange("p a c -> p (a c)"),
                    ALU.subtract), [ntt], sV, 1)
                cc_tok = [cct] * SPC2

                x_cur, xt_cur = nt_b, nn_b
                x_tok, xt_tok = nt_toks, nn_toks
                cgen = 0
                prev_ncc_last = scv2
                for lvl in range(6):
                    xng = xg[lvl % 2]
                    xtg = xgt[lvl % 2]
                    nx_tok = [None] * SPC2
                    nxt_tok = [None] * SPC2
                    ncc_tok = [None] * SPC2
                    mm_last = None
                    for c in range(SPC2):
                        guard = (nn_toks[SPC2 - 1] if lvl == 0
                                 else xt_tok[SPC2 - 1]) if c == 0 else None
                        if lvl < 5:
                            EM("tensor", TH(
                                "matmul", p_a[:, c * 128:(c + 1) * 128],
                                xt_cur[:, c, :], x_cur[:, c, :],
                                start=True, stop=True),
                                [x_tok[c], xt_tok[c], guard], sT, 1)
                        mm_last = EM("tensor", TH(
                            "matmul", p_b[:, c * 128:(c + 1) * 128],
                            x_cur[:, c, :], xt_cur[:, c, :],
                            start=True, stop=True),
                            [x_tok[c], xt_tok[c], guard], sT, 1)
                    if lvl < 5:
                        nxa = EM("scalar", TH(
                            "copy", xng[:].rearrange("p a c -> p (a c)"),
                            p_a[:, 0:512]), [mm_last], sA, 1)
                        nx_tok = [nxa] * SPC2
                    nxta = EM("scalar", TH(
                        "copy", xtg[:].rearrange("p a c -> p (a c)"),
                        p_b[:, 0:512]), [mm_last], sA, 1)
                    nxt_tok = [nxta] * SPC2
                    mm_cu_last = None
                    for c in range(SPC2):
                        mm_cu_last = EM("tensor", TH(
                            "matmul", p_c[:, c * 128:(c + 1) * 128],
                            xtg[:, c, :], ccb[cgen][:, c, :],
                            start=True, stop=True),
                            [nxt_tok[c], cc_tok[c],
                             prev_ncc_last if c == 0 else None], sT, 1)
                    ncca = EM("vector", TH(
                        "tensor_tensor",
                        ccb[1 - cgen][:].rearrange("p a c -> p (a c)"),
                        p_c[:, 0:512],
                        ccb[cgen][:].rearrange("p a c -> p (a c)"),
                        ALU.add), [mm_cu_last], sV, 1)
                    ncc_tok = [ncca] * SPC2
                    prev_ncc_last = ncca
                    x_cur, xt_cur = xng, xtg
                    x_tok, xt_tok = nx_tok, nxt_tok
                    cc_tok = ncc_tok
                    cgen = 1 - cgen
                CCF = ccb[cgen]
                cc_all = cc_tok[SPC2 - 1]

                u_toks = [None] * SPC2
                wt_toks = [None] * SPC2
                pt_toks = [None] * SPC2
                for pc in range(SPC2 // 2):
                    mm_u = mm_w = mm_p = None
                    for sl in range(2):
                        c = 2 * pc + sl
                        cw = slice(c * 128, (c + 1) * 128)
                        mm_u = EM("tensor", TH(
                            "matmul", p_a[:, sl * 256:(sl + 1) * 256],
                            CCF[:, c, :], tokb[:, c, 512:768],
                            start=True, stop=True),
                            [cc_all if sl == 0 else None,
                             tok_evs[c],
                             u_toks[2 * pc - 1] if (pc >= 1 and sl == 0)
                             else None], sT, 1)
                        for hf in range(2):
                            mm_w = EM("tensor", TH(
                                "matmul",
                                p_b[:, sl * 256 + hf * 128:
                                    sl * 256 + (hf + 1) * 128],
                                tokb[:, c, 256 + hf * 128:256 + (hf + 1) * 128],
                                CCF[:, c, :], start=True, stop=True),
                                [wt_toks[2 * pc - 1]
                                 if (pc >= 1 and sl == 0 and hf == 0)
                                 else None], sT, 1)
                        if own:
                            EM("tensor", TH(
                                "matmul", p_c[:, sl * 128:(sl + 1) * 128],
                                khT[:, 0, cw], q3[:, 0, cw],
                                start=True, stop=False),
                                [sq2t if sl == 0 else None,
                                 pt_toks[2 * pc - 1] if (pc >= 1 and sl == 0)
                                 else None,
                                 prev_ncc_last if (pc == 0 and sl == 0)
                                 else None], sT, 1)
                            mm_p = EM("tensor", TH(
                                "matmul", p_c[:, sl * 128:(sl + 1) * 128],
                                khT[:, 1, cw], q3[:, 1, cw],
                                start=False, stop=True), (), sT, 1)
                    ut = EM("scalar", TH(
                        "copy", u_b[:, 2 * pc:2 * pc + 2, :]
                        .rearrange("p a c -> p (a c)"),
                        p_a[:, 0:512]), [mm_u], sA, 1)
                    u_toks[2 * pc] = u_toks[2 * pc + 1] = ut
                    wtt = EM("scalar", TH(
                        "copy",
                        wt_b[:, 2 * pc:2 * pc + 2, :, :]
                        .rearrange("p a b c -> p (a b c)"),
                        p_b[:, 0:512]), [mm_w], sA, 1)
                    wt_toks[2 * pc] = wt_toks[2 * pc + 1] = wtt
                    if own:
                        ptt = EM("vector", TH(
                            "tensor_tensor",
                            pt_b[:, 2 * pc:2 * pc + 2, :]
                            .rearrange("p a c -> p (a c)"),
                            p_c[:, 0:256], mTRIUI4[:, 0:256], ALU.mult),
                            [mm_p], sV, 1)
                        pt_toks[2 * pc] = pt_toks[2 * pc + 1] = ptt

                pump_conv(3)
                ssb_tok = None
                for c in range(SPC2):
                    cw = slice(c * 128, (c + 1) * 128)
                    EM("tensor", TH("matmul", p_c[:, 0:256], wt_b[:, c, 0, :],
                                    S_sb[:, h, 0, :], start=True, stop=False),
                       [wt_toks[c], ssb_tok,
                        (pt_toks[SPC2 - 1] if own else prev_ncc_last)
                        if c == 0 else None], sT, 1)
                    ws1 = EM("tensor", TH("matmul", p_c[:, 0:256],
                                          wt_b[:, c, 1, :], S_sb[:, h, 1, :],
                                          start=False, stop=True), (), sT, 1)
                    ua = EM("vector", TH("tensor_tensor", uadj[:, :],
                                         u_b[:, c, :], p_c[:, 0:256],
                                         ALU.subtract), [ws1, u_toks[c]], sV, 1)
                    if own:
                        EM("tensor", TH("matmul", p_d[:, 0:256], q3[:, 0, cw],
                                        S_sb[:, h, 0, :],
                                        start=True, stop=False),
                           [ua, ev_rg if c == 0 else None, ot1_prev[0]], sT, 1)
                        EM("tensor", TH("matmul", p_d[:, 0:256], q3[:, 1, cw],
                                        S_sb[:, h, 1, :],
                                        start=False, stop=False), (), sT, 1)
                        om = EM("tensor", TH("matmul", p_d[:, 0:256],
                                             pt_b[:, c, :], uadj[:, :],
                                             start=False, stop=True),
                                [pt_toks[c]], sT, 1)
                    EM("tensor", TH("matmul", S_dl[:, 0:256],
                                    tokb[:, c, 0:128], uadj[:, :],
                                    start=True, stop=False),
                       [sadd_tok[0], ua], sT, 1)
                    su = EM("tensor", TH("matmul", S_dl[:, 256:512],
                                         tokb[:, c, 128:256], uadj[:, :],
                                         start=False, stop=True), (), sT, 1)
                    sadd_tok[0] = EM("vector", TH(
                        "tensor_tensor",
                        S_f32[:, h].rearrange("p a c -> p (a c)"),
                        S_f32[:, h].rearrange("p a c -> p (a c)"),
                        S_dl[:, :], ALU.add), [su], sV, 1)
                    ssb_tok = EM("scalar", TH(
                        "copy", S_sb[:, h].rearrange("p a c -> p (a c)"),
                        S_f32[:, h].rearrange("p a c -> p (a c)")),
                        [sadd_tok[0]], sA, 1)
                    if not own:
                        continue
                    ot1 = EM("vector", TH("tensor_scalar", out=o1[:, :],
                                          in0=p_d[:, 0:256],
                                          scalar1=sc_rq[:, c:c + 1],
                                          scalar2=None, op0=ALU.mult),
                             [om, scv1], sV, 1)
                    ot1_prev[0] = ot1
                    ot2 = EM("vector", TH("tensor_tensor", tgg[:, :],
                                          tokb[:, c, 768:1024], o1[:, :],
                                          ALU.subtract), [ot1], sV, 1)
                    ot3 = EM("vector", TH("scalar_tensor_tensor", out=ogg[:, :],
                                          in0=tgg[:, :],
                                          scalar=sc_mx[:, c:c + 1],
                                          in1=o1[:, :], op0=ALU.mult,
                                          op1=ALU.add), [ot2, scv2], sV, 1)
                    sq2e = EM("scalar", TH("activation", out=tgg[:, :],
                                           in_=ogg[:, :], func=AF.Square),
                              [ot3], sA, 1)
                    rs = EM("vector", TH("reduce_sum", stat[:, 0:1], tgg[:, :],
                                         axis=AX.X), [sq2e], sV, 1)
                    stt = EM("scalar", TH("activation", out=stat[:, 1:2],
                                          in_=stat[:, 0:1], func=AF.Sqrt,
                                          bias=cst["eps"][:, 1:2],
                                          scale=1.0 / DK), [rs], sA, 1)
                    rc = EM("vector", TH("reciprocal", stat[:, 2:3],
                                         stat[:, 1:2]), [stt], sV, 1)
                    spacer()
                    onb_t = EM("vector", TH("tensor_scalar", out=onb[:, :],
                                            in0=ogg[:, :], scalar1=stat[:, 2:3],
                                            scalar2=None, op0=ALU.mult),
                               [rc], sV, 1)
                    EM("tensor", TH("transpose", p_tr[:, 0:128],
                                    onb[:, 0:128], IDB), [onb_t], sT, 1)
                    tr1 = EM("tensor", TH("transpose", p_tr[:, 128:256],
                                          onb[:, 128:256], IDB), (), sT, 1)
                    EM("vector", TH(
                        "tensor_copy",
                        o_accT[:, 2 * h:2 * h + 2, woff + cw.start:
                               woff + cw.stop],
                        p_tr[:, 0:256].rearrange("p (a c) -> p a c", a=2)),
                        [tr1], sV, 1)

            pump_conv(1 << 30) if st == NST2 - 1 else None
            bar = prog.barrier()

        ph_bar = prog.barrier(("tensor", "vector", "scalar", "gpsimd"))
        ph_gate = [ph_bar] + dX.total_tok() + dW.total_tok()
        ctxA.close()

        sbB = ctx.enter_context
        x1T = sbB(nc.sbuf_tensor("x1T", [128, 8, TW], BF16))
        out_acc = sbB(nc.sbuf_tensor("out_acc", [128, NT, D], BF16))
        maskp = sbB(nc.sbuf_tensor("maskp", [128, NT, NE], F32))
        lgs_all = sbB(nc.sbuf_tensor("lgs_all", [128, NT, NE], F32))
        b1t = sbB(nc.sbuf_tensor("b1t", [128, 32], F32))
        w1sl = [sbB(nc.sbuf_tensor(f"w1sl{i}", [128, 8, 256], BF16)) for i in range(2)]
        rmsc = sbB(nc.sbuf_tensor("rmscs", [128, 8], F32))
        swb = sbB(nc.sbuf_tensor("swwb", [128, 8, NE], BF16))
        swbb = sbB(nc.sbuf_tensor("swbbs", [1, NE], BF16))
        b2b = sbB(nc.sbuf_tensor("b2bs", [8, D], BF16))
        xt_s = [sbB(nc.sbuf_tensor(f"xts{i}", [128, D], F32)) for i in range(2)]
        ysq = sbB(nc.sbuf_tensor("ysq", [128, D], F32))
        lst = sbB(nc.sbuf_tensor("lst", [128, 8], F32))
        expo = sbB(nc.sbuf_tensor("expo", [128, NE], F32))
        eqb = sbB(nc.sbuf_tensor("eqb", [128, NE], BF16))
        eqT = sbB(nc.sbuf_tensor("eqT", [8, 128], BF16))
        zrow = sbB(nc.sbuf_tensor("zrow", [8, 128], BF16))
        mTt = sbB(nc.sbuf_tensor("mTt", [8, 128], BF16))
        msk = sbB(nc.sbuf_tensor("msk", [128, NE], BF16))
        ctxW2 = ExitStack()
        wob = ctxW2.enter_context(nc.sbuf_tensor("wob", [128, 8, D], BF16))
        ybuf2 = ctxW2.enter_context(nc.sbuf_tensor("ybuf2", [128, 2, 512], F32))

        p_y = [sbB(nc.psum_tensor(f"p_y{i}", [128, 512], F32)) for i in range(7)]
        ctx2 = ExitStack()
        p_trb = ctx2.enter_context(nc.psum_tensor("p_trb", [128, 1024], BF16))

        wstg = [ctxW2.enter_context(nc.sbuf_tensor(f"wstgB{i}", [128, 1024], F32)) for i in range(2)]
        t_rm = dI.emit("sync", TH("dma_start", out=rmsc[:], in_=rms_h), [ph_gate])
        wo_tok = None
        wo_toks = []
        for pc in range(8):
            stg = wstg[pc % 2]
            t_wo = dI.emit("sync", TH(
                "dma_start", out=stg[:],
                in_=wo_h[pc * 128:(pc + 1) * 128, :]),
                [ph_gate, wo_toks[pc - 2] if pc >= 2 else None])
            wo_tok = EM("vector", TH(
                "tensor_scalar", out=wob[:, pc, :], in0=stg[:],
                scalar1=rmsc[:, pc:pc + 1], scalar2=None,
                op0=ALU.mult), [t_wo, t_rm, ph_bar], sV, 1)
            wo_toks.append(wo_tok)
        t_sw = dI.emit("sync", TH(
            "dma_start", out=ysq[:, 0:64].rearrange("p (a c) -> p a c", a=8),
            in_=sw_h.rearrange("(a p) c -> p a c", p=128)), [ph_gate])
        t_sw = EM("vector", TH("tensor_copy",
                               swb[:].rearrange("p a c -> p (a c)"),
                               ysq[:, 0:64]), [t_sw, ph_bar], sV, 1)
        t_swb = dI.emit("sync", TH("dma_start", out=lst[0:1, 0:8], in_=swb_h),
                        [ph_gate])
        t_swb = EM("vector", TH("tensor_copy", swbb[:], lst[0:1, 0:8]),
                   [t_swb], sV, 1)
        t_b2 = dI.emit("sync", TH("dma_start", out=xt_s[0][0:8, :], in_=b2_h),
                       [ph_gate])
        t_b2 = EM("vector", TH("tensor_copy", b2b[:, :], xt_s[0][0:8, :]),
                  [t_b2, t_sw], sV, 1)


        xt_free = [None, t_b2]
        p_free = {}
        lgs_toks = [None] * NT
        x1T_tok = [None] * NT
        oa_init = [None] * NT
        srt_prev = None
        s2_prev = None
        for t in range(NT):
            pp = t % 2
            tx = dI.emit("sync", TH("dma_start", out=xt_s[pp][:],
                                    in_=x_h[L - TW + t * 128:
                                            L - TW + (t + 1) * 128, :]),
                         [xt_free[pp] if t > 1 else
                          (t_b2 if pp == 0 else ph_gate)])
            yv = {}
            for dh in range(2):
                mm = None
                for ds in range(8):
                    mm = EM("tensor", TH(
                        "matmul", p_y[dh][:],
                        o_accT[:, ds, t * 128:(t + 1) * 128],
                        wob[:, ds, dh * 512:(dh + 1) * 512],
                        start=(ds == 0), stop=(ds == 7)),
                        [wo_toks[-1] if ds == 0 else None,
                         p_free.get(dh) if ds == 0 else None,
                         ph_bar if (t == 0 and ds == 0) else None], sT, 1)
                yv[dh] = EM("vector", TH(
                    "tensor_tensor", ybuf2[:, dh, :], p_y[dh][:],
                    xt_s[pp][:, dh * 512:(dh + 1) * 512], ALU.add),
                    [mm, tx, srt_prev if dh == 0 else None], sV, 1)
                p_free[dh] = yv[dh]
            yflat = ybuf2[:].rearrange("p a c -> p (a c)")
            s1 = EM("vector", TH("reduce_sum", lst[:, 0:1], yflat, axis=AX.X),
                    [yv[1], srt_prev], sV, 1)
            sqy = EM("scalar", TH("activation", out=ysq[:, :], in_=yflat,
                                  func=AF.Square), [yv[1], t_b2, s2_prev], sA, 1)
            s2 = EM("vector", TH("reduce_sum", lst[:, 1:2], ysq[:, :], axis=AX.X),
                    [sqy], sV, 1)
            s2_prev = s2
            mn = EM("scalar", TH("mul", lst[:, 2:3], lst[:, 0:1], 1.0 / D),
                    [s1], sA, 1)
            msq = EM("scalar", TH("activation", out=lst[:, 3:4], in_=lst[:, 2:3],
                                  func=AF.Square), [mn], sA, 1)
            var = EM("vector", TH("tensor_scalar", out=lst[:, 4:5], in0=lst[:, 1:2],
                                  scalar1=1.0 / D, scalar2=lst[:, 3:4],
                                  op0=ALU.mult, op1=ALU.subtract),
                     [s2, msq], sV, 1)
            srt = EM("scalar", TH("activation", out=lst[:, 5:6], in_=lst[:, 4:5],
                                  func=AF.Sqrt, bias=cst["eps"][:, 1:2],
                                  scale=1.0), [var], sA, 1)
            srt_prev = srt
            rst = EM("vector", TH("reciprocal", lst[:, 6:7], lst[:, 5:6]),
                     [srt], sV, 1)
            spacer()
            x1t = EM("vector", TH("tensor_scalar", out=out_acc[:, t, :], in0=yflat,
                                  scalar1=lst[:, 2:3], scalar2=lst[:, 6:7],
                                  op0=ALU.subtract, op1=ALU.mult),
                     [rst], sV, 1)
            oa_init[t] = x1t
            xt_free[pp] = x1t
            trx = None
            for ds in range(8):
                trx = EM("tensor", TH("transpose",
                                      p_trb[:, ds * 128:(ds + 1) * 128],
                                      out_acc[:, t, ds * 128:(ds + 1) * 128], IDB),
                         [x1t if ds == 0 else None,
                          x1T_tok[t - 1] if (ds == 0 and t > 0) else None], sT, 1)
            x1T_tok[t] = EM("vector", TH("tensor_copy",
                                         x1T[:, :, t * 128:(t + 1) * 128],
                                         p_trb[:].rearrange("p (a c) -> p a c", a=8)),
                            [trx], sV, 1)
            lmm = None
            for ds in range(8):
                lmm = EM("tensor", TH("matmul", p_y[2][:, 0:NE],
                                      x1T[:, ds, t * 128:(t + 1) * 128],
                                      swb[:, ds, :], start=(ds == 0), stop=False),
                         [x1T_tok[t] if ds == 0 else None,
                          t_sw if ds == 0 else None,
                          lgs_toks[t - 1] if (ds == 0 and t > 0) else None], sT, 1)
            lmm = EM("tensor", TH("matmul", p_y[2][:, 0:NE],
                                  cst["ones_row"][0:1, 0:128], swbb[0:1, :],
                                  start=False, stop=True), [t_swb], sT, 1)
            lgs_toks[t] = EM("vector", TH("tensor_copy", lgs_all[:, t, :],
                                          p_y[2][:, 0:NE]), [lmm], sV, 1)

        route_tok = [None] * NT
        mtt_tok = None
        ex_prev = srt_prev
        for t in range(NT):
            lm = EM("vector", TH("reduce_max", lst[:, 0:1], lgs_all[:, t, :],
                                 axis=AX.X),
                    [lgs_toks[NT - 1] if t == 0 else None, ex_prev], sV, 1)
            nlm = EM("scalar", TH("mul", lst[:, 1:2], lst[:, 0:1], -1.0),
                     [lm], sA, 1)
            ex = EM("scalar", TH("activation", out=expo[:, :], in_=lgs_all[:, t, :],
                                 func=AF.Exp, bias=lst[:, 1:2], scale=1.0),
                    [nlm], sA, 1)
            ex_prev = ex
            sm = EM("vector", TH("reduce_sum", lst[:, 2:3], expo[:, :], axis=AX.X),
                    [ex], sV, 1)
            eq = EM("vector", TH("tensor_single_scalar", eqb[:, :],
                                 lgs_all[:, t, :], lst[:, 0:1], ALU.is_ge),
                    [sm], sV, 1)
            rpm = EM("vector", TH("reciprocal", lst[:, 3:4], lst[:, 2:3]),
                     [eq], sV, 1)
            tr_eq = EM("tensor", TH("transpose", p_trb[0:8, 0:128], eqb[:, :], IDB),
                       [eq, mtt_tok], sT, 1)
            eqt = EM("vector", TH("tensor_copy", eqT[:, :], p_trb[0:8, 0:128]),
                     [tr_eq], sV, 1)
            cmm = EM("tensor", TH("matmul", p_y[3][0:8, 0:128], su8[:, :],
                                  eqT[:, :], start=True, stop=True),
                     [eqt, route_tok[t - 1] if t > 0 else None,
                      c_tok if t == 0 else None], sT, 1)
            zr = EM("vector", TH("tensor_single_scalar", zrow[:, :],
                                 p_y[3][0:8, 0:128], 0.0, ALU.is_equal),
                    [cmm], sV, 1)
            mt = EM("vector", TH("tensor_tensor", mTt[:, :], eqT[:, :],
                                 zrow[:, :], ALU.mult), [zr], sV, 1)
            tr_m = EM("tensor", TH("transpose", p_trb[0:128, 128:136], mTt[:, :],
                                   id8[:, :]), [mt], sT, 1)
            mk = EM("vector", TH("tensor_copy", msk[:, :], p_trb[0:128, 128:136]),
                    [tr_m], sV, 1)
            mpq = EM("vector", TH("tensor_scalar", out=maskp[:, t, :],
                                  in0=msk[:, :], scalar1=lst[:, 3:4], scalar2=None,
                                  op0=ALU.mult), [mk, rpm], sV, 1)
            ini = None
            for dh in range(2):
                bmm = EM("tensor", TH("matmul", p_y[4 + dh][:], mTt[:, :],
                                      b2b[:, dh * 512:(dh + 1) * 512],
                                      start=True, stop=True),
                         [mt, t_b2, route_tok[t - 1] if t >= 1 else None], sT, 1)
                ini = EM("vector", TH("scalar_tensor_tensor",
                                      out=out_acc[:, t, dh * 512:(dh + 1) * 512],
                                      in0=p_y[4 + dh][:], scalar=lst[:, 3:4],
                                      in1=out_acc[:, t, dh * 512:(dh + 1) * 512],
                                      op0=ALU.mult, op1=ALU.add),
                         [bmm, mpq if dh == 0 else None, oa_init[t]], sV, 1)
            route_tok[t] = ini
            mtt_tok = ini

        ctx2.close()
        p_y.append(sbB(nc.psum_tensor("p_y7", [128, 512], F32)))
        bar_b2 = prog.barrier()
        ctxW2.close()

        w2res = sbB(nc.sbuf_tensor("w2res", [128, 32, D], BF16))

        def hTs(fs, c0, c1):
            return o_accT[:, fs // 4, (fs % 4) * 512 + c0:(fs % 4) * 512 + c1]

        w1sl_free = [None, None]
        w2res_free = None
        w2res_done = []
        py_free = [None] * 8
        hT_rd = [None] * 32
        oa_tok = {t: route_tok[t] for t in range(NT)}
        b1_tok = None
        prev_h_all = None
        h_psum_free = [None] * 4
        for e in range(NE):
            b1_tok = dI.emit("sync", TH("dma_start", out=b1t[:], in_=b1_h[e]),
                             [b1_tok, prev_h_all,
                              bar_b2 if e == 0 else None])
            for q in range(4):
                qs = slice(q * 512, (q + 1) * 512)
                hts = []
                for g in range(16):
                    gp = g % 2
                    t_w1 = dSL.emit("sync", TH(
                        "dma_start", out=w1sl[gp][:],
                        in_=w1bf[e, :, :, g * 256:(g + 1) * 256]
                        .rearrange("a p c -> p a c")),
                        [w1sl_free[gp], conv1_done[e] if g <= 1 else None])
                    if q == 0:
                        if g == 0:
                            w2res_done = []
                        w2res_done.append(dSL.emit("sync", TH(
                            "dma_start", out=w2res[:, 2 * g:2 * g + 2, :],
                            in_=w2bf[e, 2 * g:(g + 1) * 2]
                            .rearrange("s p d -> p s d")),
                            [w2res_free if g == 0 else None,
                             conv2_done[e] if g == 0 else None]))
                    for i in range(2):
                        fs = g * 2 + i
                        pj = fs % 4
                        mm = None
                        for ds in range(8):
                            mm = EM("tensor", TH(
                                "matmul", p_y[pj][:],
                                w1sl[gp][:, ds, i * 128:(i + 1) * 128],
                                x1T[:, ds, qs],
                                start=(ds == 0), stop=(ds == 7)),
                                [t_w1 if ds == 0 else None,
                                 h_psum_free[pj] if ds == 0 else None,
                                 bar_b2 if (e == 0 and q == 0 and fs == 0
                                            and ds == 0) else None], sT, 1)
                        ht = EM("scalar", TH("activation",
                                             out=hTs(fs, 0, 512),
                                             in_=p_y[pj][:],
                                             func=AF.Gelu_apprx_tanh,
                                             bias=b1t[:, fs:fs + 1],
                                             scale=1.0),
                                [mm, b1_tok, hT_rd[fs]], sA, 1)
                        h_psum_free[pj] = ht
                        hts.append(ht)
                    w1sl_free[gp] = hts[-1]
                h_all = hts[-1]
                mmy = None
                for fs in range(32):
                    for tt in range(4):
                        for dh in range(2):
                            pj = tt * 2 + dh
                            mmy = EM("tensor", TH(
                                "matmul", p_y[pj][:],
                                hTs(fs, tt * 128, (tt + 1) * 128),
                                w2res[:, fs, dh * 512:(dh + 1) * 512],
                                start=(fs == 0), stop=(fs == 31)),
                                [w2res_done if (fs == 0 and tt == 0 and dh == 0)
                                 else None,
                                 h_all if (fs == 0 and tt == 0 and dh == 0)
                                 else None,
                                 py_free[pj] if fs == 0 else None], sT, 1)
                    hT_rd[fs] = mmy
                y_done = mmy
                for tt in range(4):
                    tile = q * 4 + tt
                    for dh in range(2):
                        pj = tt * 2 + dh
                        oa_tok[tile] = EM("vector", TH(
                            "scalar_tensor_tensor",
                            out=out_acc[:, tile, dh * 512:(dh + 1) * 512],
                            in0=p_y[pj][:], scalar=maskp[:, tile, e:e + 1],
                            in1=out_acc[:, tile, dh * 512:(dh + 1) * 512],
                            op0=ALU.mult, op1=ALU.add),
                            [y_done, oa_tok[tile]], sV, 1)
                        py_free[pj] = oa_tok[tile]
                for pj in range(4):
                    h_psum_free[pj] = py_free[pj]
            prev_h_all = h_all
            w2res_free = y_done

        osb = xt_s
        osb_free = [None, None]
        do_tok = None
        srt_prev = None
        s2_prev = None
        oa_last = oa_tok[NT - 1]
        for t in range(NT):
            pp = t % 2
            oaf = out_acc[:, t, :]
            s1 = EM("vector", TH("reduce_sum", lst[:, 0:1], oaf, axis=AX.X),
                    [oa_tok[t], srt_prev,
                     ex_prev if t == 0 else None], sV, 1)
            sqy = EM("scalar", TH("activation", out=ysq[:, :], in_=oaf,
                                  func=AF.Square), [oa_tok[t], s2_prev], sA, 1)
            s2 = EM("vector", TH("reduce_sum", lst[:, 1:2], ysq[:, :], axis=AX.X),
                    [sqy], sV, 1)
            s2_prev = s2
            mn = EM("scalar", TH("mul", lst[:, 2:3], lst[:, 0:1], 1.0 / D),
                    [s1], sA, 1)
            msq = EM("scalar", TH("activation", out=lst[:, 3:4], in_=lst[:, 2:3],
                                  func=AF.Square), [mn], sA, 1)
            var = EM("vector", TH("tensor_scalar", out=lst[:, 4:5], in0=lst[:, 1:2],
                                  scalar1=1.0 / D, scalar2=lst[:, 3:4],
                                  op0=ALU.mult, op1=ALU.subtract),
                     [s2, msq], sV, 1)
            srt = EM("scalar", TH("activation", out=lst[:, 5:6], in_=lst[:, 4:5],
                                  func=AF.Sqrt, bias=cst["eps"][:, 1:2],
                                  scale=1.0), [var], sA, 1)
            srt_prev = srt
            rst = EM("vector", TH("reciprocal", lst[:, 6:7], lst[:, 5:6]),
                     [srt], sV, 1)
            spacer()
            ot = EM("vector", TH("tensor_scalar", out=osb[pp][:, :], in0=oaf,
                                 scalar1=lst[:, 2:3], scalar2=lst[:, 6:7],
                                 op0=ALU.subtract, op1=ALU.mult),
                    [rst, osb_free[pp],
                     oa_last if t == 0 else None,
                     xt_free[pp] if t < 2 else None], sV, 1)
            do_tok = dO.emit("gpsimd", TH("dma_start",
                                          out=out_h[t * 128:(t + 1) * 128, :],
                                          in_=osb[pp][:, :]), [ot])
            osb_free[pp] = do_tok
        EM("gpsimd", TH("nop"), [dO.total_tok()])

        with nc.Block() as block:
            prog.run(block)
    return nc


_CACHE = {}


def kernel(**inp):
    x = np.ascontiguousarray(inp["L0_attn_hidden_states"], np.float32)
    Wq = np.ascontiguousarray(inp["L0_attn_Wq"], np.float32)
    Wk = np.ascontiguousarray(inp["L0_attn_Wk"], np.float32)
    Wv = np.ascontiguousarray(inp["L0_attn_Wv"], np.float32)
    conv_q = np.ascontiguousarray(inp["L0_attn_conv_q"], np.float32)
    conv_k = np.ascontiguousarray(inp["L0_attn_conv_k"], np.float32)
    conv_v = np.ascontiguousarray(inp["L0_attn_conv_v"], np.float32)
    Wb = np.ascontiguousarray(inp["L0_attn_Wb"], np.float32)
    Wdec = np.ascontiguousarray(inp["L0_attn_Wdec"], np.float32)
    Wmix = np.ascontiguousarray(inp["L0_attn_Wmix"], np.float32)
    conv_mix = np.ascontiguousarray(inp["L0_attn_conv_mix"], np.float32)
    mix_bias = np.ascontiguousarray(inp["L0_attn_mix_bias"], np.float32)
    rms_w = np.ascontiguousarray(inp["L0_attn_rms_w"], np.float32)
    Wo = np.ascontiguousarray(inp["L0_attn_Wo"], np.float32)
    sw_w = np.ascontiguousarray(inp["L0_ffn_switch_w"], np.float32)
    sw_b = np.ascontiguousarray(inp["L0_ffn_switch_b"], np.float32)
    w1 = np.ascontiguousarray(inp["L0_ffn_w1"], np.float32)
    b1 = np.ascontiguousarray(inp["L0_ffn_b1"], np.float32)
    w2 = np.ascontiguousarray(inp["L0_ffn_w2"], np.float32)
    b2 = np.ascontiguousarray(inp["L0_ffn_b2"], np.float32)

    if "fused" not in _CACHE:
        _CACHE["fused"] = build_fused()
    ncf = _CACHE["fused"]

    def slab3(a):
        return np.ascontiguousarray(
            a.reshape(8, 128, a.shape[1]).transpose(1, 0, 2))

    wbg_host = slab3(np.stack(
        [col for h in range(H) for col in (Wb[:, h], Wdec[:, h])], axis=1))
    rmsc = np.ascontiguousarray(np.tile(rms_w, H).reshape(8, 128).T)
    in_maps = []
    for c in range(8):
        b, w = divmod(c, 4)
        xin = np.zeros((L, D), np.float32)
        pre = w * TW
        if pre:
            xin[L - TW - pre:L - TW] = x[b, 0:pre]
        xin[L - TW:] = x[b, pre:pre + TW]
        in_maps.append(dict(
            x=xin, wq=Wq, wk=Wk, wv=Wv,
            wbg=wbg_host,
            wmx=slab3(Wmix),
            cvq=slab3(conv_q), cvk=slab3(conv_k), cvv=slab3(conv_v),
            cvm=np.ascontiguousarray(conv_mix.reshape(1, 4, 4)),
            mxb=np.ascontiguousarray(mix_bias.reshape(1, 4)),
            wo=Wo, rmsc=rmsc, sww=sw_w,
            swbi=np.ascontiguousarray(sw_b.reshape(1, NE)),
            w1=w1, b1r=np.ascontiguousarray(
                b1.reshape(NE, 32, 128).transpose(0, 2, 1)),
            w2=w2, b2=b2,
        ))
    res = run_bass_kernel_spmd(ncf, in_maps, core_ids=list(range(8)))
    out = np.zeros((2, L, D), np.float32)
    for c in range(8):
        b, w = divmod(c, 4)
        out[b, w * TW:(w + 1) * TW] = res.results[c]["out"]
    return out
